# version 01
# baseline (speedup 1.0000x reference)
import contextlib
import math
import numpy as np
import ml_dtypes
import concourse.bass as bass
import concourse.mybir as mybir
from concourse.bass_utils import run_bass_kernel_spmd

F32 = mybir.dt.float32
BF16 = mybir.dt.bfloat16
AF = mybir.ActivationFunctionType
ALU = mybir.AluOpType
AX = mybir.AxisListType

B_LOC, T, D, NCH = 2, 2048, 2048, 16
NH, NKV, HS = 16, 4, 128
WINDOW, SINK = 1024, 4
FF, NFT = 4096, 32
GB = 512
NG = (B_LOC * T) // GB
QSCALE = 1.0 / math.sqrt(HS)
EPS = 1e-5
NEGBIG = -1.0e30


def _sbimg(a):
    K, M = a.shape
    return np.ascontiguousarray(a.reshape(K // 128, 128, M).transpose(1, 0, 2).reshape(128, -1))


def host_inputs(Wq, Wk, Wv, Wo, Wg, W0a, b0a, W0b, b0b, W1a, b1a, W1b, b1b):
    f = np.float32
    out = {}
    out["wq_img"] = _sbimg(np.ascontiguousarray(np.asarray(Wq, f).T))
    out["wk_img"] = _sbimg(np.ascontiguousarray(np.asarray(Wk, f).T))
    out["wv_img"] = _sbimg(np.ascontiguousarray(np.asarray(Wv, f).T))
    out["wo_img"] = _sbimg(np.ascontiguousarray(np.asarray(Wo, f).T))
    for e, W in ((0, W0a), (1, W1a)):
        A = np.ascontiguousarray(np.asarray(W, f).T)
        out[f"w{e}a_img"] = np.ascontiguousarray(
            A.reshape(16, 128, 32, 128).transpose(1, 2, 0, 3).reshape(128, 65536))
    for e, W in ((0, W0b), (1, W1b)):
        Bt = np.ascontiguousarray(np.asarray(W, f).T)
        out[f"w{e}b_img"] = np.ascontiguousarray(
            Bt.reshape(32, 128, 4, 4, 128).transpose(1, 2, 0, 3, 4).reshape(128, 65536))
    out["wg_img"] = _sbimg(np.ascontiguousarray(np.asarray(Wg, f).T))
    out["b0a_img"] = np.ascontiguousarray(np.asarray(b0a, f).reshape(32, 128).T)
    out["b1a_img"] = np.ascontiguousarray(np.asarray(b1a, f).reshape(32, 128).T)
    out["b0b_img"] = np.ascontiguousarray(np.asarray(b0b, f).reshape(16, 128).T)
    out["b1b_img"] = np.ascontiguousarray(np.asarray(b1b, f).reshape(16, 128).T)
    ii = np.arange(128)[:, None]
    jj = np.arange(128)[None, :]
    m_lower = np.where(jj <= ii, 0.0, NEGBIG)
    m_far = np.where(jj > ii, 0.0, NEGBIG)
    m_farsink = np.where((jj > ii) | (jj < SINK), 0.0, NEGBIG)
    out["masks_img"] = np.concatenate([m_lower, m_far, m_farsink], 1).astype(ml_dtypes.bfloat16)
    out["ident_img"] = np.eye(128, dtype=ml_dtypes.bfloat16)
    out["onesb_img"] = np.ones((128, 1), dtype=ml_dtypes.bfloat16)
    out["onesr_img"] = np.ones((1, 128), np.float32)
    return out


def build(stop_after="full", dbg=False):
    es = contextlib.ExitStack()
    nc = bass.Bass("TRN2", target_bir_lowering=False, debug=False)

    def din(name, shape, dt):
        return nc.dram_tensor(name, shape, dt, kind="ExternalInput")

    x_img = din("x_img", [128, B_LOC * NCH * T], F32)
    wq_img = din("wq_img", [128, NCH * D], F32)
    wk_img = din("wk_img", [128, NCH * 512], F32)
    wv_img = din("wv_img", [128, NCH * 512], F32)
    wo_img = din("wo_img", [128, NCH * D], F32)
    wa_img = [din("w0a_img", [128, 65536], F32), din("w1a_img", [128, 65536], F32)]
    wb_img = [din("w0b_img", [128, 65536], F32), din("w1b_img", [128, 65536], F32)]
    wg_img = din("wg_img", [128, 32], F32)
    ba_img = [din("b0a_img", [128, 32], F32), din("b1a_img", [128, 32], F32)]
    bb_img = [din("b0b_img", [128, 16], F32), din("b1b_img", [128, 16], F32)]
    masks_img = din("masks_img", [128, 384], BF16)
    ident_img = din("ident_img", [128, 128], BF16)
    onesb_img = din("onesb_img", [128, 1], BF16)
    onesr_img = din("onesr_img", [1, 128], F32)

    q_scr = nc.dram_tensor("q_scr", [128, NH * 4096], BF16, kind="Internal")
    y_scr = nc.dram_tensor("y_scr", [128, NCH * 4096], BF16, kind="Internal")
    x1_scr = nc.dram_tensor("x1_scr", [128, NCH * 4096], BF16, kind="Internal")
    mrow_scr = nc.dram_tensor("mrow_scr", [1, 4096], F32, kind="Internal")
    wa_bf = [nc.dram_tensor(f"w{e}a_bf", [128, 65536], BF16, kind="Internal") for e in range(2)]
    wb_bf = [nc.dram_tensor(f"w{e}b_bf", [128, 65536], BF16, kind="Internal") for e in range(2)]
    out_img = nc.dram_tensor("out_img", [128, NCH * 4096], F32, kind="ExternalOutput")
    dbg_t = {}
    if dbg:
        dbg_t["dbg_kf"] = nc.dram_tensor("dbg_kf", [128, NKV * T], BF16, kind="ExternalOutput")
        dbg_t["dbg_vt"] = nc.dram_tensor("dbg_vt", [128, 16 * 512], BF16, kind="ExternalOutput")
        dbg_t["dbg_q"] = nc.dram_tensor("dbg_q", [128, NH * 4096], BF16, kind="ExternalOutput")
        dbg_t["dbg_y"] = nc.dram_tensor("dbg_y", [128, NCH * 4096], BF16, kind="ExternalOutput")
        dbg_t["dbg_x1"] = nc.dram_tensor("dbg_x1", [128, NCH * 4096], BF16, kind="ExternalOutput")
        dbg_t["dbg_m"] = nc.dram_tensor("dbg_m", [1, 4096], F32, kind="ExternalOutput")

    def dap(t, col0, cols, nseg=1, segstride=0):
        W = t.shape[1]
        if nseg == 1:
            return bass.AP(t, col0, [[W, 128], [1, cols]])
        return bass.AP(t, col0, [[W, 128], [segstride, nseg], [1, cols]])

    BUF1 = nc.alloc_sbuf_tensor("BUF1", [128, 32768], BF16)
    BUF2 = nc.alloc_sbuf_tensor("BUF2", [128, 32768], BF16)
    KF = nc.alloc_sbuf_tensor("KF", [128, 8192], BF16)
    VT = nc.alloc_sbuf_tensor("VT", [128, 8192], BF16)
    BUF5 = nc.alloc_sbuf_tensor("BUF5", [128, 7680], BF16)
    FSCR = nc.alloc_sbuf_tensor("FSCR", [128, 4096], F32)
    XR = nc.alloc_sbuf_tensor("XR", [128, 2048], F32)
    ROWS = nc.alloc_sbuf_tensor("ROWS", [1, 1024], F32)
    MROW = nc.alloc_sbuf_tensor("MROW", [1, 512], F32)
    masks = nc.alloc_sbuf_tensor("masks", [128, 384], BF16)
    ident = nc.alloc_sbuf_tensor("ident", [128, 128], BF16)
    onesb = nc.alloc_sbuf_tensor("onesb", [128, 1], BF16)
    onesr = nc.alloc_sbuf_tensor("onesr", [1, 128], F32)
    wgsb = nc.alloc_sbuf_tensor("wgsb", [128, 32], F32)
    wgd = nc.alloc_sbuf_tensor("wgd", [128, 16], F32)
    swgd = nc.alloc_sbuf_tensor("swgd", [1, 1], F32)
    basb = [nc.alloc_sbuf_tensor(f"b{e}a_sb", [128, 32], F32) for e in range(2)]
    bbsb = [nc.alloc_sbuf_tensor(f"b{e}b_sb", [128, 16], F32) for e in range(2)]
    bdiff = nc.alloc_sbuf_tensor("bdiff", [128, 16], F32)
    nmx = nc.alloc_sbuf_tensor("nmx", [128, 2], F32)
    rsm = nc.alloc_sbuf_tensor("rsm", [128, 2], F32)
    rcp = nc.alloc_sbuf_tensor("rcp", [128, 2], F32)
    diagb = nc.alloc_sbuf_tensor("diagb", [128, 256], BF16)
    fenb = nc.alloc_sbuf_tensor("fenb", [128, 32], BF16)

    PS = nc.alloc_psum_tensor("PS", [128, 4096], F32)

    PP0, PP1, YQ0, YQ1, CBBF_O = 0, 1280, 2560, 4608, 6656

    class Sem:
        def __init__(self, name):
            self.h = es.enter_context(nc.semaphore(name))
            self.n = 0

        def inc(self, instr, k=1):
            instr.then_inc(self.h, k)
            self.n += k
            return self.n

        def dma(self, instr):
            return self.inc(instr, 16)

    sPE, sACT, sDVE, sPOOL = Sem("sPE"), Sem("sACT"), Sem("sDVE"), Sem("sPOOL")

    def pool_fence(gp):
        gp.tensor_copy(fenb[:, 0:16], fenb[:, 16:32])
        gp.tensor_copy(fenb[:, 16:32], fenb[:, 0:16])
        gp.tensor_copy(fenb[:, 0:16], fenb[:, 16:32])
        return sPOOL.inc(gp.tensor_copy(fenb[:, 16:32], fenb[:, 0:16]))
    sLD, sLDQ, sLDX = Sem("sLD"), Sem("sLDQ"), Sem("sLDX")
    sCV, sCVO = Sem("sCV"), Sem("sCVO")
    sQW, sYW, sX1, sOUT, sMRW = Sem("sQW"), Sem("sYW"), Sem("sX1"), Sem("sOUT"), Sem("sMRW")

    CVW = 1024
    conv_jobs = []
    for e in range(2):
        for s in range(65536 // CVW):
            conv_jobs.append((wa_img[e], wa_bf[e], s * CVW))
    for e in range(2):
        for s in range(65536 // CVW):
            conv_jobs.append((wb_img[e], wb_bf[e], s * CVW))
    conv_li = [0]
    conv_si = [0]

    def emit_conversion_load(sy, njobs):
        n = 0
        while n < njobs and conv_li[0] < len(conv_jobs):
            i = conv_li[0]
            conv_li[0] += 1
            n += 1
            par = i % 2
            src, dst, col = conv_jobs[i]
            if i >= 2:
                sy.wait_ge(sCVO.h, 16 * (i - 1))
            sy.dma_start(FSCR[:, par * 1024:(par + 1) * 1024], dap(src, col, CVW)).then_inc(sCV.h, 16)
            sCV.n += 16

    def emit_conversion_store(gp, njobs):
        n = 0
        while n < njobs and conv_si[0] < len(conv_jobs):
            i = conv_si[0]
            conv_si[0] += 1
            n += 1
            par = i % 2
            src, dst, col = conv_jobs[i]
            gp.wait_ge(sCV.h, 16 * (i + 1))
            if i >= 1:
                gp.wait_ge(sCVO.h, 16 * i)
            gp.tensor_copy(BUF5[:, CBBF_O:CBBF_O + CVW], FSCR[:, par * 1024:(par + 1) * 1024])
            gp.dma_start(dap(dst, col, CVW), BUF5[:, CBBF_O:CBBF_O + CVW]).then_inc(sCVO.h, 16)
            sCVO.n += 16

    with nc.Block() as blk:
        @blk.sync
        def _(sy):
            for dst, src in ((masks, masks_img), (ident, ident_img), (onesb, onesb_img),
                             (onesr, onesr_img),
                             (wgsb, wg_img), (basb[0], ba_img[0]), (basb[1], ba_img[1]),
                             (bbsb[0], bb_img[0]), (bbsb[1], bb_img[1])):
                sLD.dma(sy.dma_start(dst[:, :], src.ap()))
            sy.wait_ge(sLD.h, sLD.n)

        @blk.vector
        def _(v):
            v.wait_ge(sLD.h, sLD.n)
            v.tensor_tensor(bdiff[:, :], bbsb[1][:, :], bbsb[0][:, :], op=ALU.subtract)
            sDVE.inc(v.tensor_tensor(wgd[:, :],
                                     bass.AP(wgsb, 0, [[32, 128], [2, 16]]),
                                     bass.AP(wgsb, 1, [[32, 128], [2, 16]]), op=ALU.subtract))

        @blk.gpsimd
        def _(gp):
            gp.wait_ge(sDVE.h, sDVE.n)
            sPOOL.inc(gp.tensor_reduce(ROWS[0:1, 0:1], wgd[:, :], axis=AX.XYZWC, op=ALU.add))

        @blk.scalar
        def _(sc):
            sc.wait_ge(sPOOL.h, sPOOL.n)
            sACT.inc(sc.copy(swgd[0:1, 0:1], ROWS[0:1, 0:1]))

    def phase_a0a1(b):
        xbase = b * (NCH * T)
        with nc.Block() as blk:
            strips = ([("x", c) for c in range(NCH)]
                      + [("wk", s) for s in range(4)] + [("wv", s) for s in range(4)])
            cast_at = {}
            pool_base = sPOOL.n

            @blk.sync
            def _(sy):
                for i, (kind, idx) in enumerate(strips):
                    par = i % 2
                    if i >= 2:
                        sy.wait_ge(sPOOL.h, pool_base + i - 1)
                    if kind == "x":
                        src = dap(x_img, xbase + idx * T, T)
                    elif kind == "wk":
                        src = dap(wk_img, idx * 2048, 2048)
                    else:
                        src = dap(wv_img, idx * 2048, 2048)
                    sLD.dma(sy.dma_start(FSCR[:, par * 2048:(par + 1) * 2048], src))

            @blk.gpsimd
            def _(gp):
                n0 = sLD.n - 16 * len(strips)
                for i, (kind, idx) in enumerate(strips):
                    par = i % 2
                    gp.wait_ge(sLD.h, n0 + 16 * (i + 1))
                    if kind == "x":
                        dst = BUF1[:, idx * 2048:(idx + 1) * 2048]
                    elif kind == "wk":
                        dst = BUF2[:, idx * 2048:(idx + 1) * 2048]
                    else:
                        dst = BUF2[:, 8192 + idx * 2048:8192 + (idx + 1) * 2048]
                    cast_at[(kind, idx)] = sPOOL.inc(
                        gp.tensor_copy(dst, FSCR[:, par * 2048:(par + 1) * 2048]))
                cast_at["fence"] = pool_fence(gp)

            kv_jobs = [("k", ko, tb) for ko in range(4) for tb in range(4)] \
                    + [("v", tt, 0) for tt in range(16)]
            mm_at, ev_at = {}, {}
            act_base = sACT.n

            @blk.tensor
            def _(te):
                te.wait_ge(sPOOL.h, cast_at["fence"])
                for j, job in enumerate(kv_jobs):
                    pcol = (j % 2) * 512
                    if j >= 2:
                        te.wait_ge(sACT.h, act_base + j - 1)
                    ins = None
                    for dc in range(NCH):
                        if job[0] == "k":
                            ko, tb = job[1], job[2]
                            lhsT = BUF2[:, dc * 512 + ko * 128:dc * 512 + ko * 128 + 128]
                            rhs = BUF1[:, dc * 2048 + tb * 512:dc * 2048 + tb * 512 + 512]
                        else:
                            tt = job[1]
                            lhsT = BUF1[:, dc * 2048 + tt * 128:dc * 2048 + tt * 128 + 128]
                            rhs = BUF2[:, 8192 + dc * 512:8192 + dc * 512 + 512]
                        ins = te.matmul(PS[:, pcol:pcol + 512], lhsT, rhs,
                                        start=(dc == 0), stop=(dc == NCH - 1))
                    mm_at[j] = sPE.inc(ins)

            @blk.scalar
            def _(sc):
                for j, job in enumerate(kv_jobs):
                    pcol = (j % 2) * 512
                    sc.wait_ge(sPE.h, mm_at[j])
                    if job[0] == "k":
                        ko, tb = job[1], job[2]
                        dst = KF[:, ko * 2048 + tb * 512:ko * 2048 + tb * 512 + 512]
                    else:
                        tt = job[1]
                        dst = VT[:, tt * 512:(tt + 1) * 512]
                    ev_at[j] = sACT.inc(sc.copy(dst, PS[:, pcol:pcol + 512]))

    def phase_a2(b):
        with nc.Block() as blk:
            cast_at = {}
            q_mm = {}
            act_base = sACT.n
            q_half_done = {}
            blk_pool_base = sPOOL.n
            qw_base = sQW.n

            for half in range(2):
                pool_base0 = blk_pool_base

                @blk.sync
                def _(sy, half=half):
                    def cast_ctr(h, c):
                        return pool_base0 + h * (NCH + 1) + c + 1
                    for c in range(NCH):
                        i = half * NCH + c
                        par = i % 4
                        if i >= 4:
                            ph, pc = divmod(i - 4, NCH)
                            sy.wait_ge(sPOOL.h, cast_ctr(ph, pc))
                        sLD.dma(sy.dma_start(FSCR[:, par * 1024:(par + 1) * 1024],
                                             dap(wq_img, c * 2048 + half * 1024, 1024)))

                @blk.gpsimd
                def _(gp, half=half):
                    n0 = sLD.n - 16 * NCH
                    for c in range(NCH):
                        i = half * NCH + c
                        par = i % 4
                        gp.wait_ge(sLD.h, n0 + 16 * (c + 1))
                        if half == 1 and c == 0:
                            gp.wait_ge(sPE.h, q_half_done[0])
                        cast_at[(half, c)] = sPOOL.inc(
                            gp.tensor_copy(BUF2[:, c * 1024:(c + 1) * 1024],
                                           FSCR[:, par * 1024:(par + 1) * 1024]))
                    cast_at[(half, "fence")] = pool_fence(gp)

                @blk.tensor
                def _(te, half=half):
                    te.wait_ge(sPOOL.h, cast_at[(half, "fence")])
                    for o8 in range(8):
                        for tb in range(4):
                            j = half * 32 + o8 * 4 + tb
                            pcol = (j % 2) * 512
                            if j >= 2:
                                te.wait_ge(sACT.h, act_base + j - 1)
                            ins = None
                            for dc in range(NCH):
                                ins = te.matmul(PS[:, pcol:pcol + 512],
                                                BUF2[:, dc * 1024 + o8 * 128:dc * 1024 + o8 * 128 + 128],
                                                BUF1[:, dc * 2048 + tb * 512:dc * 2048 + tb * 512 + 512],
                                                start=(dc == 0), stop=(dc == NCH - 1))
                            q_mm[j] = sPE.inc(ins)
                    if half == 0:
                        q_half_done[0] = sPE.n

                q_ev = {}

                @blk.scalar
                def _(sc, half=half):
                    for o8 in range(8):
                        for tb in range(4):
                            j = half * 32 + o8 * 4 + tb
                            pcol = (j % 2) * 512
                            par4 = j % 4
                            sc.wait_ge(sPE.h, q_mm[j])
                            if j >= 4:
                                sc.wait_ge(sQW.h, qw_base + 16 * (j - 3))
                            dst = BUF2[:, 16384 + par4 * 512:16384 + par4 * 512 + 512]
                            q_ev[j] = sACT.inc(sc.mul(dst, PS[:, pcol:pcol + 512], QSCALE))

                @blk.sync
                def _(sy, half=half, b=b):
                    for o8 in range(8):
                        for tb in range(4):
                            j = half * 32 + o8 * 4 + tb
                            par4 = j % 4
                            sy.wait_ge(sACT.h, q_ev[j])
                            o = half * 8 + o8
                            sQW.dma(sy.dma_start(
                                dap(q_scr, o * 4096 + b * 2048 + tb * 512, 512),
                                BUF2[:, 16384 + par4 * 512:16384 + par4 * 512 + 512]))

    def qt_geom(qt):
        nw = min(qt + 1, 9)
        k0t = qt - nw + 1
        krange = nw * 128
        has_sink = qt >= 9
        kw = krange + (SINK if has_sink else 0)
        return nw, k0t, krange, has_sink, kw

    def phase_a4(b):
        with nc.Block() as blk:
            cv0 = conv_si[0]
            qld_at, scdone = {}, {}
            exp_at, ptev_at, trdone, diag_at = {}, {}, {}, {}
            max_at, avmm_at, yev_at, ydma_at = {}, {}, {}, {}

            @blk.sync
            def _(sy):
                for qt in (0, 1):
                    sLDX.dma(sy.dma_start(BUF2[:, (qt % 2) * 2048:(qt % 2) * 2048 + 2048],
                                          dap(q_scr, b * 2048 + qt * 128, 128, nseg=16, segstride=4096)))
                    qld_at[qt] = sLDX.n

            for qt in range(16):
                nw, k0t, krange, has_sink, kw = qt_geom(qt)
                par = qt % 2

                for h in range(NH):
                    kv = h // 4
                    hpar = h % 2
                    sb = 1536 * hpar
                    ppo = PP0 if hpar == 0 else PP1
                    ycol = 3072 if (h % 2) == 0 else 3584
                    prev = (qt, h - 2) if h >= 2 else ((qt - 1, NH + h - 2) if qt >= 1 else None)

                    @blk.tensor
                    def _(te, qt=qt, h=h, kv=kv, sb=sb, nw=nw, k0t=k0t, krange=krange,
                          has_sink=has_sink, par=par, prev=prev):
                        te.wait_ge(sLDX.h, qld_at[qt])
                        if prev is not None and prev in exp_at:
                            te.wait_ge(sACT.h, max(exp_at[prev], ptev_at[prev]))
                        lhsq = BUF2[:, par * 2048 + h * 128:par * 2048 + h * 128 + 128]
                        col = 0
                        ins = None
                        while col < krange:
                            w = min(512, krange - col)
                            rhs = KF[:, kv * 2048 + (k0t * 128 + col):kv * 2048 + (k0t * 128 + col) + w]
                            ins = te.matmul(PS[:, sb + col:sb + col + w], lhsq, rhs,
                                            start=True, stop=True)
                            col += w
                        if has_sink:
                            ins = te.matmul(PS[:, sb + krange:sb + krange + SINK], lhsq,
                                            KF[:, kv * 2048:kv * 2048 + SINK], start=True, stop=True)
                        max_at[(qt, h, "sc")] = sPE.inc(ins)
                        if h == NH - 1:
                            scdone[qt] = sPE.n

                    @blk.vector
                    def _(v, qt=qt, h=h, sb=sb, krange=krange, hpar=hpar, prev=prev):
                        v.wait_ge(sPE.h, max_at[(qt, h, "sc")])
                        if prev is not None and prev in exp_at:
                            v.wait_ge(sACT.h, exp_at[prev])
                        v.tensor_tensor(PS[:, sb + krange - 128:sb + krange],
                                        PS[:, sb + krange - 128:sb + krange],
                                        masks[:, 0:128], op=ALU.add)
                        if qt >= 9:
                            v.tensor_tensor(PS[:, sb:sb + 128], PS[:, sb:sb + 128],
                                            masks[:, 128:256], op=ALU.add)
                        elif qt == 8:
                            v.tensor_tensor(PS[:, sb:sb + 128], PS[:, sb:sb + 128],
                                            masks[:, 256:384], op=ALU.add)
                        max_at[(qt, h)] = sDVE.inc(
                            v.reduce_max(nmx[:, hpar:hpar + 1], PS[:, sb + krange - 128:sb + krange],
                                         axis=AX.X, negate=True))

                    @blk.scalar
                    def _(sc, qt=qt, h=h, sb=sb, kw=kw, hpar=hpar, ppo=ppo, prev=prev):
                        sc.wait_ge(sDVE.h, max_at[(qt, h)])
                        if prev is not None and prev in avmm_at:
                            sc.wait_ge(sPE.h, avmm_at[prev])
                        exp_at[(qt, h)] = sACT.inc(
                            sc.activation(BUF5[:, ppo:ppo + kw], PS[:, sb:sb + kw], AF.Exp,
                                          bias=nmx[:, hpar:hpar + 1], scale=1.0,
                                          accum_out=rsm[:, hpar:hpar + 1]))

                    @blk.vector
                    def _(v, qt=qt, h=h, hpar=hpar):
                        v.wait_ge(sACT.h, exp_at[(qt, h)])
                        sDVE.inc(v.reciprocal(rcp[:, hpar:hpar + 1], rsm[:, hpar:hpar + 1]))

                    @blk.scalar
                    def _(sc, qt=qt, h=h, hpar=hpar, prev=prev):
                        sc.wait_ge(sDVE.h, sDVE.n)
                        if prev is not None and prev in trdone:
                            sc.wait_ge(sPE.h, trdone[prev])
                        diag_at[(qt, h)] = sACT.inc(
                            sc.mul(diagb[:, hpar * 128:hpar * 128 + 128], ident[:, :],
                                   rcp[:, hpar:hpar + 1]))

                    ntile = nw + (1 if has_sink else 0)
                    ngrp = (ntile + 3) // 4
                    ptmm = {}

                    @blk.tensor
                    def _(te, qt=qt, h=h, sb=sb, nw=nw, has_sink=has_sink, krange=krange,
                          ntile=ntile, hpar=hpar, ppo=ppo, ptmm=ptmm):
                        te.wait_ge(sACT.h, diag_at[(qt, h)])
                        dg = diagb[:, hpar * 128:hpar * 128 + 128]
                        ins = None
                        for i in range(ntile):
                            sink_t = has_sink and (i == ntile - 1)
                            if sink_t:
                                ins = te.matmul(PS[0:4, sb + i * 128:sb + (i + 1) * 128],
                                                BUF5[:, ppo + krange:ppo + krange + SINK], dg,
                                                start=True, stop=True)
                            else:
                                ins = te.matmul(PS[:, sb + i * 128:sb + (i + 1) * 128],
                                                BUF5[:, ppo + i * 128:ppo + i * 128 + 128], dg,
                                                start=True, stop=True)
                            if i % 4 == 3 or i == ntile - 1:
                                ptmm[i // 4] = sPE.inc(ins)
                        trdone[(qt, h)] = sPE.n

                    @blk.scalar
                    def _(sc, qt=qt, h=h, sb=sb, ntile=ntile, ngrp=ngrp, ppo=ppo, ptmm=ptmm):
                        ins = None
                        for gdx in range(ngrp):
                            i0, i1 = gdx * 4, min(gdx * 4 + 4, ntile)
                            sc.wait_ge(sPE.h, ptmm[gdx])
                            ins = sc.copy(BUF5[:, ppo + i0 * 128:ppo + i1 * 128],
                                          PS[:, sb + i0 * 128:sb + i1 * 128])
                        ptev_at[(qt, h)] = sACT.inc(ins)

                    @blk.tensor
                    def _(te, qt=qt, h=h, kv=kv, nw=nw, k0t=k0t, has_sink=has_sink,
                          ppo=ppo, ycol=ycol, prev=prev):
                        te.wait_ge(sACT.h, ptev_at[(qt, h)])
                        if prev is not None and prev in yev_at:
                            te.wait_ge(sACT.h, yev_at[prev])
                        ins = None
                        for i in range(nw):
                            kt = k0t + i
                            ins = te.matmul(PS[:, ycol:ycol + 128],
                                            VT[:, kt * 512 + kv * 128:kt * 512 + kv * 128 + 128],
                                            BUF5[:, ppo + i * 128:ppo + i * 128 + 128],
                                            start=(i == 0), stop=(i == nw - 1 and not has_sink))
                        if has_sink:
                            ins = te.matmul(PS[:, ycol:ycol + 128],
                                            VT[0:4, kv * 128:kv * 128 + 128],
                                            BUF5[0:4, ppo + nw * 128:ppo + nw * 128 + 128],
                                            start=False, stop=True)
                        avmm_at[(qt, h)] = sPE.inc(ins)

                    @blk.scalar
                    def _(sc, qt=qt, h=h, ycol=ycol, par=par):
                        sc.wait_ge(sPE.h, avmm_at[(qt, h)])
                        if h == 0 and qt >= 2:
                            sc.wait_ge(sYW.h, ydma_at[qt - 2])
                        yo = (YQ0 if par == 0 else YQ1) + h * 128
                        yev_at[(qt, h)] = sACT.inc(sc.copy(BUF5[:, yo:yo + 128],
                                                           PS[:, ycol:ycol + 128]))

                @blk.sync
                def _(sy, qt=qt, b=b):
                    if qt + 2 < 16:
                        sy.wait_ge(sPE.h, scdone[qt])
                        nqt = qt + 2
                        sLDX.dma(sy.dma_start(
                            BUF2[:, (nqt % 2) * 2048:(nqt % 2) * 2048 + 2048],
                            dap(q_scr, b * 2048 + nqt * 128, 128, nseg=16, segstride=4096)))
                        qld_at[nqt] = sLDX.n
                    emit_conversion_load(sy, 8)

                @blk.gpsimd
                def _(gp, qt=qt, b=b, par=par):
                    gp.wait_ge(sACT.h, yev_at[(qt, NH - 1)])
                    yo = YQ0 if par == 0 else YQ1
                    sYW.dma(gp.dma_start(
                        dap(y_scr, b * 2048 + qt * 128, 128, nseg=16, segstride=4096),
                        BUF5[:, yo:yo + 2048]))
                    ydma_at[qt] = sYW.n
                    emit_conversion_store(gp, 8)

            @blk.sync
            def _(sy):
                emit_conversion_load(sy, conv_si[0] - conv_li[0])

    def phase_aprime():
        ntb = (B_LOC * T) // 256
        with nc.Block() as blk:
            cast_at = {}
            pool_base = sPOOL.n

            @blk.sync
            def _(sy):
                for c in range(NCH):
                    par = c % 2
                    if c >= 2:
                        sy.wait_ge(sPOOL.h, pool_base + c - 1)
                    sLD.dma(sy.dma_start(FSCR[:, par * 2048:(par + 1) * 2048],
                                         dap(wo_img, c * 2048, 2048)))

            @blk.gpsimd
            def _(gp):
                n0 = sLD.n - 16 * NCH
                for c in range(NCH):
                    par = c % 2
                    gp.wait_ge(sLD.h, n0 + 16 * (c + 1))
                    cast_at[c] = sPOOL.inc(gp.tensor_copy(BUF1[:, c * 2048:(c + 1) * 2048],
                                                          FSCR[:, par * 2048:(par + 1) * 2048]))
                cast_at["fence"] = pool_fence(gp)
                gp.wait_ge(sYW.h, sYW.n)

            ytb_at, xr_at, wo_at, ev_at = {}, {}, {}, {}
            zb_at, sq_at, stats_at, rowmath_done, apm_at = {}, {}, {}, {}, {}
            bc_done = {}

            @blk.sync
            def _(sy):
                for tb in (0, 1):
                    sLDX.dma(sy.dma_start(BUF2[:, (tb % 2) * 4096:(tb % 2) * 4096 + 4096],
                                          dap(y_scr, tb * 256, 256, nseg=16, segstride=4096)))
                    ytb_at[tb] = sLDX.n

            for tb in range(ntb):
                par = tb % 2
                xoff = (tb // 8) * (NCH * T) + (tb % 8) * 256

                for dot in range(NCH):
                    slot = dot % 4
                    gidx = tb * NCH + dot

                    @blk.sync
                    def _(sy, tb=tb, dot=dot, slot=slot, gidx=gidx, xoff=xoff):
                        if gidx >= 4:
                            pk = (tb, dot - 4) if dot >= 4 else (tb - 1, dot + 12)
                            sy.wait_ge(sDVE.h, ev_at[pk])
                        sLD.dma(sy.dma_start(XR[:, slot * 256:(slot + 1) * 256],
                                             dap(x_img, xoff + dot * T, 256)))
                        xr_at[(tb, dot)] = sLD.n

                    @blk.tensor
                    def _(te, tb=tb, dot=dot, par=par):
                        if dot == 0:
                            te.wait_ge(sPOOL.h, cast_at["fence"])
                            te.wait_ge(sLDX.h, ytb_at[tb])
                        pcol = (dot % 2) * 512
                        pk = (tb, dot - 2) if dot >= 2 else ((tb - 1, dot + 14) if tb >= 1 else None)
                        if pk is not None:
                            te.wait_ge(sDVE.h, ev_at[pk])
                        ins = None
                        for dc in range(NCH):
                            ins = te.matmul(PS[:, pcol:pcol + 256],
                                            BUF1[:, dc * 2048 + dot * 128:dc * 2048 + dot * 128 + 128],
                                            BUF2[:, par * 4096 + dc * 256:par * 4096 + dc * 256 + 256],
                                            start=(dc == 0), stop=(dc == NCH - 1))
                        wo_at[(tb, dot)] = sPE.inc(ins)

                    @blk.vector
                    def _(v, tb=tb, dot=dot, slot=slot):
                        v.wait_ge(sPE.h, wo_at[(tb, dot)])
                        v.wait_ge(sLD.h, xr_at[(tb, dot)])
                        if tb >= 1 and dot == 0:
                            v.wait_ge(sDVE.h, apm_at[(tb - 1, NCH - 1)])
                            v.wait_ge(sACT.h, sq_at[(tb - 1, NCH - 1)])
                            v.wait_ge(sPE.h, stats_at[(tb - 1, NCH - 1)])
                        pcol = (dot % 2) * 512
                        ev_at[(tb, dot)] = sDVE.inc(
                            v.tensor_add(FSCR[:, dot * 256:(dot + 1) * 256],
                                         PS[:, pcol:pcol + 256], XR[:, slot * 256:(slot + 1) * 256]))

                    @blk.scalar
                    def _(sc, tb=tb, dot=dot):
                        sc.wait_ge(sDVE.h, ev_at[(tb, dot)])
                        if (tb, dot - 2) in stats_at:
                            sc.wait_ge(sPE.h, stats_at[(tb, dot - 2)])
                        elif tb >= 1 and dot < 2:
                            sc.wait_ge(sPE.h, stats_at[(tb - 1, dot + 14)])
                        sc.copy(BUF5[:, (dot % 2) * 256:(dot % 2) * 256 + 256],
                                FSCR[:, dot * 256:(dot + 1) * 256])
                        sq_at[(tb, dot)] = sACT.inc(
                            sc.activation(BUF5[:, 512 + (dot % 2) * 256:512 + (dot % 2) * 256 + 256],
                                          BUF5[:, (dot % 2) * 256:(dot % 2) * 256 + 256], AF.Square))

                    @blk.tensor
                    def _(te, tb=tb, dot=dot):
                        te.wait_ge(sACT.h, sq_at[(tb, dot)])
                        if dot == 0 and tb >= 1:
                            te.wait_ge(sACT.h, rowmath_done[tb - 1])
                        te.matmul(PS[0:1, 1024:1280], onesb[:, 0:1],
                                  BUF5[:, (dot % 2) * 256:(dot % 2) * 256 + 256],
                                  start=(dot == 0), stop=(dot == NCH - 1))
                        te.matmul(PS[0:1, 1536:1792], onesb[:, 0:1],
                                  BUF5[:, 512 + (dot % 2) * 256:512 + (dot % 2) * 256 + 256],
                                  start=(dot == 0), stop=(dot == NCH - 1))
                        ins = te.matmul(PS[0:1, 2048:2304], wgd[:, dot:dot + 1],
                                        FSCR[:, dot * 256:(dot + 1) * 256],
                                        start=(dot == 0), stop=(dot == NCH - 1))
                        stats_at[(tb, dot)] = sPE.inc(ins)

                @blk.scalar
                def _(sc, tb=tb):
                    sc.wait_ge(sPE.h, stats_at[(tb, NCH - 1)])
                    if tb >= 1:
                        sc.wait_ge(sACT.h, bc_done[tb - 1])
                    sACT.inc(sc.mul(ROWS[0:1, 0:256], PS[0:1, 1024:1280], 1.0 / D))
                    sACT.inc(sc.mul(ROWS[0:1, 256:512], PS[0:1, 1536:1792], 1.0 / D))
                    sACT.inc(sc.copy(MROW[0:1, 0:256], PS[0:1, 2048:2304]))
                    rowmath_done[tb] = sACT.n

                @blk.gpsimd
                def _(gp, tb=tb):
                    gp.wait_ge(sACT.h, sACT.n)
                    sPOOL.inc(gp.tensor_tensor(ROWS[0:1, 512:768], ROWS[0:1, 0:256],
                                               ROWS[0:1, 0:256], op=ALU.mult))

                @blk.vector
                def _(v, tb=tb):
                    v.wait_ge(sPOOL.h, sPOOL.n)
                    sDVE.inc(v.scalar_tensor_tensor(ROWS[0:1, 768:1024], ROWS[0:1, 256:512],
                                                    EPS, ROWS[0:1, 512:768],
                                                    op0=ALU.add, op1=ALU.subtract))

                @blk.scalar
                def _(sc, tb=tb):
                    sc.wait_ge(sDVE.h, sDVE.n)
                    sACT.inc(sc.activation(ROWS[0:1, 512:768], ROWS[0:1, 768:1024], AF.Sqrt))
                    sACT.inc(sc.mul(ROWS[0:1, 256:512], ROWS[0:1, 0:256], swgd[0:1, 0:1]))

                @blk.vector
                def _(v, tb=tb):
                    v.wait_ge(sACT.h, sACT.n)
                    v.wait_ge(sMRW.h, sMRW.n)
                    sDVE.inc(v.reciprocal(ROWS[0:1, 768:1024], ROWS[0:1, 512:768]))
                    sDVE.inc(v.tensor_tensor(MROW[0:1, 256:512], MROW[0:1, 0:256],
                                             ROWS[0:1, 256:512], op=ALU.is_ge))

                @blk.gpsimd
                def _(gp, tb=tb):
                    gp.wait_ge(sDVE.h, sDVE.n)
                    sPOOL.inc(gp.tensor_tensor(ROWS[0:1, 512:768], ROWS[0:1, 0:256],
                                               ROWS[0:1, 768:1024], op=ALU.mult))

                @blk.vector
                def _(v, tb=tb):
                    v.wait_ge(sPOOL.h, sPOOL.n)
                    sDVE.inc(v.tensor_scalar_mul(ROWS[0:1, 512:768], ROWS[0:1, 512:768], -1.0))

                @blk.sync
                def _(sy, tb=tb):
                    sy.wait_ge(sDVE.h, sDVE.n)
                    sMRW.dma(sy.dma_start(bass.AP(mrow_scr, tb * 256, [[4096, 1], [1, 256]]),
                                          MROW[0:1, 256:512]))

                @blk.tensor
                def _(te, tb=tb):
                    te.wait_ge(sDVE.h, sDVE.n)
                    if tb >= 1:
                        te.wait_ge(sACT.h, bc_done[tb - 1])
                    te.matmul(PS[:, 2560:2816], onesr[0:1, :], ROWS[0:1, 768:1024],
                              start=True, stop=True)
                    sPE.inc(te.matmul(PS[:, 2816:3072], onesr[0:1, :], ROWS[0:1, 512:768],
                                      start=True, stop=True))
                    bc_done[(tb, "mm")] = sPE.n

                @blk.scalar
                def _(sc, tb=tb):
                    sc.wait_ge(sPE.h, bc_done[(tb, "mm")])
                    sc.copy(XR[:, 1024:1280], PS[:, 2560:2816])
                    bc_done[tb] = sACT.inc(sc.copy(XR[:, 1280:1536], PS[:, 2816:3072]))

                x1o = 16384 + par * 4096
                for dot in range(NCH):
                    @blk.vector
                    def _(v, tb=tb, dot=dot):
                        v.wait_ge(sACT.h, bc_done[tb])
                        if dot >= 2:
                            v.wait_ge(sPOOL.h, apm_at[(tb, dot - 2, "p")])
                        tpar = dot % 2
                        apm_at[(tb, dot)] = sDVE.inc(
                            v.tensor_tensor(XR[:, 1536 + tpar * 256:1536 + tpar * 256 + 256],
                                            FSCR[:, dot * 256:(dot + 1) * 256],
                                            XR[:, 1024:1280], op=ALU.mult))

                    @blk.gpsimd
                    def _(gp, tb=tb, dot=dot, x1o=x1o):
                        gp.wait_ge(sDVE.h, apm_at[(tb, dot)])
                        if dot == 0 and tb >= 2:
                            gp.wait_ge(sX1.h, sX1.n - 16)
                        tpar = dot % 2
                        apm_at[(tb, dot, "p")] = sPOOL.inc(
                            gp.tensor_tensor(BUF2[:, x1o + dot * 256:x1o + dot * 256 + 256],
                                             XR[:, 1536 + tpar * 256:1536 + tpar * 256 + 256],
                                             XR[:, 1280:1536], op=ALU.add))

                @blk.gpsimd
                def _(gp, tb=tb, x1o=x1o):
                    sX1.dma(gp.dma_start(dap(x1_scr, tb * 256, 256, nseg=16, segstride=4096),
                                         BUF2[:, x1o:x1o + 4096]))

                if tb + 2 < ntb:
                    @blk.sync
                    def _(sy, tb=tb):
                        sy.wait_ge(sPE.h, wo_at[(tb, NCH - 1)])
                        ntb2 = tb + 2
                        sLDX.dma(sy.dma_start(
                            BUF2[:, (ntb2 % 2) * 4096:(ntb2 % 2) * 4096 + 4096],
                            dap(y_scr, ntb2 * 256, 256, nseg=16, segstride=4096)))
                        ytb_at[ntb2] = sLDX.n

            @blk.gpsimd
            def _(gp):
                emit_conversion_store(gp, len(conv_jobs))
                gp.wait_ge(sX1.h, sX1.n)
                gp.wait_ge(sMRW.h, sMRW.n)
                gp.wait_ge(sCVO.h, sCVO.n)

            @blk.sync
            def _(sy):
                emit_conversion_load(sy, len(conv_jobs))

    def phase_b(gi):
        with nc.Block() as blk:
            x1_at, mb_at, wea_at, web_at = {}, {}, {}, {}
            hmm_at, hev_at, omm_at, cons_at = {}, {}, {}, {}
            sel_at, z2_at, sq2_at = {}, {}, {}

            @blk.sync
            def _(sy):
                sLDX.dma(sy.dma_start(KF[:, 0:8192],
                                      dap(x1_scr, gi * GB, GB, nseg=16, segstride=4096)))
                x1_at["x1"] = sLDX.n
                sLDX.dma(sy.dma_start(MROW[0:1, 0:512],
                                      bass.AP(mrow_scr, gi * GB, [[4096, 1], [1, 512]])))
                x1_at["mr"] = sLDX.n
                for i in (0, 1):
                    e, s = divmod(i, 32)
                    sLD.dma(sy.dma_start(BUF1[:, 16384 + (i % 2) * 2048:16384 + (i % 2) * 2048 + 2048],
                                         dap(wa_bf[e], s * 2048, 2048)))
                    wea_at[i] = sLD.n

            @blk.sync
            def _(sy, gi=gi):
                sLDX.dma(sy.dma_start(FSCR[:, 3072:3584],
                                      bass.AP(mrow_scr, gi * GB, [[0, 128], [1, 512]])))
                mb_at[0] = sLDX.n

            for i in range(64):
                e, ft = divmod(i, 32)
                par = i % 2
                pcol = (i % 2) * 512

                @blk.tensor
                def _(te, i=i, e=e, ft=ft, par=par, pcol=pcol):
                    te.wait_ge(sLD.h, wea_at[i])
                    if i == 0:
                        te.wait_ge(sLDX.h, x1_at["x1"])
                    if i >= 2:
                        te.wait_ge(sACT.h, hev_at[i - 2])
                    ins = None
                    for dc in range(NCH):
                        ins = te.matmul(PS[:, pcol:pcol + 512],
                                        BUF1[:, 16384 + par * 2048 + dc * 128:
                                             16384 + par * 2048 + dc * 128 + 128],
                                        KF[:, dc * 512:(dc + 1) * 512],
                                        start=(dc == 0), stop=(dc == NCH - 1))
                    hmm_at[i] = sPE.inc(ins)

                if i + 2 < 64:
                    @blk.sync
                    def _(sy, i=i):
                        sy.wait_ge(sPE.h, hmm_at[i])
                        j = i + 2
                        e2, s2 = divmod(j, 32)
                        sLD.dma(sy.dma_start(
                            BUF1[:, 16384 + (j % 2) * 2048:16384 + (j % 2) * 2048 + 2048],
                            dap(wa_bf[e2], s2 * 2048, 2048)))
                        wea_at[j] = sLD.n

                @blk.scalar
                def _(sc, i=i, e=e, ft=ft, pcol=pcol):
                    sc.wait_ge(sPE.h, hmm_at[i])
                    if e == 1 and ft < 32:
                        if ft == 0:
                            sc.wait_ge(sPE.h, omm_at[(0, 3)])
                    fn = AF.Gelu if e == 0 else AF.Relu
                    hev_at[i] = sACT.inc(
                        sc.activation(BUF1[:, ft * 512:(ft + 1) * 512], PS[:, pcol:pcol + 512],
                                      fn, bias=basb[e][:, ft:ft + 1], scale=1.0))

                if ft == 31:
                    for q in range(4):
                        @blk.sync
                        def _(sy, e=e, q=q):
                            j = e * 4 + q
                            if j >= 2:
                                pe2, pq2 = divmod(j - 2, 4)
                                sy.wait_ge(sPE.h, omm_at[(pe2, pq2)])
                            sLDQ.dma(sy.dma_start(
                                BUF2[:, (q % 2) * 16384:(q % 2) * 16384 + 16384],
                                dap(wb_bf[e], q * 16384, 16384)))
                            web_at[(e, q)] = sLDQ.n

                        @blk.tensor
                        def _(te, e=e, q=q):
                            te.wait_ge(sLDQ.h, web_at[(e, q)])
                            te.wait_ge(sACT.h, hev_at[e * 32 + 31])
                            pk = (e, q - 1) if q >= 1 else ((e - 1, 3) if e == 1 else None)
                            if pk is not None:
                                eng, val = cons_at[pk]
                                te.wait_ge(eng, val)
                            ins = None
                            for Fc in range(NFT):
                                for dot in range(4):
                                    ins = te.matmul(
                                        PS[:, 1024 + dot * 512:1024 + dot * 512 + 512],
                                        BUF2[:, (q % 2) * 16384 + Fc * 512 + dot * 128:
                                             (q % 2) * 16384 + Fc * 512 + dot * 128 + 128],
                                        BUF1[:, Fc * 512:(Fc + 1) * 512],
                                        start=(Fc == 0), stop=(Fc == NFT - 1))
                            omm_at[(e, q)] = sPE.inc(ins)

                        if e == 0:
                            @blk.scalar
                            def _(sc, q=q):
                                sc.wait_ge(sPE.h, omm_at[(0, q)])
                                ins = None
                                for dot in range(4):
                                    cg = q * 4 + dot
                                    ins = sc.copy(
                                        BUF1[:, 20480 + cg * 512:20480 + cg * 512 + 512],
                                        PS[:, 1024 + dot * 512:1024 + dot * 512 + 512])
                                cons_at[(0, q)] = (sACT.h, sACT.inc(ins))
                        else:
                            for dot in range(4):
                                @blk.vector
                                def _(v, q=q, dot=dot):
                                    cg = q * 4 + dot
                                    if dot == 0:
                                        v.wait_ge(sPE.h, omm_at[(1, q)])
                                        v.wait_ge(sLDX.h, mb_at[0])

                                    sDVE.inc(v.scalar_tensor_tensor(
                                        XR[:, 0:512],
                                        PS[:, 1024 + dot * 512:1024 + dot * 512 + 512],
                                        bbsb[1][:, cg:cg + 1], KF[:, cg * 512:(cg + 1) * 512],
                                        op0=ALU.add, op1=ALU.add))
                                    ins = v.scalar_tensor_tensor(
                                        XR[:, 512:1024],
                                        PS[:, 1024 + dot * 512:1024 + dot * 512 + 512],
                                        bdiff[:, cg:cg + 1],
                                        BUF1[:, 20480 + cg * 512:20480 + cg * 512 + 512],
                                        op0=ALU.add, op1=ALU.subtract)
                                    sel_at[cg] = sDVE.inc(ins)
                                    if dot == 3:
                                        cons_at[(1, q)] = (sDVE.h, sDVE.n)

                                @blk.vector
                                def _(v, q=q, dot=dot):
                                    cg = q * 4 + dot
                                    v.tensor_tensor(XR[:, 512:1024], XR[:, 512:1024],
                                                    FSCR[:, 3072:3584], op=ALU.mult)
                                    z2_at[cg] = sDVE.inc(
                                        v.tensor_tensor(VT[:, cg * 512:(cg + 1) * 512],
                                                        XR[:, 0:512], XR[:, 512:1024],
                                                        op=ALU.subtract))

            for cg in range(NCH):
                @blk.scalar
                def _(sc, cg=cg):
                    sc.wait_ge(sDVE.h, z2_at[cg])
                    if cg >= 2:
                        sc.wait_ge(sPE.h, sq2_at[cg - 2])
                    sq2_at[(cg, "a")] = sACT.inc(
                        sc.activation(BUF5[:, (cg % 2) * 512:(cg % 2) * 512 + 512],
                                      VT[:, cg * 512:(cg + 1) * 512], AF.Square))

                @blk.tensor
                def _(te, cg=cg):
                    te.wait_ge(sACT.h, sq2_at[(cg, "a")])
                    te.matmul(PS[0:1, 3072:3584], onesb[:, 0:1], VT[:, cg * 512:(cg + 1) * 512],
                              start=(cg == 0), stop=(cg == NCH - 1))
                    ins = te.matmul(PS[0:1, 3584:4096], onesb[:, 0:1],
                                    BUF5[:, (cg % 2) * 512:(cg % 2) * 512 + 512],
                                    start=(cg == 0), stop=(cg == NCH - 1))
                    sq2_at[cg] = sPE.inc(ins)

            @blk.scalar
            def _(sc):
                sc.wait_ge(sPE.h, sq2_at[NCH - 1])
                sACT.inc(sc.mul(ROWS[0:1, 0:512], PS[0:1, 3072:3584], 1.0 / D))
                sACT.inc(sc.mul(ROWS[0:1, 512:1024], PS[0:1, 3584:4096], 1.0 / D))

            @blk.gpsimd
            def _(gp):
                gp.wait_ge(sACT.h, sACT.n)
                sPOOL.inc(gp.tensor_tensor(MROW[0:1, 0:512], ROWS[0:1, 0:512],
                                           ROWS[0:1, 0:512], op=ALU.mult))

            @blk.vector
            def _(v):
                v.wait_ge(sPOOL.h, sPOOL.n)
                sDVE.inc(v.scalar_tensor_tensor(ROWS[0:1, 512:1024], ROWS[0:1, 512:1024],
                                                EPS, MROW[0:1, 0:512],
                                                op0=ALU.add, op1=ALU.subtract))

            @blk.scalar
            def _(sc):
                sc.wait_ge(sDVE.h, sDVE.n)
                sACT.inc(sc.activation(ROWS[0:1, 512:1024], ROWS[0:1, 512:1024], AF.Sqrt))

            @blk.vector
            def _(v):
                v.wait_ge(sACT.h, sACT.n)
                sDVE.inc(v.reciprocal(ROWS[0:1, 512:1024], ROWS[0:1, 512:1024]))

            @blk.gpsimd
            def _(gp):
                gp.wait_ge(sDVE.h, sDVE.n)
                sPOOL.inc(gp.tensor_tensor(MROW[0:1, 0:512], ROWS[0:1, 0:512],
                                           ROWS[0:1, 512:1024], op=ALU.mult))

            @blk.vector
            def _(v):
                v.wait_ge(sPOOL.h, sPOOL.n)
                sDVE.inc(v.tensor_scalar_mul(MROW[0:1, 0:512], MROW[0:1, 0:512], -1.0))

            bc2 = {}

            @blk.tensor
            def _(te):
                te.wait_ge(sDVE.h, sDVE.n)
                te.matmul(PS[:, 3072:3584], onesr[0:1, :], ROWS[0:1, 512:1024],
                          start=True, stop=True)
                sPE.inc(te.matmul(PS[:, 3584:4096], onesr[0:1, :], MROW[0:1, 0:512],
                                  start=True, stop=True))
                bc2["mm"] = sPE.n

            @blk.scalar
            def _(sc):
                sc.wait_ge(sPE.h, bc2["mm"])
                sc.copy(FSCR[:, 2048:2560], PS[:, 3072:3584])
                bc2[0] = sACT.inc(sc.copy(FSCR[:, 2560:3072], PS[:, 3584:4096]))

            ap2 = {}
            for cg in range(NCH):
                @blk.vector
                def _(v, cg=cg):
                    if cg == 0:
                        v.wait_ge(sACT.h, bc2[0])
                    if cg >= 2:
                        v.wait_ge(sPOOL.h, ap2[(cg - 2, "p")])
                    tpar = cg % 2
                    ap2[cg] = sDVE.inc(
                        v.tensor_tensor(FSCR[:, 1024 + tpar * 512:1024 + tpar * 512 + 512],
                                        VT[:, cg * 512:(cg + 1) * 512], FSCR[:, 2048:2560],
                                        op=ALU.mult))

                @blk.gpsimd
                def _(gp, cg=cg, gi=gi):
                    gp.wait_ge(sDVE.h, ap2[cg])
                    if cg >= 2:
                        gp.wait_ge(sOUT.h, sOUT.n - 16)
                    tpar = cg % 2
                    ap2[(cg, "p")] = sPOOL.inc(
                        gp.tensor_tensor(FSCR[:, tpar * 512:tpar * 512 + 512],
                                         FSCR[:, 1024 + tpar * 512:1024 + tpar * 512 + 512],
                                         FSCR[:, 2560:3072], op=ALU.add))
                    sOUT.dma(gp.dma_start(dap(out_img, cg * 4096 + gi * GB, GB),
                                          FSCR[:, tpar * 512:tpar * 512 + 512]))

    phase_a0a1(0)
    phase_a2(0)
    if stop_after != "qkv":
        phase_a4(0)
        if stop_after != "attn0":
            phase_a0a1(1)
            phase_a2(1)
            phase_a4(1)
            phase_aprime()
            if stop_after == "full":
                for gi in range(NG):
                    phase_b(gi)

    with nc.Block() as blk:
        @blk.gpsimd
        def _(gp):
            if dbg:
                sOUT.dma(gp.dma_start(dbg_t["dbg_kf"].ap(), KF[:, 0:NKV * 2048]))
                sOUT.dma(gp.dma_start(dbg_t["dbg_vt"].ap(), VT[:, :]))
                sOUT.dma(gp.dma_start(dbg_t["dbg_q"].ap(), q_scr.ap()))
                if stop_after in ("attn0", "x1", "full"):
                    sOUT.dma(gp.dma_start(dbg_t["dbg_y"].ap(), y_scr.ap()))
                if stop_after in ("x1", "full"):
                    sOUT.dma(gp.dma_start(dbg_t["dbg_x1"].ap(), x1_scr.ap()))
                    sOUT.dma(gp.dma_start(dbg_t["dbg_m"].ap(), mrow_scr.ap()))
            gp.wait_ge(sOUT.h, sOUT.n)
            gp.wait_ge(sX1.h, sX1.n)
            gp.wait_ge(sYW.h, sYW.n)
            gp.wait_ge(sQW.h, sQW.n)
            gp.wait_ge(sCVO.h, sCVO.n)
            gp.wait_ge(sMRW.h, sMRW.n)

    return nc


def kernel(L0_attn_x, L0_attn_Wq, L0_attn_Wk, L0_attn_Wv, L0_attn_Wo,
           L0_ffn_Wg, L0_ffn_W0a, L0_ffn_b0a, L0_ffn_W0b, L0_ffn_b0b,
           L0_ffn_W1a, L0_ffn_b1a, L0_ffn_W1b, L0_ffn_b1b):
    x = np.asarray(L0_attn_x, np.float32)
    common = host_inputs(L0_attn_Wq, L0_attn_Wk, L0_attn_Wv, L0_attn_Wo, L0_ffn_Wg,
                         L0_ffn_W0a, L0_ffn_b0a, L0_ffn_W0b, L0_ffn_b0b,
                         L0_ffn_W1a, L0_ffn_b1a, L0_ffn_W1b, L0_ffn_b1b)
    in_maps = []
    for core in range(8):
        imgs = [_sbimg(np.ascontiguousarray(x[core * 2 + j].T)) for j in range(2)]
        m = dict(common)
        m["x_img"] = np.ascontiguousarray(np.concatenate(imgs, axis=1))
        in_maps.append(m)
    nc = build("full", dbg=False)
    res = run_bass_kernel_spmd(nc, in_maps, core_ids=list(range(8)))
    outs = []
    for r in res.results:
        o = np.asarray(r["out_img"]).reshape(128, 16, 4096).transpose(1, 0, 2).reshape(2048, 4096)
        outs.append(np.ascontiguousarray(o[:, 0:2048].T))
        outs.append(np.ascontiguousarray(o[:, 2048:4096].T))
    return np.stack(outs).astype(np.float32)


# version 02
# speedup vs baseline: 1.3678x; 1.3678x over previous
import contextlib
import math
import numpy as np
import ml_dtypes
import concourse.bass as bass
import concourse.mybir as mybir
from concourse.bass_utils import run_bass_kernel_spmd

F32 = mybir.dt.float32
BF16 = mybir.dt.bfloat16
AF = mybir.ActivationFunctionType
ALU = mybir.AluOpType
AX = mybir.AxisListType

B_LOC, T, D, NCH = 2, 2048, 2048, 16
NH, NKV, HS = 16, 4, 128
WINDOW, SINK = 1024, 4
FF, NFT = 4096, 32
GB = 512
NG = (B_LOC * T) // GB
QSCALE = 1.0 / math.sqrt(HS)
EPS = 1e-5
NEGBIG = -1.0e30


def _sbimg(a):
    K, M = a.shape
    return np.ascontiguousarray(a.reshape(K // 128, 128, M).transpose(1, 0, 2).reshape(128, -1))


def host_inputs(Wq, Wk, Wv, Wo, Wg, W0a, b0a, W0b, b0b, W1a, b1a, W1b, b1b):
    f = np.float32
    out = {}
    out["wq_img"] = _sbimg(np.ascontiguousarray(np.asarray(Wq, f).T))
    out["wk_img"] = _sbimg(np.ascontiguousarray(np.asarray(Wk, f).T))
    out["wv_img"] = _sbimg(np.ascontiguousarray(np.asarray(Wv, f).T))
    out["wo_img"] = _sbimg(np.ascontiguousarray(np.asarray(Wo, f).T))
    for e, W in ((0, W0a), (1, W1a)):
        A = np.ascontiguousarray(np.asarray(W, f).T)
        out[f"w{e}a_img"] = np.ascontiguousarray(
            A.reshape(16, 128, 32, 128).transpose(1, 2, 0, 3).reshape(128, 65536))
    for e, W in ((0, W0b), (1, W1b)):
        Bt = np.ascontiguousarray(np.asarray(W, f).T)
        out[f"w{e}b_img"] = np.ascontiguousarray(
            Bt.reshape(32, 128, 4, 4, 128).transpose(1, 2, 0, 3, 4).reshape(128, 65536))
    out["wg_img"] = _sbimg(np.ascontiguousarray(np.asarray(Wg, f).T))
    out["b0a_img"] = np.ascontiguousarray(np.asarray(b0a, f).reshape(32, 128).T)
    out["b1a_img"] = np.ascontiguousarray(np.asarray(b1a, f).reshape(32, 128).T)
    out["b0b_img"] = np.ascontiguousarray(np.asarray(b0b, f).reshape(16, 128).T)
    out["b1b_img"] = np.ascontiguousarray(np.asarray(b1b, f).reshape(16, 128).T)
    ii = np.arange(128)[:, None]
    jj = np.arange(128)[None, :]
    m_lower = np.where(jj <= ii, 0.0, NEGBIG)
    m_far = np.where(jj > ii, 0.0, NEGBIG)
    m_farsink = np.where((jj > ii) | (jj < SINK), 0.0, NEGBIG)
    out["masks_img"] = np.concatenate([m_lower, m_far, m_farsink], 1).astype(ml_dtypes.bfloat16)
    out["ident_img"] = np.eye(128, dtype=ml_dtypes.bfloat16)
    out["onesb_img"] = np.ones((128, 1), dtype=ml_dtypes.bfloat16)
    out["onesr_img"] = np.ones((1, 128), np.float32)
    return out


def build(stop_after="full", dbg=False):
    es = contextlib.ExitStack()
    nc = bass.Bass("TRN2", target_bir_lowering=False, debug=False)

    def din(name, shape, dt):
        return nc.dram_tensor(name, shape, dt, kind="ExternalInput")

    x_img = din("x_img", [128, B_LOC * NCH * T], F32)
    wq_img = din("wq_img", [128, NCH * D], F32)
    wk_img = din("wk_img", [128, NCH * 512], F32)
    wv_img = din("wv_img", [128, NCH * 512], F32)
    wo_img = din("wo_img", [128, NCH * D], F32)
    wa_img = [din("w0a_img", [128, 65536], F32), din("w1a_img", [128, 65536], F32)]
    wb_img = [din("w0b_img", [128, 65536], F32), din("w1b_img", [128, 65536], F32)]
    wg_img = din("wg_img", [128, 32], F32)
    ba_img = [din("b0a_img", [128, 32], F32), din("b1a_img", [128, 32], F32)]
    bb_img = [din("b0b_img", [128, 16], F32), din("b1b_img", [128, 16], F32)]
    masks_img = din("masks_img", [128, 384], BF16)
    ident_img = din("ident_img", [128, 128], BF16)
    onesb_img = din("onesb_img", [128, 1], BF16)
    onesr_img = din("onesr_img", [1, 128], F32)

    q_scr = nc.dram_tensor("q_scr", [128, NH * 4096], BF16, kind="Internal")
    y_scr = nc.dram_tensor("y_scr", [128, NCH * 4096], BF16, kind="Internal")
    x1_scr = nc.dram_tensor("x1_scr", [128, NCH * 4096], BF16, kind="Internal")
    mrow_scr = nc.dram_tensor("mrow_scr", [1, 4096], F32, kind="Internal")
    wa_bf = [nc.dram_tensor(f"w{e}a_bf", [128, 65536], BF16, kind="Internal") for e in range(2)]
    wb_bf = [nc.dram_tensor(f"w{e}b_bf", [128, 65536], BF16, kind="Internal") for e in range(2)]
    out_img = nc.dram_tensor("out_img", [128, NCH * 4096], F32, kind="ExternalOutput")
    dbg_t = {}
    if dbg:
        dbg_t["dbg_kf"] = nc.dram_tensor("dbg_kf", [128, NKV * T], BF16, kind="ExternalOutput")
        dbg_t["dbg_vt"] = nc.dram_tensor("dbg_vt", [128, 16 * 512], BF16, kind="ExternalOutput")
        dbg_t["dbg_q"] = nc.dram_tensor("dbg_q", [128, NH * 4096], BF16, kind="ExternalOutput")
        dbg_t["dbg_y"] = nc.dram_tensor("dbg_y", [128, NCH * 4096], BF16, kind="ExternalOutput")
        dbg_t["dbg_x1"] = nc.dram_tensor("dbg_x1", [128, NCH * 4096], BF16, kind="ExternalOutput")
        dbg_t["dbg_m"] = nc.dram_tensor("dbg_m", [1, 4096], F32, kind="ExternalOutput")

    def dap(t, col0, cols, nseg=1, segstride=0):
        W = t.shape[1]
        if nseg == 1:
            return bass.AP(t, col0, [[W, 128], [1, cols]])
        return bass.AP(t, col0, [[W, 128], [segstride, nseg], [1, cols]])

    BUF1 = nc.alloc_sbuf_tensor("BUF1", [128, 32768], BF16)
    BUF2 = nc.alloc_sbuf_tensor("BUF2", [128, 32768], BF16)
    KF = nc.alloc_sbuf_tensor("KF", [128, 8192], BF16)
    VT = nc.alloc_sbuf_tensor("VT", [128, 8192], BF16)
    BUF5 = nc.alloc_sbuf_tensor("BUF5", [128, 7680], BF16)
    FSCR = nc.alloc_sbuf_tensor("FSCR", [128, 4096], F32)
    XR = nc.alloc_sbuf_tensor("XR", [128, 2048], F32)
    ROWS = nc.alloc_sbuf_tensor("ROWS", [1, 1024], F32)
    MROW = nc.alloc_sbuf_tensor("MROW", [1, 512], F32)
    masks = nc.alloc_sbuf_tensor("masks", [128, 384], BF16)
    ident = nc.alloc_sbuf_tensor("ident", [128, 128], BF16)
    onesb = nc.alloc_sbuf_tensor("onesb", [128, 1], BF16)
    onesr = nc.alloc_sbuf_tensor("onesr", [1, 128], F32)
    wgsb = nc.alloc_sbuf_tensor("wgsb", [128, 32], F32)
    wgd = nc.alloc_sbuf_tensor("wgd", [128, 16], F32)
    swgd = nc.alloc_sbuf_tensor("swgd", [1, 1], F32)
    basb = [nc.alloc_sbuf_tensor(f"b{e}a_sb", [128, 32], F32) for e in range(2)]
    bbsb = [nc.alloc_sbuf_tensor(f"b{e}b_sb", [128, 16], F32) for e in range(2)]
    bdiff = nc.alloc_sbuf_tensor("bdiff", [128, 16], F32)
    nmx = nc.alloc_sbuf_tensor("nmx", [128, 2], F32)
    rsm = nc.alloc_sbuf_tensor("rsm", [128, 2], F32)
    rcp = nc.alloc_sbuf_tensor("rcp", [128, 2], F32)
    diagb = nc.alloc_sbuf_tensor("diagb", [128, 256], BF16)
    fenb = nc.alloc_sbuf_tensor("fenb", [128, 32], BF16)

    PS = nc.alloc_psum_tensor("PS", [128, 4096], F32)

    PP0, PP1, YQ0, YQ1, CBBF_O = 0, 1280, 2560, 4608, 6656

    class Sem:
        def __init__(self, name):
            self.h = es.enter_context(nc.semaphore(name))
            self.n = 0

        def inc(self, instr, k=1):
            instr.then_inc(self.h, k)
            self.n += k
            return self.n

        def dma(self, instr):
            return self.inc(instr, 16)

    sPE, sACT, sDVE, sPOOL = Sem("sPE"), Sem("sACT"), Sem("sDVE"), Sem("sPOOL")

    def pool_fence(gp):
        gp.tensor_copy(fenb[:, 0:16], fenb[:, 16:32])
        gp.tensor_copy(fenb[:, 16:32], fenb[:, 0:16])
        gp.tensor_copy(fenb[:, 0:16], fenb[:, 16:32])
        return sPOOL.inc(gp.tensor_copy(fenb[:, 16:32], fenb[:, 0:16]))
    sLD, sLDQ, sLDX = Sem("sLD"), Sem("sLDQ"), Sem("sLDX")
    sCV, sCVO = Sem("sCV"), Sem("sCVO")
    sQW, sYW, sX1, sOUT, sMRW = Sem("sQW"), Sem("sYW"), Sem("sX1"), Sem("sOUT"), Sem("sMRW")

    CVW = 1024
    conv_jobs = []
    for e in range(2):
        for s in range(65536 // CVW):
            conv_jobs.append((wa_img[e], wa_bf[e], s * CVW))
    for e in range(2):
        for s in range(65536 // CVW):
            conv_jobs.append((wb_img[e], wb_bf[e], s * CVW))
    conv_li = [0]
    conv_si = [0]

    def emit_conversion_load(sy, njobs):
        n = 0
        while n < njobs and conv_li[0] < len(conv_jobs):
            i = conv_li[0]
            conv_li[0] += 1
            n += 1
            par = i % 2
            src, dst, col = conv_jobs[i]
            if i >= 2:
                sy.wait_ge(sCVO.h, 16 * (i - 1))
            sy.dma_start(FSCR[:, par * 1024:(par + 1) * 1024], dap(src, col, CVW)).then_inc(sCV.h, 16)
            sCV.n += 16

    def emit_conversion_store(gp, njobs):
        n = 0
        while n < njobs and conv_si[0] < len(conv_jobs):
            i = conv_si[0]
            conv_si[0] += 1
            n += 1
            par = i % 2
            src, dst, col = conv_jobs[i]
            gp.wait_ge(sCV.h, 16 * (i + 1))
            if i >= 1:
                gp.wait_ge(sCVO.h, 16 * i)
            gp.tensor_copy(BUF5[:, CBBF_O:CBBF_O + CVW], FSCR[:, par * 1024:(par + 1) * 1024])
            gp.dma_start(dap(dst, col, CVW), BUF5[:, CBBF_O:CBBF_O + CVW]).then_inc(sCVO.h, 16)
            sCVO.n += 16

    with nc.Block() as blk:
        @blk.sync
        def _(sy):
            for dst, src in ((masks, masks_img), (ident, ident_img), (onesb, onesb_img),
                             (onesr, onesr_img),
                             (wgsb, wg_img), (basb[0], ba_img[0]), (basb[1], ba_img[1]),
                             (bbsb[0], bb_img[0]), (bbsb[1], bb_img[1])):
                sLD.dma(sy.dma_start(dst[:, :], src.ap()))
            sy.wait_ge(sLD.h, sLD.n)

        @blk.vector
        def _(v):
            v.wait_ge(sLD.h, sLD.n)
            v.tensor_tensor(bdiff[:, :], bbsb[1][:, :], bbsb[0][:, :], op=ALU.subtract)
            sDVE.inc(v.tensor_tensor(wgd[:, :],
                                     bass.AP(wgsb, 0, [[32, 128], [2, 16]]),
                                     bass.AP(wgsb, 1, [[32, 128], [2, 16]]), op=ALU.subtract))

        @blk.gpsimd
        def _(gp):
            gp.wait_ge(sDVE.h, sDVE.n)
            sPOOL.inc(gp.tensor_reduce(ROWS[0:1, 0:1], wgd[:, :], axis=AX.XYZWC, op=ALU.add))

        @blk.scalar
        def _(sc):
            sc.wait_ge(sPOOL.h, sPOOL.n)
            sACT.inc(sc.copy(swgd[0:1, 0:1], ROWS[0:1, 0:1]))

    def phase_a0a1(b):
        xbase = b * (NCH * T)
        with nc.Block() as blk:
            strips = ([("x", c) for c in range(NCH)]
                      + [("wk", s) for s in range(4)] + [("wv", s) for s in range(4)])
            cast_at = {}
            pool_base = sPOOL.n

            @blk.sync
            def _(sy):
                for i, (kind, idx) in enumerate(strips):
                    par = i % 2
                    if i >= 2:
                        sy.wait_ge(sPOOL.h, pool_base + i - 1)
                    if kind == "x":
                        src = dap(x_img, xbase + idx * T, T)
                    elif kind == "wk":
                        src = dap(wk_img, idx * 2048, 2048)
                    else:
                        src = dap(wv_img, idx * 2048, 2048)
                    sLD.dma(sy.dma_start(FSCR[:, par * 2048:(par + 1) * 2048], src))

            @blk.gpsimd
            def _(gp):
                n0 = sLD.n - 16 * len(strips)
                for i, (kind, idx) in enumerate(strips):
                    par = i % 2
                    gp.wait_ge(sLD.h, n0 + 16 * (i + 1))
                    if kind == "x":
                        dst = BUF1[:, idx * 2048:(idx + 1) * 2048]
                    elif kind == "wk":
                        dst = BUF2[:, idx * 2048:(idx + 1) * 2048]
                    else:
                        dst = BUF2[:, 8192 + idx * 2048:8192 + (idx + 1) * 2048]
                    cast_at[(kind, idx)] = sPOOL.inc(
                        gp.tensor_copy(dst, FSCR[:, par * 2048:(par + 1) * 2048]))
                cast_at["fence"] = pool_fence(gp)

            kv_jobs = [("k", ko, tb) for ko in range(4) for tb in range(4)] \
                    + [("v", tt, 0) for tt in range(16)]
            mm_at, ev_at = {}, {}
            act_base = sACT.n

            @blk.tensor
            def _(te):
                te.wait_ge(sPOOL.h, cast_at["fence"])
                for j, job in enumerate(kv_jobs):
                    pcol = (j % 2) * 512
                    if j >= 2:
                        te.wait_ge(sACT.h, act_base + j - 1)
                    ins = None
                    for dc in range(NCH):
                        if job[0] == "k":
                            ko, tb = job[1], job[2]
                            lhsT = BUF2[:, dc * 512 + ko * 128:dc * 512 + ko * 128 + 128]
                            rhs = BUF1[:, dc * 2048 + tb * 512:dc * 2048 + tb * 512 + 512]
                        else:
                            tt = job[1]
                            lhsT = BUF1[:, dc * 2048 + tt * 128:dc * 2048 + tt * 128 + 128]
                            rhs = BUF2[:, 8192 + dc * 512:8192 + dc * 512 + 512]
                        ins = te.matmul(PS[:, pcol:pcol + 512], lhsT, rhs,
                                        start=(dc == 0), stop=(dc == NCH - 1))
                    mm_at[j] = sPE.inc(ins)

            @blk.scalar
            def _(sc):
                for j, job in enumerate(kv_jobs):
                    pcol = (j % 2) * 512
                    sc.wait_ge(sPE.h, mm_at[j])
                    if job[0] == "k":
                        ko, tb = job[1], job[2]
                        dst = KF[:, ko * 2048 + tb * 512:ko * 2048 + tb * 512 + 512]
                    else:
                        tt = job[1]
                        dst = VT[:, tt * 512:(tt + 1) * 512]
                    ev_at[j] = sACT.inc(sc.copy(dst, PS[:, pcol:pcol + 512]))

    def phase_a2(b):
        with nc.Block() as blk:
            cast_at = {}
            q_mm = {}
            act_base = sACT.n
            q_half_done = {}
            blk_pool_base = sPOOL.n
            qw_base = sQW.n

            for half in range(2):
                pool_base0 = blk_pool_base

                @blk.sync
                def _(sy, half=half):
                    def cast_ctr(h, c):
                        return pool_base0 + h * (NCH + 1) + c + 1
                    for c in range(NCH):
                        i = half * NCH + c
                        par = i % 4
                        if i >= 4:
                            ph, pc = divmod(i - 4, NCH)
                            sy.wait_ge(sPOOL.h, cast_ctr(ph, pc))
                        sLD.dma(sy.dma_start(FSCR[:, par * 1024:(par + 1) * 1024],
                                             dap(wq_img, c * 2048 + half * 1024, 1024)))

                @blk.gpsimd
                def _(gp, half=half):
                    n0 = sLD.n - 16 * NCH
                    for c in range(NCH):
                        i = half * NCH + c
                        par = i % 4
                        gp.wait_ge(sLD.h, n0 + 16 * (c + 1))
                        if half == 1 and c == 0:
                            gp.wait_ge(sPE.h, q_half_done[0])
                        cast_at[(half, c)] = sPOOL.inc(
                            gp.tensor_copy(BUF2[:, c * 1024:(c + 1) * 1024],
                                           FSCR[:, par * 1024:(par + 1) * 1024]))
                    cast_at[(half, "fence")] = pool_fence(gp)

                @blk.tensor
                def _(te, half=half):
                    te.wait_ge(sPOOL.h, cast_at[(half, "fence")])
                    for o8 in range(8):
                        for tb in range(4):
                            j = half * 32 + o8 * 4 + tb
                            pcol = (j % 2) * 512
                            if j >= 2:
                                te.wait_ge(sACT.h, act_base + j - 1)
                            ins = None
                            for dc in range(NCH):
                                ins = te.matmul(PS[:, pcol:pcol + 512],
                                                BUF2[:, dc * 1024 + o8 * 128:dc * 1024 + o8 * 128 + 128],
                                                BUF1[:, dc * 2048 + tb * 512:dc * 2048 + tb * 512 + 512],
                                                start=(dc == 0), stop=(dc == NCH - 1))
                            q_mm[j] = sPE.inc(ins)
                    if half == 0:
                        q_half_done[0] = sPE.n

                q_ev = {}

                @blk.scalar
                def _(sc, half=half):
                    for o8 in range(8):
                        for tb in range(4):
                            j = half * 32 + o8 * 4 + tb
                            pcol = (j % 2) * 512
                            par4 = j % 4
                            sc.wait_ge(sPE.h, q_mm[j])
                            if j >= 4:
                                sc.wait_ge(sQW.h, qw_base + 16 * (j - 3))
                            dst = BUF2[:, 16384 + par4 * 512:16384 + par4 * 512 + 512]
                            q_ev[j] = sACT.inc(sc.mul(dst, PS[:, pcol:pcol + 512], QSCALE))

                @blk.sync
                def _(sy, half=half, b=b):
                    for o8 in range(8):
                        for tb in range(4):
                            j = half * 32 + o8 * 4 + tb
                            par4 = j % 4
                            sy.wait_ge(sACT.h, q_ev[j])
                            o = half * 8 + o8
                            sQW.dma(sy.dma_start(
                                dap(q_scr, o * 4096 + b * 2048 + tb * 512, 512),
                                BUF2[:, 16384 + par4 * 512:16384 + par4 * 512 + 512]))

    def qt_geom(qt):
        nw = min(qt + 1, 9)
        k0t = qt - nw + 1
        krange = nw * 128
        has_sink = qt >= 9
        kw = krange + (SINK if has_sink else 0)
        return nw, k0t, krange, has_sink, kw

    def phase_a4(b):
        NHEADS = 16 * NH
        with nc.Block() as blk:
            qld_at, scdone = {}, {}
            sc_at, max_at, exp_at, recip_at, diag_at = {}, {}, {}, {}, {}
            trans_at, ptev_at, av_at, yev_at, ydma_at = {}, {}, {}, {}, {}

            @blk.sync
            def _(sy):
                for qt in (0, 1):
                    sLDX.dma(sy.dma_start(BUF2[:, (qt % 2) * 2048:(qt % 2) * 2048 + 2048],
                                          dap(q_scr, b * 2048 + qt * 128, 128, nseg=16, segstride=4096)))
                    qld_at[qt] = sLDX.n

            def geom(g):
                qt, h = divmod(g, NH)
                nw, k0t, krange, has_sink, kw = qt_geom(qt)
                return dict(qt=qt, h=h, kv=h // 4, par=g % 2, nw=nw, k0t=k0t,
                            krange=krange, has_sink=has_sink, kw=kw,
                            ntile=nw + (1 if has_sink else 0))

            for t in range(NHEADS + 2):
                gs = geom(t) if t < NHEADS else None
                g1 = geom(t - 1) if t >= 1 else None
                g2 = geom(t - 2) if t >= 2 else None

                if 1 <= t <= NHEADS:
                    @blk.vector
                    def _(v, t=t, g1=g1):
                        v.wait_ge(sACT.h, exp_at[t - 1])
                        recip_at[t - 1] = sDVE.inc(
                            v.reciprocal(rcp[:, g1["par"]:g1["par"] + 1],
                                         rsm[:, g1["par"]:g1["par"] + 1]))

                    @blk.scalar
                    def _(sc, t=t, g1=g1):
                        sc.wait_ge(sDVE.h, recip_at[t - 1])
                        if t >= 3:
                            sc.wait_ge(sPE.h, trans_at[t - 3])
                        diag_at[t - 1] = sACT.inc(
                            sc.mul(diagb[:, g1["par"] * 128:g1["par"] * 128 + 128],
                                   ident[:, :], rcp[:, g1["par"]:g1["par"] + 1]))

                @blk.tensor
                def _(te, t=t, gs=gs, g1=g1, g2=g2, b=b):
                    if t >= 2:
                        te.wait_ge(sDVE.h, ptev_at[t - 2])
                        if t >= 4:
                            te.wait_ge(sACT.h, yev_at[t - 4])
                        sb2 = 1536 * g2["par"]
                        ppo = PP0 if g2["par"] == 0 else PP1
                        ycol = 3072 if g2["par"] == 0 else 3584
                        ins = None
                        for i in range(g2["nw"]):
                            kt = g2["k0t"] + i
                            ins = te.matmul(PS[:, ycol:ycol + 128],
                                            VT[:, kt * 512 + g2["kv"] * 128:kt * 512 + g2["kv"] * 128 + 128],
                                            BUF5[:, ppo + i * 128:ppo + i * 128 + 128],
                                            start=(i == 0),
                                            stop=(i == g2["nw"] - 1 and not g2["has_sink"]))
                        if g2["has_sink"]:
                            ins = te.matmul(PS[:, ycol:ycol + 128],
                                            VT[0:4, g2["kv"] * 128:g2["kv"] * 128 + 128],
                                            BUF5[0:4, ppo + g2["nw"] * 128:ppo + g2["nw"] * 128 + 128],
                                            start=False, stop=True)
                        av_at[t - 2] = sPE.inc(ins)
                    if t < NHEADS:
                        te.wait_ge(sLDX.h, qld_at[gs["qt"]])
                        if t >= 2:
                            te.wait_ge(sACT.h, exp_at[t - 2])
                            te.wait_ge(sDVE.h, ptev_at[t - 2])
                        sb = 1536 * gs["par"]
                        lhsq = BUF2[:, (gs["qt"] % 2) * 2048 + gs["h"] * 128:
                                    (gs["qt"] % 2) * 2048 + gs["h"] * 128 + 128]
                        col = 0
                        ins = None
                        while col < gs["krange"]:
                            w = min(512, gs["krange"] - col)
                            rhs = KF[:, gs["kv"] * 2048 + gs["k0t"] * 128 + col:
                                     gs["kv"] * 2048 + gs["k0t"] * 128 + col + w]
                            ins = te.matmul(PS[:, sb + col:sb + col + w], lhsq, rhs,
                                            start=True, stop=True)
                            col += w
                        if gs["has_sink"]:
                            ins = te.matmul(PS[:, sb + gs["krange"]:sb + gs["krange"] + SINK],
                                            lhsq, KF[:, gs["kv"] * 2048:gs["kv"] * 2048 + SINK],
                                            start=True, stop=True)
                        sc_at[t] = sPE.inc(ins)
                        if gs["h"] == NH - 1:
                            scdone[gs["qt"]] = sPE.n
                    if 1 <= t <= NHEADS:
                        te.wait_ge(sACT.h, diag_at[t - 1])
                        sb1 = 1536 * g1["par"]
                        ppo1 = PP0 if g1["par"] == 0 else PP1
                        dg = diagb[:, g1["par"] * 128:g1["par"] * 128 + 128]
                        ins = None
                        for i in range(g1["ntile"]):
                            sink_t = g1["has_sink"] and (i == g1["ntile"] - 1)
                            if sink_t:
                                ins = te.matmul(PS[0:4, sb1 + i * 128:sb1 + (i + 1) * 128],
                                                BUF5[:, ppo1 + g1["krange"]:ppo1 + g1["krange"] + SINK],
                                                dg, start=True, stop=True)
                            else:
                                ins = te.matmul(PS[:, sb1 + i * 128:sb1 + (i + 1) * 128],
                                                BUF5[:, ppo1 + i * 128:ppo1 + i * 128 + 128],
                                                dg, start=True, stop=True)
                        trans_at[t - 1] = sPE.inc(ins)

                @blk.vector
                def _(v, t=t, gs=gs, g1=g1):
                    if t < NHEADS:
                        v.wait_ge(sPE.h, sc_at[t])
                        if t >= 2:
                            v.wait_ge(sACT.h, exp_at[t - 2])
                        sb = 1536 * gs["par"]
                        qt = gs["qt"]
                        krange = gs["krange"]
                        v.tensor_tensor(PS[:, sb + krange - 128:sb + krange],
                                        PS[:, sb + krange - 128:sb + krange],
                                        masks[:, 0:128], op=ALU.add)
                        if qt >= 9:
                            v.tensor_tensor(PS[:, sb:sb + 128], PS[:, sb:sb + 128],
                                            masks[:, 128:256], op=ALU.add)
                        elif qt == 8:
                            v.tensor_tensor(PS[:, sb:sb + 128], PS[:, sb:sb + 128],
                                            masks[:, 256:384], op=ALU.add)
                        max_at[t] = sDVE.inc(
                            v.reduce_max(nmx[:, gs["par"]:gs["par"] + 1],
                                         PS[:, sb + krange - 128:sb + krange],
                                         axis=AX.X, negate=True))
                    if 1 <= t <= NHEADS:
                        v.wait_ge(sPE.h, trans_at[t - 1])
                        sb1 = 1536 * g1["par"]
                        ppo1 = PP0 if g1["par"] == 0 else PP1
                        w = g1["ntile"] * 128
                        ptev_at[t - 1] = sDVE.inc(
                            v.tensor_copy(BUF5[:, ppo1:ppo1 + w], PS[:, sb1:sb1 + w]))

                @blk.scalar
                def _(sc, t=t, gs=gs, g2=g2):
                    if t < NHEADS:
                        sc.wait_ge(sDVE.h, max_at[t])
                        if t >= 2:
                            sc.wait_ge(sPE.h, av_at[t - 2])
                        sb = 1536 * gs["par"]
                        ppo = PP0 if gs["par"] == 0 else PP1
                        exp_at[t] = sACT.inc(
                            sc.activation(BUF5[:, ppo:ppo + gs["kw"]], PS[:, sb:sb + gs["kw"]],
                                          AF.Exp, bias=nmx[:, gs["par"]:gs["par"] + 1],
                                          scale=1.0, accum_out=rsm[:, gs["par"]:gs["par"] + 1]))
                    if t >= 2:
                        sc.wait_ge(sPE.h, av_at[t - 2])
                        qt2, h2 = divmod(t - 2, NH)
                        if h2 == 0 and qt2 >= 2:
                            sc.wait_ge(sYW.h, ydma_at[qt2 - 2])
                        ycol = 3072 if g2["par"] == 0 else 3584
                        yo = (YQ0 if qt2 % 2 == 0 else YQ1) + h2 * 128
                        yev_at[t - 2] = sACT.inc(sc.copy(BUF5[:, yo:yo + 128],
                                                         PS[:, ycol:ycol + 128]))

                if t < NHEADS and t % NH == NH - 1:
                    @blk.sync
                    def _(sy, t=t, b=b):
                        qt = t // NH
                        if qt + 2 < 16:
                            sy.wait_ge(sPE.h, scdone[qt])
                            nqt = qt + 2
                            sLDX.dma(sy.dma_start(
                                BUF2[:, (nqt % 2) * 2048:(nqt % 2) * 2048 + 2048],
                                dap(q_scr, b * 2048 + nqt * 128, 128, nseg=16, segstride=4096)))
                            qld_at[nqt] = sLDX.n
                        emit_conversion_load(sy, 8)

                if t >= 2 and (t - 2) % NH == NH - 1:
                    @blk.gpsimd
                    def _(gp, t=t, b=b):
                        qt = (t - 2) // NH
                        gp.wait_ge(sACT.h, yev_at[t - 2])
                        yo = YQ0 if qt % 2 == 0 else YQ1
                        sYW.dma(gp.dma_start(
                            dap(y_scr, b * 2048 + qt * 128, 128, nseg=16, segstride=4096),
                            BUF5[:, yo:yo + 2048]))
                        ydma_at[qt] = sYW.n
                        emit_conversion_store(gp, 8)

            @blk.sync
            def _(sy):
                emit_conversion_load(sy, conv_si[0] - conv_li[0])

    def phase_aprime():
        ntb = (B_LOC * T) // 256
        with nc.Block() as blk:
            cast_at = {}
            pool_base = sPOOL.n

            @blk.sync
            def _(sy):
                for c in range(NCH):
                    par = c % 2
                    if c >= 2:
                        sy.wait_ge(sPOOL.h, pool_base + c - 1)
                    sLD.dma(sy.dma_start(FSCR[:, par * 2048:(par + 1) * 2048],
                                         dap(wo_img, c * 2048, 2048)))

            @blk.gpsimd
            def _(gp):
                n0 = sLD.n - 16 * NCH
                for c in range(NCH):
                    par = c % 2
                    gp.wait_ge(sLD.h, n0 + 16 * (c + 1))
                    cast_at[c] = sPOOL.inc(gp.tensor_copy(BUF1[:, c * 2048:(c + 1) * 2048],
                                                          FSCR[:, par * 2048:(par + 1) * 2048]))
                cast_at["fence"] = pool_fence(gp)
                gp.wait_ge(sYW.h, sYW.n)

            ytb_at, xr_at, wo_at, ev_at = {}, {}, {}, {}
            zb_at, sq_at, stats_at, rowmath_done, apm_at = {}, {}, {}, {}, {}
            bc_done = {}

            @blk.sync
            def _(sy):
                for tb in (0, 1):
                    sLDX.dma(sy.dma_start(BUF2[:, (tb % 2) * 4096:(tb % 2) * 4096 + 4096],
                                          dap(y_scr, tb * 256, 256, nseg=16, segstride=4096)))
                    ytb_at[tb] = sLDX.n

            def emit_stats(te, tb, dot):
                te.wait_ge(sACT.h, sq_at[(tb, dot)])
                if dot == 0 and tb >= 1:
                    te.wait_ge(sACT.h, rowmath_done[tb - 1])
                te.matmul(PS[0:1, 1024:1280], onesb[:, 0:1],
                          BUF5[:, (dot % 2) * 256:(dot % 2) * 256 + 256],
                          start=(dot == 0), stop=(dot == NCH - 1))
                te.matmul(PS[0:1, 1536:1792], onesb[:, 0:1],
                          BUF5[:, 512 + (dot % 2) * 256:512 + (dot % 2) * 256 + 256],
                          start=(dot == 0), stop=(dot == NCH - 1))
                ins = te.matmul(PS[0:1, 2048:2304], wgd[:, dot:dot + 1],
                                FSCR[:, dot * 256:(dot + 1) * 256],
                                start=(dot == 0), stop=(dot == NCH - 1))
                stats_at[(tb, dot)] = sPE.inc(ins)

            for tb in range(ntb):
                par = tb % 2
                xoff = (tb // 8) * (NCH * T) + (tb % 8) * 256

                for dot in range(NCH):
                    slot = dot % 4
                    gidx = tb * NCH + dot

                    @blk.sync
                    def _(sy, tb=tb, dot=dot, slot=slot, gidx=gidx, xoff=xoff):
                        if gidx >= 4:
                            pk = (tb, dot - 4) if dot >= 4 else (tb - 1, dot + 12)
                            sy.wait_ge(sDVE.h, ev_at[pk])
                        sLD.dma(sy.dma_start(XR[:, slot * 256:(slot + 1) * 256],
                                             dap(x_img, xoff + dot * T, 256)))
                        xr_at[(tb, dot)] = sLD.n

                    @blk.tensor
                    def _(te, tb=tb, dot=dot, par=par):
                        if dot == 0:
                            te.wait_ge(sPOOL.h, cast_at["fence"])
                            te.wait_ge(sLDX.h, ytb_at[tb])
                        pcol = (dot % 2) * 512
                        pk = (tb, dot - 2) if dot >= 2 else ((tb - 1, dot + 14) if tb >= 1 else None)
                        if pk is not None:
                            te.wait_ge(sDVE.h, ev_at[pk])
                        ins = None
                        for dc in range(NCH):
                            ins = te.matmul(PS[:, pcol:pcol + 256],
                                            BUF1[:, dc * 2048 + dot * 128:dc * 2048 + dot * 128 + 128],
                                            BUF2[:, par * 4096 + dc * 256:par * 4096 + dc * 256 + 256],
                                            start=(dc == 0), stop=(dc == NCH - 1))
                        wo_at[(tb, dot)] = sPE.inc(ins)
                        if dot >= 1:
                            emit_stats(te, tb, dot - 1)

                    @blk.vector
                    def _(v, tb=tb, dot=dot, slot=slot):
                        v.wait_ge(sPE.h, wo_at[(tb, dot)])
                        v.wait_ge(sLD.h, xr_at[(tb, dot)])
                        if tb >= 1 and dot == 0:
                            v.wait_ge(sDVE.h, apm_at[(tb - 1, NCH - 1)])
                            v.wait_ge(sACT.h, sq_at[(tb - 1, NCH - 1)])
                            v.wait_ge(sPE.h, stats_at[(tb - 1, NCH - 1)])
                        pcol = (dot % 2) * 512
                        ev_at[(tb, dot)] = sDVE.inc(
                            v.tensor_add(FSCR[:, dot * 256:(dot + 1) * 256],
                                         PS[:, pcol:pcol + 256], XR[:, slot * 256:(slot + 1) * 256]))

                    @blk.scalar
                    def _(sc, tb=tb, dot=dot):
                        sc.wait_ge(sDVE.h, ev_at[(tb, dot)])
                        if (tb, dot - 2) in stats_at:
                            sc.wait_ge(sPE.h, stats_at[(tb, dot - 2)])
                        elif tb >= 1 and dot < 2:
                            sc.wait_ge(sPE.h, stats_at[(tb - 1, dot + 14)])
                        sc.copy(BUF5[:, (dot % 2) * 256:(dot % 2) * 256 + 256],
                                FSCR[:, dot * 256:(dot + 1) * 256])
                        sq_at[(tb, dot)] = sACT.inc(
                            sc.activation(BUF5[:, 512 + (dot % 2) * 256:512 + (dot % 2) * 256 + 256],
                                          BUF5[:, (dot % 2) * 256:(dot % 2) * 256 + 256], AF.Square))

                @blk.tensor
                def _(te, tb=tb):
                    emit_stats(te, tb, NCH - 1)

                @blk.scalar
                def _(sc, tb=tb):
                    sc.wait_ge(sPE.h, stats_at[(tb, NCH - 1)])
                    if tb >= 1:
                        sc.wait_ge(sACT.h, bc_done[tb - 1])
                    sACT.inc(sc.mul(ROWS[0:1, 0:256], PS[0:1, 1024:1280], 1.0 / D))
                    sACT.inc(sc.mul(ROWS[0:1, 256:512], PS[0:1, 1536:1792], 1.0 / D))
                    sACT.inc(sc.copy(MROW[0:1, 0:256], PS[0:1, 2048:2304]))
                    rowmath_done[tb] = sACT.n

                @blk.gpsimd
                def _(gp, tb=tb):
                    gp.wait_ge(sACT.h, sACT.n)
                    sPOOL.inc(gp.tensor_tensor(ROWS[0:1, 512:768], ROWS[0:1, 0:256],
                                               ROWS[0:1, 0:256], op=ALU.mult))

                @blk.vector
                def _(v, tb=tb):
                    v.wait_ge(sPOOL.h, sPOOL.n)
                    sDVE.inc(v.scalar_tensor_tensor(ROWS[0:1, 768:1024], ROWS[0:1, 256:512],
                                                    EPS, ROWS[0:1, 512:768],
                                                    op0=ALU.add, op1=ALU.subtract))

                @blk.scalar
                def _(sc, tb=tb):
                    sc.wait_ge(sDVE.h, sDVE.n)
                    sACT.inc(sc.activation(ROWS[0:1, 512:768], ROWS[0:1, 768:1024], AF.Sqrt))
                    sACT.inc(sc.mul(ROWS[0:1, 256:512], ROWS[0:1, 0:256], swgd[0:1, 0:1]))

                @blk.vector
                def _(v, tb=tb):
                    v.wait_ge(sACT.h, sACT.n)
                    v.wait_ge(sMRW.h, sMRW.n)
                    sDVE.inc(v.reciprocal(ROWS[0:1, 768:1024], ROWS[0:1, 512:768]))
                    sDVE.inc(v.tensor_tensor(MROW[0:1, 256:512], MROW[0:1, 0:256],
                                             ROWS[0:1, 256:512], op=ALU.is_ge))

                @blk.gpsimd
                def _(gp, tb=tb):
                    gp.wait_ge(sDVE.h, sDVE.n)
                    sPOOL.inc(gp.tensor_tensor(ROWS[0:1, 512:768], ROWS[0:1, 0:256],
                                               ROWS[0:1, 768:1024], op=ALU.mult))

                @blk.vector
                def _(v, tb=tb):
                    v.wait_ge(sPOOL.h, sPOOL.n)
                    sDVE.inc(v.tensor_scalar_mul(ROWS[0:1, 512:768], ROWS[0:1, 512:768], -1.0))

                @blk.sync
                def _(sy, tb=tb):
                    sy.wait_ge(sDVE.h, sDVE.n)
                    sMRW.dma(sy.dma_start(bass.AP(mrow_scr, tb * 256, [[4096, 1], [1, 256]]),
                                          MROW[0:1, 256:512]))

                @blk.tensor
                def _(te, tb=tb):
                    te.wait_ge(sDVE.h, sDVE.n)
                    if tb >= 1:
                        te.wait_ge(sACT.h, bc_done[tb - 1])
                    te.matmul(PS[:, 2560:2816], onesr[0:1, :], ROWS[0:1, 768:1024],
                              start=True, stop=True)
                    sPE.inc(te.matmul(PS[:, 2816:3072], onesr[0:1, :], ROWS[0:1, 512:768],
                                      start=True, stop=True))
                    bc_done[(tb, "mm")] = sPE.n

                @blk.scalar
                def _(sc, tb=tb):
                    sc.wait_ge(sPE.h, bc_done[(tb, "mm")])
                    sc.copy(XR[:, 1024:1280], PS[:, 2560:2816])
                    bc_done[tb] = sACT.inc(sc.copy(XR[:, 1280:1536], PS[:, 2816:3072]))

                x1o = 16384 + par * 4096
                for dot in range(NCH):
                    @blk.vector
                    def _(v, tb=tb, dot=dot):
                        v.wait_ge(sACT.h, bc_done[tb])
                        if dot >= 2:
                            v.wait_ge(sPOOL.h, apm_at[(tb, dot - 2, "p")])
                        tpar = dot % 2
                        apm_at[(tb, dot)] = sDVE.inc(
                            v.tensor_tensor(XR[:, 1536 + tpar * 256:1536 + tpar * 256 + 256],
                                            FSCR[:, dot * 256:(dot + 1) * 256],
                                            XR[:, 1024:1280], op=ALU.mult))

                    @blk.gpsimd
                    def _(gp, tb=tb, dot=dot, x1o=x1o):
                        gp.wait_ge(sDVE.h, apm_at[(tb, dot)])
                        if dot == 0 and tb >= 2:
                            gp.wait_ge(sX1.h, sX1.n - 16)
                        tpar = dot % 2
                        apm_at[(tb, dot, "p")] = sPOOL.inc(
                            gp.tensor_tensor(BUF2[:, x1o + dot * 256:x1o + dot * 256 + 256],
                                             XR[:, 1536 + tpar * 256:1536 + tpar * 256 + 256],
                                             XR[:, 1280:1536], op=ALU.add))

                @blk.gpsimd
                def _(gp, tb=tb, x1o=x1o):
                    sX1.dma(gp.dma_start(dap(x1_scr, tb * 256, 256, nseg=16, segstride=4096),
                                         BUF2[:, x1o:x1o + 4096]))

                if tb + 2 < ntb:
                    @blk.sync
                    def _(sy, tb=tb):
                        sy.wait_ge(sPE.h, wo_at[(tb, NCH - 1)])
                        ntb2 = tb + 2
                        sLDX.dma(sy.dma_start(
                            BUF2[:, (ntb2 % 2) * 4096:(ntb2 % 2) * 4096 + 4096],
                            dap(y_scr, ntb2 * 256, 256, nseg=16, segstride=4096)))
                        ytb_at[ntb2] = sLDX.n

            @blk.gpsimd
            def _(gp):
                emit_conversion_store(gp, len(conv_jobs))
                gp.wait_ge(sX1.h, sX1.n)
                gp.wait_ge(sMRW.h, sMRW.n)
                gp.wait_ge(sCVO.h, sCVO.n)

            @blk.sync
            def _(sy):
                emit_conversion_load(sy, len(conv_jobs))

    def phase_b(gi):
        with nc.Block() as blk:
            x1_at, mb_at, wea_at, web_at = {}, {}, {}, {}
            hmm_at, hev_at, omm_at, cons_at = {}, {}, {}, {}
            sel_at, z2_at, sq2_at = {}, {}, {}

            @blk.sync
            def _(sy):
                sLDX.dma(sy.dma_start(KF[:, 0:8192],
                                      dap(x1_scr, gi * GB, GB, nseg=16, segstride=4096)))
                x1_at["x1"] = sLDX.n
                sLDX.dma(sy.dma_start(MROW[0:1, 0:512],
                                      bass.AP(mrow_scr, gi * GB, [[4096, 1], [1, 512]])))
                x1_at["mr"] = sLDX.n
                for i in (0, 1):
                    e, s = divmod(i, 32)
                    sLD.dma(sy.dma_start(BUF1[:, 16384 + (i % 2) * 2048:16384 + (i % 2) * 2048 + 2048],
                                         dap(wa_bf[e], s * 2048, 2048)))
                    wea_at[i] = sLD.n
                for q in (0, 1):
                    sLDQ.dma(sy.dma_start(BUF2[:, (q % 2) * 16384:(q % 2) * 16384 + 16384],
                                          dap(wb_bf[0], q * 16384, 16384)))
                    web_at[(0, q)] = sLDQ.n

            @blk.sync
            def _(sy, gi=gi):
                sLDX.dma(sy.dma_start(FSCR[:, 3072:3584],
                                      bass.AP(mrow_scr, gi * GB, [[0, 128], [1, 512]])))
                mb_at[0] = sLDX.n

            for i in range(64):
                e, ft = divmod(i, 32)
                par = i % 2
                pcol = (i % 2) * 512

                @blk.tensor
                def _(te, i=i, e=e, ft=ft, par=par, pcol=pcol):
                    te.wait_ge(sLD.h, wea_at[i])
                    if i == 0:
                        te.wait_ge(sLDX.h, x1_at["x1"])
                    if i >= 2:
                        te.wait_ge(sACT.h, hev_at[i - 2])
                    ins = None
                    for dc in range(NCH):
                        ins = te.matmul(PS[:, pcol:pcol + 512],
                                        BUF1[:, 16384 + par * 2048 + dc * 128:
                                             16384 + par * 2048 + dc * 128 + 128],
                                        KF[:, dc * 512:(dc + 1) * 512],
                                        start=(dc == 0), stop=(dc == NCH - 1))
                    hmm_at[i] = sPE.inc(ins)

                if i + 2 < 64:
                    @blk.sync
                    def _(sy, i=i):
                        sy.wait_ge(sPE.h, hmm_at[i])
                        j = i + 2
                        e2, s2 = divmod(j, 32)
                        sLD.dma(sy.dma_start(
                            BUF1[:, 16384 + (j % 2) * 2048:16384 + (j % 2) * 2048 + 2048],
                            dap(wa_bf[e2], s2 * 2048, 2048)))
                        wea_at[j] = sLD.n

                @blk.scalar
                def _(sc, i=i, e=e, ft=ft, pcol=pcol):
                    sc.wait_ge(sPE.h, hmm_at[i])
                    if e == 1 and ft < 32:
                        if ft == 0:
                            sc.wait_ge(sPE.h, omm_at[(0, 3)])
                    fn = AF.Gelu if e == 0 else AF.Relu
                    hev_at[i] = sACT.inc(
                        sc.activation(BUF1[:, ft * 512:(ft + 1) * 512], PS[:, pcol:pcol + 512],
                                      fn, bias=basb[e][:, ft:ft + 1], scale=1.0))

                if ft == 31:
                    for q in range(4):
                        if not (e == 0 and q < 2):
                            @blk.sync
                            def _(sy, e=e, q=q):
                                j = e * 4 + q
                                pe2, pq2 = divmod(j - 2, 4)
                                sy.wait_ge(sPE.h, omm_at[(pe2, pq2)])
                                sLDQ.dma(sy.dma_start(
                                    BUF2[:, (q % 2) * 16384:(q % 2) * 16384 + 16384],
                                    dap(wb_bf[e], q * 16384, 16384)))
                                web_at[(e, q)] = sLDQ.n

                        @blk.tensor
                        def _(te, e=e, q=q):
                            te.wait_ge(sLDQ.h, web_at[(e, q)])
                            te.wait_ge(sACT.h, hev_at[e * 32 + 31])
                            pk = (e, q - 1) if q >= 1 else ((e - 1, 3) if e == 1 else None)
                            if pk is not None:
                                eng, val = cons_at[pk]
                                te.wait_ge(eng, val)
                            ins = None
                            for Fc in range(NFT):
                                for dot in range(4):
                                    ins = te.matmul(
                                        PS[:, 1024 + dot * 512:1024 + dot * 512 + 512],
                                        BUF2[:, (q % 2) * 16384 + Fc * 512 + dot * 128:
                                             (q % 2) * 16384 + Fc * 512 + dot * 128 + 128],
                                        BUF1[:, Fc * 512:(Fc + 1) * 512],
                                        start=(Fc == 0), stop=(Fc == NFT - 1))
                            omm_at[(e, q)] = sPE.inc(ins)

                        if e == 0:
                            @blk.scalar
                            def _(sc, q=q):
                                sc.wait_ge(sPE.h, omm_at[(0, q)])
                                ins = None
                                for dot in range(4):
                                    cg = q * 4 + dot
                                    ins = sc.copy(
                                        BUF1[:, 20480 + cg * 512:20480 + cg * 512 + 512],
                                        PS[:, 1024 + dot * 512:1024 + dot * 512 + 512])
                                cons_at[(0, q)] = (sACT.h, sACT.inc(ins))
                        else:
                            for dot in range(4):
                                @blk.vector
                                def _(v, q=q, dot=dot):
                                    cg = q * 4 + dot
                                    if dot == 0:
                                        v.wait_ge(sPE.h, omm_at[(1, q)])
                                        v.wait_ge(sLDX.h, mb_at[0])

                                    sDVE.inc(v.scalar_tensor_tensor(
                                        XR[:, 0:512],
                                        PS[:, 1024 + dot * 512:1024 + dot * 512 + 512],
                                        bbsb[1][:, cg:cg + 1], KF[:, cg * 512:(cg + 1) * 512],
                                        op0=ALU.add, op1=ALU.add))
                                    ins = v.scalar_tensor_tensor(
                                        XR[:, 512:1024],
                                        PS[:, 1024 + dot * 512:1024 + dot * 512 + 512],
                                        bdiff[:, cg:cg + 1],
                                        BUF1[:, 20480 + cg * 512:20480 + cg * 512 + 512],
                                        op0=ALU.add, op1=ALU.subtract)
                                    sel_at[cg] = sDVE.inc(ins)
                                    if dot == 3:
                                        cons_at[(1, q)] = (sDVE.h, sDVE.n)

                                @blk.vector
                                def _(v, q=q, dot=dot):
                                    cg = q * 4 + dot
                                    v.tensor_tensor(XR[:, 512:1024], XR[:, 512:1024],
                                                    FSCR[:, 3072:3584], op=ALU.mult)
                                    z2_at[cg] = sDVE.inc(
                                        v.tensor_tensor(VT[:, cg * 512:(cg + 1) * 512],
                                                        XR[:, 0:512], XR[:, 512:1024],
                                                        op=ALU.subtract))

            for cg in range(NCH):
                @blk.scalar
                def _(sc, cg=cg):
                    sc.wait_ge(sDVE.h, z2_at[cg])
                    if cg >= 2:
                        sc.wait_ge(sPE.h, sq2_at[cg - 2])
                    sq2_at[(cg, "a")] = sACT.inc(
                        sc.activation(BUF5[:, (cg % 2) * 512:(cg % 2) * 512 + 512],
                                      VT[:, cg * 512:(cg + 1) * 512], AF.Square))

                @blk.tensor
                def _(te, cg=cg):
                    te.wait_ge(sACT.h, sq2_at[(cg, "a")])
                    te.matmul(PS[0:1, 3072:3584], onesb[:, 0:1], VT[:, cg * 512:(cg + 1) * 512],
                              start=(cg == 0), stop=(cg == NCH - 1))
                    ins = te.matmul(PS[0:1, 3584:4096], onesb[:, 0:1],
                                    BUF5[:, (cg % 2) * 512:(cg % 2) * 512 + 512],
                                    start=(cg == 0), stop=(cg == NCH - 1))
                    sq2_at[cg] = sPE.inc(ins)

            @blk.scalar
            def _(sc):
                sc.wait_ge(sPE.h, sq2_at[NCH - 1])
                sACT.inc(sc.mul(ROWS[0:1, 0:512], PS[0:1, 3072:3584], 1.0 / D))
                sACT.inc(sc.mul(ROWS[0:1, 512:1024], PS[0:1, 3584:4096], 1.0 / D))

            @blk.gpsimd
            def _(gp):
                gp.wait_ge(sACT.h, sACT.n)
                sPOOL.inc(gp.tensor_tensor(MROW[0:1, 0:512], ROWS[0:1, 0:512],
                                           ROWS[0:1, 0:512], op=ALU.mult))

            @blk.vector
            def _(v):
                v.wait_ge(sPOOL.h, sPOOL.n)
                sDVE.inc(v.scalar_tensor_tensor(ROWS[0:1, 512:1024], ROWS[0:1, 512:1024],
                                                EPS, MROW[0:1, 0:512],
                                                op0=ALU.add, op1=ALU.subtract))

            @blk.scalar
            def _(sc):
                sc.wait_ge(sDVE.h, sDVE.n)
                sACT.inc(sc.activation(ROWS[0:1, 512:1024], ROWS[0:1, 512:1024], AF.Sqrt))

            @blk.vector
            def _(v):
                v.wait_ge(sACT.h, sACT.n)
                sDVE.inc(v.reciprocal(ROWS[0:1, 512:1024], ROWS[0:1, 512:1024]))

            @blk.gpsimd
            def _(gp):
                gp.wait_ge(sDVE.h, sDVE.n)
                sPOOL.inc(gp.tensor_tensor(MROW[0:1, 0:512], ROWS[0:1, 0:512],
                                           ROWS[0:1, 512:1024], op=ALU.mult))

            @blk.vector
            def _(v):
                v.wait_ge(sPOOL.h, sPOOL.n)
                sDVE.inc(v.tensor_scalar_mul(MROW[0:1, 0:512], MROW[0:1, 0:512], -1.0))

            bc2 = {}

            @blk.tensor
            def _(te):
                te.wait_ge(sDVE.h, sDVE.n)
                te.matmul(PS[:, 3072:3584], onesr[0:1, :], ROWS[0:1, 512:1024],
                          start=True, stop=True)
                sPE.inc(te.matmul(PS[:, 3584:4096], onesr[0:1, :], MROW[0:1, 0:512],
                                  start=True, stop=True))
                bc2["mm"] = sPE.n

            @blk.scalar
            def _(sc):
                sc.wait_ge(sPE.h, bc2["mm"])
                sc.copy(FSCR[:, 2048:2560], PS[:, 3072:3584])
                bc2[0] = sACT.inc(sc.copy(FSCR[:, 2560:3072], PS[:, 3584:4096]))

            ap2 = {}
            for cg in range(NCH):
                @blk.vector
                def _(v, cg=cg):
                    if cg == 0:
                        v.wait_ge(sACT.h, bc2[0])
                    if cg >= 2:
                        v.wait_ge(sPOOL.h, ap2[(cg - 2, "p")])
                    tpar = cg % 2
                    ap2[cg] = sDVE.inc(
                        v.tensor_tensor(FSCR[:, 1024 + tpar * 512:1024 + tpar * 512 + 512],
                                        VT[:, cg * 512:(cg + 1) * 512], FSCR[:, 2048:2560],
                                        op=ALU.mult))

                @blk.gpsimd
                def _(gp, cg=cg, gi=gi):
                    gp.wait_ge(sDVE.h, ap2[cg])
                    if cg >= 2:
                        gp.wait_ge(sOUT.h, sOUT.n - 16)
                    tpar = cg % 2
                    ap2[(cg, "p")] = sPOOL.inc(
                        gp.tensor_tensor(FSCR[:, tpar * 512:tpar * 512 + 512],
                                         FSCR[:, 1024 + tpar * 512:1024 + tpar * 512 + 512],
                                         FSCR[:, 2560:3072], op=ALU.add))
                    sOUT.dma(gp.dma_start(dap(out_img, cg * 4096 + gi * GB, GB),
                                          FSCR[:, tpar * 512:tpar * 512 + 512]))

    def scoped(name, f, *a):
        with nc.named_scope(name):
            f(*a)

    scoped("a0a1_b0", phase_a0a1, 0)
    scoped("a2_b0", phase_a2, 0)
    if stop_after != "qkv":
        scoped("a4_b0", phase_a4, 0)
        if stop_after != "attn0":
            scoped("a0a1_b1", phase_a0a1, 1)
            scoped("a2_b1", phase_a2, 1)
            scoped("a4_b1", phase_a4, 1)
            scoped("aprime", phase_aprime)
            if stop_after == "full":
                for gi in range(NG):
                    scoped(f"b{gi}", phase_b, gi)

    with nc.Block() as blk:
        @blk.gpsimd
        def _(gp):
            if dbg:
                sOUT.dma(gp.dma_start(dbg_t["dbg_kf"].ap(), KF[:, 0:NKV * 2048]))
                sOUT.dma(gp.dma_start(dbg_t["dbg_vt"].ap(), VT[:, :]))
                sOUT.dma(gp.dma_start(dbg_t["dbg_q"].ap(), q_scr.ap()))
                if stop_after in ("attn0", "x1", "full"):
                    sOUT.dma(gp.dma_start(dbg_t["dbg_y"].ap(), y_scr.ap()))
                if stop_after in ("x1", "full"):
                    sOUT.dma(gp.dma_start(dbg_t["dbg_x1"].ap(), x1_scr.ap()))
                    sOUT.dma(gp.dma_start(dbg_t["dbg_m"].ap(), mrow_scr.ap()))
            gp.wait_ge(sOUT.h, sOUT.n)
            gp.wait_ge(sX1.h, sX1.n)
            gp.wait_ge(sYW.h, sYW.n)
            gp.wait_ge(sQW.h, sQW.n)
            gp.wait_ge(sCVO.h, sCVO.n)
            gp.wait_ge(sMRW.h, sMRW.n)

    return nc


def kernel(L0_attn_x, L0_attn_Wq, L0_attn_Wk, L0_attn_Wv, L0_attn_Wo,
           L0_ffn_Wg, L0_ffn_W0a, L0_ffn_b0a, L0_ffn_W0b, L0_ffn_b0b,
           L0_ffn_W1a, L0_ffn_b1a, L0_ffn_W1b, L0_ffn_b1b):
    x = np.asarray(L0_attn_x, np.float32)
    common = host_inputs(L0_attn_Wq, L0_attn_Wk, L0_attn_Wv, L0_attn_Wo, L0_ffn_Wg,
                         L0_ffn_W0a, L0_ffn_b0a, L0_ffn_W0b, L0_ffn_b0b,
                         L0_ffn_W1a, L0_ffn_b1a, L0_ffn_W1b, L0_ffn_b1b)
    in_maps = []
    for core in range(8):
        imgs = [_sbimg(np.ascontiguousarray(x[core * 2 + j].T)) for j in range(2)]
        m = dict(common)
        m["x_img"] = np.ascontiguousarray(np.concatenate(imgs, axis=1))
        in_maps.append(m)
    nc = build("full", dbg=False)
    res = run_bass_kernel_spmd(nc, in_maps, core_ids=list(range(8)))
    outs = []
    for r in res.results:
        o = np.asarray(r["out_img"]).reshape(128, 16, 4096).transpose(1, 0, 2).reshape(2048, 4096)
        outs.append(np.ascontiguousarray(o[:, 0:2048].T))
        outs.append(np.ascontiguousarray(o[:, 2048:4096].T))
    return np.stack(outs).astype(np.float32)


# version 03
# speedup vs baseline: 1.4676x; 1.0729x over previous
import contextlib
import math
import numpy as np
import ml_dtypes
import concourse.bass as bass
import concourse.mybir as mybir
from concourse.bass_utils import run_bass_kernel_spmd

F32 = mybir.dt.float32
BF16 = mybir.dt.bfloat16
AF = mybir.ActivationFunctionType
ALU = mybir.AluOpType
AX = mybir.AxisListType

B_LOC, T, D, NCH = 2, 2048, 2048, 16
NH, NKV, HS = 16, 4, 128
WINDOW, SINK = 1024, 4
FF, NFT = 4096, 32
GB = 512
NG = (B_LOC * T) // GB
QSCALE = 1.0 / math.sqrt(HS)
EPS = 1e-5
NEGBIG = -1.0e30


def _sbimg(a):
    K, M = a.shape
    return np.ascontiguousarray(a.reshape(K // 128, 128, M).transpose(1, 0, 2).reshape(128, -1))


def host_inputs(Wq, Wk, Wv, Wo, Wg, W0a, b0a, W0b, b0b, W1a, b1a, W1b, b1b):
    f = np.float32
    out = {}
    out["wq_img"] = _sbimg(np.ascontiguousarray(np.asarray(Wq, f).T))
    out["wk_img"] = _sbimg(np.ascontiguousarray(np.asarray(Wk, f).T))
    out["wv_img"] = _sbimg(np.ascontiguousarray(np.asarray(Wv, f).T))
    out["wo_img"] = _sbimg(np.ascontiguousarray(np.asarray(Wo, f).T))
    for e, W in ((0, W0a), (1, W1a)):
        A = np.ascontiguousarray(np.asarray(W, f).T)
        out[f"w{e}a_img"] = np.ascontiguousarray(
            A.reshape(16, 128, 32, 128).transpose(1, 2, 0, 3).reshape(128, 65536))
    for e, W in ((0, W0b), (1, W1b)):
        Bt = np.ascontiguousarray(np.asarray(W, f).T)
        out[f"w{e}b_img"] = np.ascontiguousarray(
            Bt.reshape(32, 128, 4, 4, 128).transpose(1, 2, 0, 3, 4).reshape(128, 65536))
    out["wg_img"] = _sbimg(np.ascontiguousarray(np.asarray(Wg, f).T))
    out["b0a_img"] = np.ascontiguousarray(np.asarray(b0a, f).reshape(32, 128).T)
    out["b1a_img"] = np.ascontiguousarray(np.asarray(b1a, f).reshape(32, 128).T)
    out["b0b_img"] = np.ascontiguousarray(np.asarray(b0b, f).reshape(16, 128).T)
    out["b1b_img"] = np.ascontiguousarray(np.asarray(b1b, f).reshape(16, 128).T)
    ii = np.arange(128)[:, None]
    jj = np.arange(128)[None, :]
    m_lower = np.where(jj <= ii, 0.0, NEGBIG)
    m_far = np.where(jj > ii, 0.0, NEGBIG)
    m_farsink = np.where((jj > ii) | (jj < SINK), 0.0, NEGBIG)
    out["masks_img"] = np.concatenate([m_lower, m_far, m_farsink], 1).astype(ml_dtypes.bfloat16)
    out["ident_img"] = np.eye(128, dtype=ml_dtypes.bfloat16)
    out["onesb_img"] = np.ones((128, 1), dtype=ml_dtypes.bfloat16)
    out["onesr_img"] = np.ones((1, 128), np.float32)
    return out


def build(stop_after="full", dbg=False):
    es = contextlib.ExitStack()
    nc = bass.Bass("TRN2", target_bir_lowering=False, debug=False)

    def din(name, shape, dt):
        return nc.dram_tensor(name, shape, dt, kind="ExternalInput")

    x_img = din("x_img", [128, B_LOC * NCH * T], F32)
    wq_img = din("wq_img", [128, NCH * D], F32)
    wk_img = din("wk_img", [128, NCH * 512], F32)
    wv_img = din("wv_img", [128, NCH * 512], F32)
    wo_img = din("wo_img", [128, NCH * D], F32)
    wa_img = [din("w0a_img", [128, 65536], F32), din("w1a_img", [128, 65536], F32)]
    wb_img = [din("w0b_img", [128, 65536], F32), din("w1b_img", [128, 65536], F32)]
    wg_img = din("wg_img", [128, 32], F32)
    ba_img = [din("b0a_img", [128, 32], F32), din("b1a_img", [128, 32], F32)]
    bb_img = [din("b0b_img", [128, 16], F32), din("b1b_img", [128, 16], F32)]
    masks_img = din("masks_img", [128, 384], BF16)
    ident_img = din("ident_img", [128, 128], BF16)
    onesb_img = din("onesb_img", [128, 1], BF16)
    onesr_img = din("onesr_img", [1, 128], F32)

    q_scr = nc.dram_tensor("q_scr", [128, NH * 4096], BF16, kind="Internal")
    y_scr = nc.dram_tensor("y_scr", [128, NCH * 4096], BF16, kind="Internal")
    x1_scr = nc.dram_tensor("x1_scr", [128, NCH * 4096], BF16, kind="Internal")
    mrow_scr = nc.dram_tensor("mrow_scr", [1, 4096], F32, kind="Internal")
    wa_bf = [nc.dram_tensor(f"w{e}a_bf", [128, 65536], BF16, kind="Internal") for e in range(2)]
    wb_bf = [nc.dram_tensor(f"w{e}b_bf", [128, 65536], BF16, kind="Internal") for e in range(2)]
    out_img = nc.dram_tensor("out_img", [128, NCH * 4096], F32, kind="ExternalOutput")
    dbg_t = {}
    if dbg:
        dbg_t["dbg_kf"] = nc.dram_tensor("dbg_kf", [128, NKV * T], BF16, kind="ExternalOutput")
        dbg_t["dbg_vt"] = nc.dram_tensor("dbg_vt", [128, 16 * 512], BF16, kind="ExternalOutput")
        dbg_t["dbg_q"] = nc.dram_tensor("dbg_q", [128, NH * 4096], BF16, kind="ExternalOutput")
        dbg_t["dbg_y"] = nc.dram_tensor("dbg_y", [128, NCH * 4096], BF16, kind="ExternalOutput")
        dbg_t["dbg_x1"] = nc.dram_tensor("dbg_x1", [128, NCH * 4096], BF16, kind="ExternalOutput")
        dbg_t["dbg_m"] = nc.dram_tensor("dbg_m", [1, 4096], F32, kind="ExternalOutput")

    def dap(t, col0, cols, nseg=1, segstride=0):
        W = t.shape[1]
        if nseg == 1:
            return bass.AP(t, col0, [[W, 128], [1, cols]])
        return bass.AP(t, col0, [[W, 128], [segstride, nseg], [1, cols]])

    BUF1 = nc.alloc_sbuf_tensor("BUF1", [128, 32768], BF16)
    BUF2 = nc.alloc_sbuf_tensor("BUF2", [128, 32768], BF16)
    KF = nc.alloc_sbuf_tensor("KF", [128, 8192], BF16)
    VT = nc.alloc_sbuf_tensor("VT", [128, 8192], BF16)
    BUF5 = nc.alloc_sbuf_tensor("BUF5", [128, 7680], BF16)
    FSCR = nc.alloc_sbuf_tensor("FSCR", [128, 4096], F32)
    XR = nc.alloc_sbuf_tensor("XR", [128, 2048], F32)
    ROWS = nc.alloc_sbuf_tensor("ROWS", [1, 1024], F32)
    MROW = nc.alloc_sbuf_tensor("MROW", [1, 512], F32)
    masks = nc.alloc_sbuf_tensor("masks", [128, 384], BF16)
    ident = nc.alloc_sbuf_tensor("ident", [128, 128], BF16)
    onesb = nc.alloc_sbuf_tensor("onesb", [128, 1], BF16)
    onesr = nc.alloc_sbuf_tensor("onesr", [1, 128], F32)
    wgsb = nc.alloc_sbuf_tensor("wgsb", [128, 32], F32)
    wgd = nc.alloc_sbuf_tensor("wgd", [128, 16], F32)
    swgd = nc.alloc_sbuf_tensor("swgd", [1, 1], F32)
    basb = [nc.alloc_sbuf_tensor(f"b{e}a_sb", [128, 32], F32) for e in range(2)]
    bbsb = [nc.alloc_sbuf_tensor(f"b{e}b_sb", [128, 16], F32) for e in range(2)]
    bdiff = nc.alloc_sbuf_tensor("bdiff", [128, 16], F32)
    nmx = nc.alloc_sbuf_tensor("nmx", [128, 2], F32)
    rsm = nc.alloc_sbuf_tensor("rsm", [128, 2], F32)
    rcp = nc.alloc_sbuf_tensor("rcp", [128, 2], F32)
    diagb = nc.alloc_sbuf_tensor("diagb", [128, 256], BF16)
    fenb = nc.alloc_sbuf_tensor("fenb", [128, 32], BF16)

    PS = nc.alloc_psum_tensor("PS", [128, 4096], F32)

    PP0, PP1, YQ0, YQ1, CBBF_O = 0, 1280, 2560, 4608, 6656

    class Sem:
        def __init__(self, name):
            self.h = es.enter_context(nc.semaphore(name))
            self.n = 0

        def inc(self, instr, k=1):
            instr.then_inc(self.h, k)
            self.n += k
            return self.n

        def dma(self, instr):
            return self.inc(instr, 16)

    sPE, sACT, sDVE, sPOOL = Sem("sPE"), Sem("sACT"), Sem("sDVE"), Sem("sPOOL")

    def pool_fence(gp):
        gp.tensor_copy(fenb[:, 0:16], fenb[:, 16:32])
        gp.tensor_copy(fenb[:, 16:32], fenb[:, 0:16])
        gp.tensor_copy(fenb[:, 0:16], fenb[:, 16:32])
        return sPOOL.inc(gp.tensor_copy(fenb[:, 16:32], fenb[:, 0:16]))
    sLD, sLDQ, sLDX = Sem("sLD"), Sem("sLDQ"), Sem("sLDX")
    sCV, sCVO = Sem("sCV"), Sem("sCVO")
    sQW, sYW, sX1, sOUT, sMRW = Sem("sQW"), Sem("sYW"), Sem("sX1"), Sem("sOUT"), Sem("sMRW")

    CVW = 1024
    conv_jobs = []
    for e in range(2):
        for s in range(65536 // CVW):
            conv_jobs.append((wa_img[e], wa_bf[e], s * CVW))
    for e in range(2):
        for s in range(65536 // CVW):
            conv_jobs.append((wb_img[e], wb_bf[e], s * CVW))
    conv_li = [0]
    conv_si = [0]

    def emit_conversion_load(sy, njobs):
        n = 0
        while n < njobs and conv_li[0] < len(conv_jobs):
            i = conv_li[0]
            conv_li[0] += 1
            n += 1
            par = i % 2
            src, dst, col = conv_jobs[i]
            if i >= 2:
                sy.wait_ge(sCVO.h, 16 * (i - 1))
            sy.dma_start(FSCR[:, par * 1024:(par + 1) * 1024], dap(src, col, CVW)).then_inc(sCV.h, 16)
            sCV.n += 16

    def emit_conversion_store(gp, njobs):
        n = 0
        while n < njobs and conv_si[0] < len(conv_jobs):
            i = conv_si[0]
            conv_si[0] += 1
            n += 1
            par = i % 2
            src, dst, col = conv_jobs[i]
            gp.wait_ge(sCV.h, 16 * (i + 1))
            if i >= 1:
                gp.wait_ge(sCVO.h, 16 * i)
            gp.tensor_copy(BUF5[:, CBBF_O:CBBF_O + CVW], FSCR[:, par * 1024:(par + 1) * 1024])
            gp.dma_start(dap(dst, col, CVW), BUF5[:, CBBF_O:CBBF_O + CVW]).then_inc(sCVO.h, 16)
            sCVO.n += 16

    with nc.Block() as blk:
        @blk.sync
        def _(sy):
            for dst, src in ((masks, masks_img), (ident, ident_img), (onesb, onesb_img),
                             (onesr, onesr_img),
                             (wgsb, wg_img), (basb[0], ba_img[0]), (basb[1], ba_img[1]),
                             (bbsb[0], bb_img[0]), (bbsb[1], bb_img[1])):
                sLD.dma(sy.dma_start(dst[:, :], src.ap()))
            sy.wait_ge(sLD.h, sLD.n)

        @blk.vector
        def _(v):
            v.wait_ge(sLD.h, sLD.n)
            v.tensor_tensor(bdiff[:, :], bbsb[1][:, :], bbsb[0][:, :], op=ALU.subtract)
            sDVE.inc(v.tensor_tensor(wgd[:, :],
                                     bass.AP(wgsb, 0, [[32, 128], [2, 16]]),
                                     bass.AP(wgsb, 1, [[32, 128], [2, 16]]), op=ALU.subtract))

        @blk.gpsimd
        def _(gp):
            gp.wait_ge(sDVE.h, sDVE.n)
            sPOOL.inc(gp.tensor_reduce(ROWS[0:1, 0:1], wgd[:, :], axis=AX.XYZWC, op=ALU.add))

        @blk.scalar
        def _(sc):
            sc.wait_ge(sPOOL.h, sPOOL.n)
            sACT.inc(sc.copy(swgd[0:1, 0:1], ROWS[0:1, 0:1]))

    def phase_a0a1(b):
        xbase = b * (NCH * T)
        with nc.Block() as blk:
            strips = ([("x", c) for c in range(NCH)]
                      + [("wk", s) for s in range(4)] + [("wv", s) for s in range(4)])
            cast_at = {}
            pool_base = sPOOL.n

            @blk.sync
            def _(sy):
                for i, (kind, idx) in enumerate(strips):
                    par = i % 2
                    if i >= 2:
                        sy.wait_ge(sPOOL.h, pool_base + i - 1)
                    if kind == "x":
                        src = dap(x_img, xbase + idx * T, T)
                    elif kind == "wk":
                        src = dap(wk_img, idx * 2048, 2048)
                    else:
                        src = dap(wv_img, idx * 2048, 2048)
                    sLD.dma(sy.dma_start(FSCR[:, par * 2048:(par + 1) * 2048], src))

            @blk.gpsimd
            def _(gp):
                n0 = sLD.n - 16 * len(strips)
                for i, (kind, idx) in enumerate(strips):
                    par = i % 2
                    gp.wait_ge(sLD.h, n0 + 16 * (i + 1))
                    if kind == "x":
                        dst = BUF1[:, idx * 2048:(idx + 1) * 2048]
                    elif kind == "wk":
                        dst = BUF2[:, idx * 2048:(idx + 1) * 2048]
                    else:
                        dst = BUF2[:, 8192 + idx * 2048:8192 + (idx + 1) * 2048]
                    cast_at[(kind, idx)] = sPOOL.inc(
                        gp.tensor_copy(dst, FSCR[:, par * 2048:(par + 1) * 2048]))
                cast_at["fence"] = pool_fence(gp)

            kv_jobs = [("k", ko, tb) for ko in range(4) for tb in range(4)] \
                    + [("v", tt, 0) for tt in range(16)]
            mm_at, ev_at = {}, {}
            act_base = sACT.n

            @blk.tensor
            def _(te):
                te.wait_ge(sPOOL.h, cast_at["fence"])
                for j, job in enumerate(kv_jobs):
                    pcol = (j % 2) * 512
                    if j >= 2:
                        te.wait_ge(sACT.h, act_base + j - 1)
                    ins = None
                    for dc in range(NCH):
                        if job[0] == "k":
                            ko, tb = job[1], job[2]
                            lhsT = BUF2[:, dc * 512 + ko * 128:dc * 512 + ko * 128 + 128]
                            rhs = BUF1[:, dc * 2048 + tb * 512:dc * 2048 + tb * 512 + 512]
                        else:
                            tt = job[1]
                            lhsT = BUF1[:, dc * 2048 + tt * 128:dc * 2048 + tt * 128 + 128]
                            rhs = BUF2[:, 8192 + dc * 512:8192 + dc * 512 + 512]
                        ins = te.matmul(PS[:, pcol:pcol + 512], lhsT, rhs,
                                        start=(dc == 0), stop=(dc == NCH - 1))
                    mm_at[j] = sPE.inc(ins)

            @blk.scalar
            def _(sc):
                for j, job in enumerate(kv_jobs):
                    pcol = (j % 2) * 512
                    sc.wait_ge(sPE.h, mm_at[j])
                    if job[0] == "k":
                        ko, tb = job[1], job[2]
                        dst = KF[:, ko * 2048 + tb * 512:ko * 2048 + tb * 512 + 512]
                    else:
                        tt = job[1]
                        dst = VT[:, tt * 512:(tt + 1) * 512]
                    ev_at[j] = sACT.inc(sc.copy(dst, PS[:, pcol:pcol + 512]))

    def phase_a2(b):
        with nc.Block() as blk:
            cast_at = {}
            q_mm = {}
            act_base = sACT.n
            q_half_done = {}
            blk_pool_base = sPOOL.n
            qw_base = sQW.n

            for half in range(2):
                pool_base0 = blk_pool_base

                @blk.sync
                def _(sy, half=half):
                    def cast_ctr(h, c):
                        return pool_base0 + h * (NCH + 1) + c + 1
                    for c in range(NCH):
                        i = half * NCH + c
                        par = i % 4
                        if i >= 4:
                            ph, pc = divmod(i - 4, NCH)
                            sy.wait_ge(sPOOL.h, cast_ctr(ph, pc))
                        sLD.dma(sy.dma_start(FSCR[:, par * 1024:(par + 1) * 1024],
                                             dap(wq_img, c * 2048 + half * 1024, 1024)))

                @blk.gpsimd
                def _(gp, half=half):
                    n0 = sLD.n - 16 * NCH
                    for c in range(NCH):
                        i = half * NCH + c
                        par = i % 4
                        gp.wait_ge(sLD.h, n0 + 16 * (c + 1))
                        if half == 1 and c == 0:
                            gp.wait_ge(sPE.h, q_half_done[0])
                        cast_at[(half, c)] = sPOOL.inc(
                            gp.tensor_copy(BUF2[:, c * 1024:(c + 1) * 1024],
                                           FSCR[:, par * 1024:(par + 1) * 1024]))
                    cast_at[(half, "fence")] = pool_fence(gp)

                @blk.tensor
                def _(te, half=half):
                    te.wait_ge(sPOOL.h, cast_at[(half, "fence")])
                    for o8 in range(8):
                        for tb in range(4):
                            j = half * 32 + o8 * 4 + tb
                            pcol = (j % 2) * 512
                            if j >= 2:
                                te.wait_ge(sACT.h, act_base + j - 1)
                            ins = None
                            for dc in range(NCH):
                                ins = te.matmul(PS[:, pcol:pcol + 512],
                                                BUF2[:, dc * 1024 + o8 * 128:dc * 1024 + o8 * 128 + 128],
                                                BUF1[:, dc * 2048 + tb * 512:dc * 2048 + tb * 512 + 512],
                                                start=(dc == 0), stop=(dc == NCH - 1))
                            q_mm[j] = sPE.inc(ins)
                    if half == 0:
                        q_half_done[0] = sPE.n

                q_ev = {}

                @blk.scalar
                def _(sc, half=half):
                    for o8 in range(8):
                        for tb in range(4):
                            j = half * 32 + o8 * 4 + tb
                            pcol = (j % 2) * 512
                            par4 = j % 4
                            sc.wait_ge(sPE.h, q_mm[j])
                            if j >= 4:
                                sc.wait_ge(sQW.h, qw_base + 16 * (j - 3))
                            dst = BUF2[:, 16384 + par4 * 512:16384 + par4 * 512 + 512]
                            q_ev[j] = sACT.inc(sc.mul(dst, PS[:, pcol:pcol + 512], QSCALE))

                @blk.sync
                def _(sy, half=half, b=b):
                    for o8 in range(8):
                        for tb in range(4):
                            j = half * 32 + o8 * 4 + tb
                            par4 = j % 4
                            sy.wait_ge(sACT.h, q_ev[j])
                            o = half * 8 + o8
                            sQW.dma(sy.dma_start(
                                dap(q_scr, o * 4096 + b * 2048 + tb * 512, 512),
                                BUF2[:, 16384 + par4 * 512:16384 + par4 * 512 + 512]))

    def qt_geom(qt):
        nw = min(qt + 1, 9)
        k0t = qt - nw + 1
        krange = nw * 128
        has_sink = qt >= 9
        kw = krange + (SINK if has_sink else 0)
        return nw, k0t, krange, has_sink, kw

    def phase_a4(b):
        NHEADS = 16 * NH
        with nc.Block() as blk:
            qld_at, scdone = {}, {}
            sc_at, max_at, exp_at, recip_at, diag_at = {}, {}, {}, {}, {}
            trans_at, ptev_at, av_at, yev_at, ydma_at = {}, {}, {}, {}, {}

            @blk.sync
            def _(sy):
                for qt in (0, 1):
                    sLDX.dma(sy.dma_start(BUF2[:, (qt % 2) * 2048:(qt % 2) * 2048 + 2048],
                                          dap(q_scr, b * 2048 + qt * 128, 128, nseg=16, segstride=4096)))
                    qld_at[qt] = sLDX.n

            def geom(g):
                qt, h = divmod(g, NH)
                nw, k0t, krange, has_sink, kw = qt_geom(qt)
                return dict(qt=qt, h=h, kv=h // 4, par=g % 2, nw=nw, k0t=k0t,
                            krange=krange, has_sink=has_sink, kw=kw,
                            ntile=nw + (1 if has_sink else 0))

            for t in range(NHEADS + 2):
                gs = geom(t) if t < NHEADS else None
                g1 = geom(t - 1) if t >= 1 else None
                g2 = geom(t - 2) if t >= 2 else None

                if 1 <= t <= NHEADS:
                    @blk.vector
                    def _(v, t=t, g1=g1):
                        v.wait_ge(sACT.h, exp_at[t - 1])
                        recip_at[t - 1] = sDVE.inc(
                            v.reciprocal(rcp[:, g1["par"]:g1["par"] + 1],
                                         rsm[:, g1["par"]:g1["par"] + 1]))

                    @blk.scalar
                    def _(sc, t=t, g1=g1):
                        sc.wait_ge(sDVE.h, recip_at[t - 1])
                        if t >= 3:
                            sc.wait_ge(sPE.h, trans_at[t - 3])
                        diag_at[t - 1] = sACT.inc(
                            sc.mul(diagb[:, g1["par"] * 128:g1["par"] * 128 + 128],
                                   ident[:, :], rcp[:, g1["par"]:g1["par"] + 1]))

                @blk.tensor
                def _(te, t=t, gs=gs, g1=g1, g2=g2, b=b):
                    if t >= 2:
                        te.wait_ge(sDVE.h, ptev_at[t - 2])
                        if t >= 4:
                            te.wait_ge(sACT.h, yev_at[t - 4])
                        sb2 = 1536 * g2["par"]
                        ppo = PP0 if g2["par"] == 0 else PP1
                        ycol = 3072 if g2["par"] == 0 else 3584
                        ins = None
                        for i in range(g2["nw"]):
                            kt = g2["k0t"] + i
                            ins = te.matmul(PS[:, ycol:ycol + 128],
                                            VT[:, kt * 512 + g2["kv"] * 128:kt * 512 + g2["kv"] * 128 + 128],
                                            BUF5[:, ppo + i * 128:ppo + i * 128 + 128],
                                            start=(i == 0),
                                            stop=(i == g2["nw"] - 1 and not g2["has_sink"]))
                        if g2["has_sink"]:
                            ins = te.matmul(PS[:, ycol:ycol + 128],
                                            VT[0:4, g2["kv"] * 128:g2["kv"] * 128 + 128],
                                            BUF5[0:4, ppo + g2["nw"] * 128:ppo + g2["nw"] * 128 + 128],
                                            start=False, stop=True)
                        av_at[t - 2] = sPE.inc(ins)
                    if t < NHEADS:
                        te.wait_ge(sLDX.h, qld_at[gs["qt"]])
                        if t >= 2:
                            te.wait_ge(sACT.h, exp_at[t - 2])
                            te.wait_ge(sDVE.h, ptev_at[t - 2])
                        sb = 1536 * gs["par"]
                        lhsq = BUF2[:, (gs["qt"] % 2) * 2048 + gs["h"] * 128:
                                    (gs["qt"] % 2) * 2048 + gs["h"] * 128 + 128]
                        col = 0
                        ins = None
                        while col < gs["krange"]:
                            w = min(512, gs["krange"] - col)
                            rhs = KF[:, gs["kv"] * 2048 + gs["k0t"] * 128 + col:
                                     gs["kv"] * 2048 + gs["k0t"] * 128 + col + w]
                            ins = te.matmul(PS[:, sb + col:sb + col + w], lhsq, rhs,
                                            start=True, stop=True)
                            col += w
                        if gs["has_sink"]:
                            ins = te.matmul(PS[:, sb + gs["krange"]:sb + gs["krange"] + SINK],
                                            lhsq, KF[:, gs["kv"] * 2048:gs["kv"] * 2048 + SINK],
                                            start=True, stop=True)
                        sc_at[t] = sPE.inc(ins)
                        if gs["h"] == NH - 1:
                            scdone[gs["qt"]] = sPE.n
                    if 1 <= t <= NHEADS:
                        te.wait_ge(sACT.h, diag_at[t - 1])
                        sb1 = 1536 * g1["par"]
                        ppo1 = PP0 if g1["par"] == 0 else PP1
                        dg = diagb[:, g1["par"] * 128:g1["par"] * 128 + 128]
                        ins = None
                        for i in range(g1["ntile"]):
                            sink_t = g1["has_sink"] and (i == g1["ntile"] - 1)
                            if sink_t:
                                ins = te.matmul(PS[0:4, sb1 + i * 128:sb1 + (i + 1) * 128],
                                                BUF5[:, ppo1 + g1["krange"]:ppo1 + g1["krange"] + SINK],
                                                dg, start=True, stop=True)
                            else:
                                ins = te.matmul(PS[:, sb1 + i * 128:sb1 + (i + 1) * 128],
                                                BUF5[:, ppo1 + i * 128:ppo1 + i * 128 + 128],
                                                dg, start=True, stop=True)
                        trans_at[t - 1] = sPE.inc(ins)

                @blk.vector
                def _(v, t=t, gs=gs, g1=g1):
                    if t < NHEADS:
                        v.wait_ge(sPE.h, sc_at[t])
                        if t >= 2:
                            v.wait_ge(sACT.h, exp_at[t - 2])
                        sb = 1536 * gs["par"]
                        qt = gs["qt"]
                        krange = gs["krange"]
                        v.tensor_tensor(PS[:, sb + krange - 128:sb + krange],
                                        PS[:, sb + krange - 128:sb + krange],
                                        masks[:, 0:128], op=ALU.add)
                        if qt >= 9:
                            v.tensor_tensor(PS[:, sb:sb + 128], PS[:, sb:sb + 128],
                                            masks[:, 128:256], op=ALU.add)
                        elif qt == 8:
                            v.tensor_tensor(PS[:, sb:sb + 128], PS[:, sb:sb + 128],
                                            masks[:, 256:384], op=ALU.add)
                        max_at[t] = sDVE.inc(
                            v.reduce_max(nmx[:, gs["par"]:gs["par"] + 1],
                                         PS[:, sb + krange - 128:sb + krange],
                                         axis=AX.X, negate=True))
                    if 1 <= t <= NHEADS:
                        v.wait_ge(sPE.h, trans_at[t - 1])
                        sb1 = 1536 * g1["par"]
                        ppo1 = PP0 if g1["par"] == 0 else PP1
                        w = g1["ntile"] * 128
                        ptev_at[t - 1] = sDVE.inc(
                            v.tensor_copy(BUF5[:, ppo1:ppo1 + w], PS[:, sb1:sb1 + w]))

                @blk.scalar
                def _(sc, t=t, gs=gs, g2=g2):
                    if t < NHEADS:
                        sc.wait_ge(sDVE.h, max_at[t])
                        if t >= 2:
                            sc.wait_ge(sPE.h, av_at[t - 2])
                        sb = 1536 * gs["par"]
                        ppo = PP0 if gs["par"] == 0 else PP1
                        exp_at[t] = sACT.inc(
                            sc.activation(BUF5[:, ppo:ppo + gs["kw"]], PS[:, sb:sb + gs["kw"]],
                                          AF.Exp, bias=nmx[:, gs["par"]:gs["par"] + 1],
                                          scale=1.0, accum_out=rsm[:, gs["par"]:gs["par"] + 1]))
                    if t >= 2:
                        sc.wait_ge(sPE.h, av_at[t - 2])
                        qt2, h2 = divmod(t - 2, NH)
                        if h2 == 0 and qt2 >= 2:
                            sc.wait_ge(sYW.h, ydma_at[qt2 - 2])
                        ycol = 3072 if g2["par"] == 0 else 3584
                        yo = (YQ0 if qt2 % 2 == 0 else YQ1) + h2 * 128
                        yev_at[t - 2] = sACT.inc(sc.copy(BUF5[:, yo:yo + 128],
                                                         PS[:, ycol:ycol + 128]))

                if t < NHEADS and t % NH == NH - 1:
                    @blk.sync
                    def _(sy, t=t, b=b):
                        qt = t // NH
                        if qt + 2 < 16:
                            sy.wait_ge(sPE.h, scdone[qt])
                            nqt = qt + 2
                            sLDX.dma(sy.dma_start(
                                BUF2[:, (nqt % 2) * 2048:(nqt % 2) * 2048 + 2048],
                                dap(q_scr, b * 2048 + nqt * 128, 128, nseg=16, segstride=4096)))
                            qld_at[nqt] = sLDX.n
                        emit_conversion_load(sy, 8)

                if t >= 2 and (t - 2) % NH == NH - 1:
                    @blk.gpsimd
                    def _(gp, t=t, b=b):
                        qt = (t - 2) // NH
                        gp.wait_ge(sACT.h, yev_at[t - 2])
                        yo = YQ0 if qt % 2 == 0 else YQ1
                        sYW.dma(gp.dma_start(
                            dap(y_scr, b * 2048 + qt * 128, 128, nseg=16, segstride=4096),
                            BUF5[:, yo:yo + 2048]))
                        ydma_at[qt] = sYW.n
                        emit_conversion_store(gp, 8)

            @blk.sync
            def _(sy):
                emit_conversion_load(sy, conv_si[0] - conv_li[0])

    def phase_aprime():
        ntb = (B_LOC * T) // 256
        with nc.Block() as blk:
            cast_at = {}
            pool_base = sPOOL.n

            @blk.sync
            def _(sy):
                for c in range(NCH):
                    par = c % 2
                    if c >= 2:
                        sy.wait_ge(sPOOL.h, pool_base + c - 1)
                    sLD.dma(sy.dma_start(FSCR[:, par * 2048:(par + 1) * 2048],
                                         dap(wo_img, c * 2048, 2048)))

            @blk.gpsimd
            def _(gp):
                n0 = sLD.n - 16 * NCH
                for c in range(NCH):
                    par = c % 2
                    gp.wait_ge(sLD.h, n0 + 16 * (c + 1))
                    cast_at[c] = sPOOL.inc(gp.tensor_copy(BUF1[:, c * 2048:(c + 1) * 2048],
                                                          FSCR[:, par * 2048:(par + 1) * 2048]))
                cast_at["fence"] = pool_fence(gp)
                gp.wait_ge(sYW.h, sYW.n)

            ytb_at, xr_at, wo_at, ev_at = {}, {}, {}, {}
            zb_at, sq_at, stats_at, rowmath_done, apm_at = {}, {}, {}, {}, {}
            bc_done = {}

            @blk.sync
            def _(sy):
                for tb in (0, 1):
                    sLDX.dma(sy.dma_start(BUF2[:, (tb % 2) * 4096:(tb % 2) * 4096 + 4096],
                                          dap(y_scr, tb * 256, 256, nseg=16, segstride=4096)))
                    ytb_at[tb] = sLDX.n

            def emit_stats(te, tb, dot):
                te.wait_ge(sACT.h, sq_at[(tb, dot)])
                if dot == 0 and tb >= 1:
                    te.wait_ge(sACT.h, rowmath_done[tb - 1])
                te.matmul(PS[0:1, 1024:1280], onesb[:, 0:1],
                          BUF5[:, (dot % 2) * 256:(dot % 2) * 256 + 256],
                          start=(dot == 0), stop=(dot == NCH - 1))
                te.matmul(PS[0:1, 1536:1792], onesb[:, 0:1],
                          BUF5[:, 512 + (dot % 2) * 256:512 + (dot % 2) * 256 + 256],
                          start=(dot == 0), stop=(dot == NCH - 1))
                ins = te.matmul(PS[0:1, 2048:2304], wgd[:, dot:dot + 1],
                                FSCR[:, dot * 256:(dot + 1) * 256],
                                start=(dot == 0), stop=(dot == NCH - 1))
                stats_at[(tb, dot)] = sPE.inc(ins)

            for tb in range(ntb):
                par = tb % 2
                xoff = (tb // 8) * (NCH * T) + (tb % 8) * 256

                for dot in range(NCH):
                    slot = dot % 4
                    gidx = tb * NCH + dot

                    @blk.sync
                    def _(sy, tb=tb, dot=dot, slot=slot, gidx=gidx, xoff=xoff):
                        if gidx >= 4:
                            pk = (tb, dot - 4) if dot >= 4 else (tb - 1, dot + 12)
                            sy.wait_ge(sDVE.h, ev_at[pk])
                        sLD.dma(sy.dma_start(XR[:, slot * 256:(slot + 1) * 256],
                                             dap(x_img, xoff + dot * T, 256)))
                        xr_at[(tb, dot)] = sLD.n

                    @blk.tensor
                    def _(te, tb=tb, dot=dot, par=par):
                        if dot == 0:
                            te.wait_ge(sPOOL.h, cast_at["fence"])
                            te.wait_ge(sLDX.h, ytb_at[tb])
                        pcol = (dot % 2) * 512
                        pk = (tb, dot - 2) if dot >= 2 else ((tb - 1, dot + 14) if tb >= 1 else None)
                        if pk is not None:
                            te.wait_ge(sDVE.h, ev_at[pk])
                        ins = None
                        for dc in range(NCH):
                            ins = te.matmul(PS[:, pcol:pcol + 256],
                                            BUF1[:, dc * 2048 + dot * 128:dc * 2048 + dot * 128 + 128],
                                            BUF2[:, par * 4096 + dc * 256:par * 4096 + dc * 256 + 256],
                                            start=(dc == 0), stop=(dc == NCH - 1))
                        wo_at[(tb, dot)] = sPE.inc(ins)
                        if dot >= 1:
                            emit_stats(te, tb, dot - 1)

                    @blk.vector
                    def _(v, tb=tb, dot=dot, slot=slot):
                        v.wait_ge(sPE.h, wo_at[(tb, dot)])
                        v.wait_ge(sLD.h, xr_at[(tb, dot)])
                        if tb >= 1 and dot == 0:
                            v.wait_ge(sDVE.h, apm_at[(tb - 1, NCH - 1)])
                            v.wait_ge(sACT.h, sq_at[(tb - 1, NCH - 1)])
                            v.wait_ge(sPE.h, stats_at[(tb - 1, NCH - 1)])
                        pcol = (dot % 2) * 512
                        ev_at[(tb, dot)] = sDVE.inc(
                            v.tensor_add(FSCR[:, dot * 256:(dot + 1) * 256],
                                         PS[:, pcol:pcol + 256], XR[:, slot * 256:(slot + 1) * 256]))

                    @blk.scalar
                    def _(sc, tb=tb, dot=dot):
                        sc.wait_ge(sDVE.h, ev_at[(tb, dot)])
                        if (tb, dot - 2) in stats_at:
                            sc.wait_ge(sPE.h, stats_at[(tb, dot - 2)])
                        elif tb >= 1 and dot < 2:
                            sc.wait_ge(sPE.h, stats_at[(tb - 1, dot + 14)])
                        sc.copy(BUF5[:, (dot % 2) * 256:(dot % 2) * 256 + 256],
                                FSCR[:, dot * 256:(dot + 1) * 256])
                        sq_at[(tb, dot)] = sACT.inc(
                            sc.activation(BUF5[:, 512 + (dot % 2) * 256:512 + (dot % 2) * 256 + 256],
                                          BUF5[:, (dot % 2) * 256:(dot % 2) * 256 + 256], AF.Square))

                @blk.tensor
                def _(te, tb=tb):
                    emit_stats(te, tb, NCH - 1)

                @blk.scalar
                def _(sc, tb=tb):
                    sc.wait_ge(sPE.h, stats_at[(tb, NCH - 1)])
                    if tb >= 1:
                        sc.wait_ge(sACT.h, bc_done[tb - 1])
                    sACT.inc(sc.mul(ROWS[0:1, 0:256], PS[0:1, 1024:1280], 1.0 / D))
                    sACT.inc(sc.mul(ROWS[0:1, 256:512], PS[0:1, 1536:1792], 1.0 / D))
                    sACT.inc(sc.copy(MROW[0:1, 0:256], PS[0:1, 2048:2304]))
                    rowmath_done[tb] = sACT.n

                @blk.gpsimd
                def _(gp, tb=tb):
                    gp.wait_ge(sACT.h, sACT.n)
                    sPOOL.inc(gp.tensor_tensor(ROWS[0:1, 512:768], ROWS[0:1, 0:256],
                                               ROWS[0:1, 0:256], op=ALU.mult))

                @blk.vector
                def _(v, tb=tb):
                    v.wait_ge(sPOOL.h, sPOOL.n)
                    sDVE.inc(v.scalar_tensor_tensor(ROWS[0:1, 768:1024], ROWS[0:1, 256:512],
                                                    EPS, ROWS[0:1, 512:768],
                                                    op0=ALU.add, op1=ALU.subtract))

                @blk.scalar
                def _(sc, tb=tb):
                    sc.wait_ge(sDVE.h, sDVE.n)
                    sACT.inc(sc.activation(ROWS[0:1, 512:768], ROWS[0:1, 768:1024], AF.Sqrt))
                    sACT.inc(sc.mul(ROWS[0:1, 256:512], ROWS[0:1, 0:256], swgd[0:1, 0:1]))

                @blk.vector
                def _(v, tb=tb):
                    v.wait_ge(sACT.h, sACT.n)
                    v.wait_ge(sMRW.h, sMRW.n)
                    sDVE.inc(v.reciprocal(ROWS[0:1, 768:1024], ROWS[0:1, 512:768]))
                    sDVE.inc(v.tensor_tensor(MROW[0:1, 256:512], MROW[0:1, 0:256],
                                             ROWS[0:1, 256:512], op=ALU.is_ge))

                @blk.gpsimd
                def _(gp, tb=tb):
                    gp.wait_ge(sDVE.h, sDVE.n)
                    sPOOL.inc(gp.tensor_tensor(ROWS[0:1, 512:768], ROWS[0:1, 0:256],
                                               ROWS[0:1, 768:1024], op=ALU.mult))

                @blk.vector
                def _(v, tb=tb):
                    v.wait_ge(sPOOL.h, sPOOL.n)
                    sDVE.inc(v.tensor_scalar_mul(ROWS[0:1, 512:768], ROWS[0:1, 512:768], -1.0))

                @blk.sync
                def _(sy, tb=tb):
                    sy.wait_ge(sDVE.h, sDVE.n)
                    sMRW.dma(sy.dma_start(bass.AP(mrow_scr, tb * 256, [[4096, 1], [1, 256]]),
                                          MROW[0:1, 256:512]))

                @blk.tensor
                def _(te, tb=tb):
                    te.wait_ge(sDVE.h, sDVE.n)
                    if tb >= 1:
                        te.wait_ge(sACT.h, bc_done[tb - 1])
                    te.matmul(PS[:, 2560:2816], onesr[0:1, :], ROWS[0:1, 768:1024],
                              start=True, stop=True)
                    sPE.inc(te.matmul(PS[:, 2816:3072], onesr[0:1, :], ROWS[0:1, 512:768],
                                      start=True, stop=True))
                    bc_done[(tb, "mm")] = sPE.n

                @blk.scalar
                def _(sc, tb=tb):
                    sc.wait_ge(sPE.h, bc_done[(tb, "mm")])
                    sc.copy(XR[:, 1024:1280], PS[:, 2560:2816])
                    bc_done[tb] = sACT.inc(sc.copy(XR[:, 1280:1536], PS[:, 2816:3072]))

                x1o = 16384 + par * 4096
                for dot in range(NCH):
                    @blk.vector
                    def _(v, tb=tb, dot=dot):
                        v.wait_ge(sACT.h, bc_done[tb])
                        if dot >= 2:
                            v.wait_ge(sPOOL.h, apm_at[(tb, dot - 2, "p")])
                        tpar = dot % 2
                        apm_at[(tb, dot)] = sDVE.inc(
                            v.tensor_tensor(XR[:, 1536 + tpar * 256:1536 + tpar * 256 + 256],
                                            FSCR[:, dot * 256:(dot + 1) * 256],
                                            XR[:, 1024:1280], op=ALU.mult))

                    @blk.gpsimd
                    def _(gp, tb=tb, dot=dot, x1o=x1o):
                        gp.wait_ge(sDVE.h, apm_at[(tb, dot)])
                        if dot == 0 and tb >= 2:
                            gp.wait_ge(sX1.h, sX1.n - 16)
                        tpar = dot % 2
                        apm_at[(tb, dot, "p")] = sPOOL.inc(
                            gp.tensor_tensor(BUF2[:, x1o + dot * 256:x1o + dot * 256 + 256],
                                             XR[:, 1536 + tpar * 256:1536 + tpar * 256 + 256],
                                             XR[:, 1280:1536], op=ALU.add))

                @blk.gpsimd
                def _(gp, tb=tb, x1o=x1o):
                    sX1.dma(gp.dma_start(dap(x1_scr, tb * 256, 256, nseg=16, segstride=4096),
                                         BUF2[:, x1o:x1o + 4096]))

                if tb + 2 < ntb:
                    @blk.sync
                    def _(sy, tb=tb):
                        sy.wait_ge(sPE.h, wo_at[(tb, NCH - 1)])
                        ntb2 = tb + 2
                        sLDX.dma(sy.dma_start(
                            BUF2[:, (ntb2 % 2) * 4096:(ntb2 % 2) * 4096 + 4096],
                            dap(y_scr, ntb2 * 256, 256, nseg=16, segstride=4096)))
                        ytb_at[ntb2] = sLDX.n

            @blk.gpsimd
            def _(gp):
                emit_conversion_store(gp, len(conv_jobs))
                gp.wait_ge(sX1.h, sX1.n)
                gp.wait_ge(sMRW.h, sMRW.n)
                gp.wait_ge(sCVO.h, sCVO.n)

            @blk.sync
            def _(sy):
                emit_conversion_load(sy, len(conv_jobs))

    def phase_b(gi):
        with nc.Block() as blk:
            x1_at, mb_at, wea_at, web_at = {}, {}, {}, {}

            def wea_slot(i):
                s = i % 4
                return 16384 + s * 2048 if s < 2 else 28672 + (s - 2) * 2048
            hmm_at, hev_at, omm_at, cons_at = {}, {}, {}, {}
            sel_at, z2_at, sq2_at = {}, {}, {}

            @blk.sync
            def _(sy):
                sLDX.dma(sy.dma_start(KF[:, 0:8192],
                                      dap(x1_scr, gi * GB, GB, nseg=16, segstride=4096)))
                x1_at["x1"] = sLDX.n
                sLDX.dma(sy.dma_start(MROW[0:1, 0:512],
                                      bass.AP(mrow_scr, gi * GB, [[4096, 1], [1, 512]])))
                x1_at["mr"] = sLDX.n
                for i in (0, 1, 2, 3):
                    e, s = divmod(i, 32)
                    wo_ = wea_slot(i)
                    sLD.dma(sy.dma_start(BUF1[:, wo_:wo_ + 2048], dap(wa_bf[e], s * 2048, 2048)))
                    wea_at[i] = sLD.n
                for q in (0, 1):
                    sLDQ.dma(sy.dma_start(BUF2[:, (q % 2) * 16384:(q % 2) * 16384 + 16384],
                                          dap(wb_bf[0], q * 16384, 16384)))
                    web_at[(0, q)] = sLDQ.n

            @blk.sync
            def _(sy, gi=gi):
                sLDX.dma(sy.dma_start(FSCR[:, 3072:3584],
                                      bass.AP(mrow_scr, gi * GB, [[0, 128], [1, 512]])))
                mb_at[0] = sLDX.n

            for i in range(64):
                e, ft = divmod(i, 32)
                par = i % 2
                pcol = (i % 2) * 512

                @blk.tensor
                def _(te, i=i, e=e, ft=ft, pcol=pcol):
                    te.wait_ge(sLD.h, wea_at[i])
                    if i == 0:
                        te.wait_ge(sLDX.h, x1_at["x1"])
                    if i >= 2:
                        te.wait_ge(sACT.h, hev_at[i - 2])
                    wo_ = wea_slot(i)
                    ins = None
                    for dc in range(NCH):
                        ins = te.matmul(PS[:, pcol:pcol + 512],
                                        BUF1[:, wo_ + dc * 128:wo_ + dc * 128 + 128],
                                        KF[:, dc * 512:(dc + 1) * 512],
                                        start=(dc == 0), stop=(dc == NCH - 1))
                    hmm_at[i] = sPE.inc(ins)

                if i + 4 < 64:
                    @blk.sync
                    def _(sy, i=i):
                        sy.wait_ge(sPE.h, hmm_at[i])
                        j = i + 4
                        e2, s2 = divmod(j, 32)
                        wo_ = wea_slot(j)
                        sLD.dma(sy.dma_start(BUF1[:, wo_:wo_ + 2048],
                                             dap(wa_bf[e2], s2 * 2048, 2048)))
                        wea_at[j] = sLD.n

                @blk.scalar
                def _(sc, i=i, e=e, ft=ft, pcol=pcol):
                    sc.wait_ge(sPE.h, hmm_at[i])
                    if e == 1 and ft < 32:
                        if ft == 0:
                            sc.wait_ge(sPE.h, omm_at[(0, 3)])
                    fn = AF.Gelu if e == 0 else AF.Relu
                    hev_at[i] = sACT.inc(
                        sc.activation(BUF1[:, ft * 512:(ft + 1) * 512], PS[:, pcol:pcol + 512],
                                      fn, bias=basb[e][:, ft:ft + 1], scale=1.0))

                if ft == 31:
                    for q in range(4):
                        if not (e == 0 and q < 2):
                            @blk.sync
                            def _(sy, e=e, q=q):
                                j = e * 4 + q
                                pe2, pq2 = divmod(j - 2, 4)
                                sy.wait_ge(sPE.h, omm_at[(pe2, pq2)])
                                sLDQ.dma(sy.dma_start(
                                    BUF2[:, (q % 2) * 16384:(q % 2) * 16384 + 16384],
                                    dap(wb_bf[e], q * 16384, 16384)))
                                web_at[(e, q)] = sLDQ.n

                        for p in range(2):
                            gpi = q * 2 + p
                            bk = 1024 + (gpi % 2) * 1024

                            @blk.tensor
                            def _(te, e=e, q=q, p=p, gpi=gpi, bk=bk):
                                if p == 0:
                                    te.wait_ge(sLDQ.h, web_at[(e, q)])
                                    te.wait_ge(sACT.h, hev_at[e * 32 + 31])
                                pk = (e, gpi - 2) if gpi >= 2 else ((e - 1, gpi + 6) if e == 1 else None)
                                if pk is not None:
                                    eng, val = cons_at[pk]
                                    te.wait_ge(eng, val)
                                ins = None
                                for Fc in range(NFT):
                                    for d2 in range(2):
                                        dot = p * 2 + d2
                                        ins = te.matmul(
                                            PS[:, bk + d2 * 512:bk + d2 * 512 + 512],
                                            BUF2[:, (q % 2) * 16384 + Fc * 512 + dot * 128:
                                                 (q % 2) * 16384 + Fc * 512 + dot * 128 + 128],
                                            BUF1[:, Fc * 512:(Fc + 1) * 512],
                                            start=(Fc == 0), stop=(Fc == NFT - 1))
                                omm_at[(e, gpi, "p")] = sPE.inc(ins)
                                if p == 1:
                                    omm_at[(e, q)] = sPE.n

                            if e == 0:
                                @blk.scalar
                                def _(sc, q=q, p=p, gpi=gpi, bk=bk):
                                    sc.wait_ge(sPE.h, omm_at[(0, gpi, "p")])
                                    ins = None
                                    for d2 in range(2):
                                        cg = q * 4 + p * 2 + d2
                                        ins = sc.copy(
                                            BUF1[:, 20480 + cg * 512:20480 + cg * 512 + 512],
                                            PS[:, bk + d2 * 512:bk + d2 * 512 + 512])
                                    cons_at[(0, gpi)] = (sACT.h, sACT.inc(ins))
                            else:
                                @blk.vector
                                def _(v, q=q, p=p, gpi=gpi, bk=bk):
                                    v.wait_ge(sPE.h, omm_at[(1, gpi, "p")])
                                    if gpi == 0:
                                        v.wait_ge(sLDX.h, mb_at[0])
                                    ins = None
                                    for d2 in range(2):
                                        cg = q * 4 + p * 2 + d2
                                        sDVE.inc(v.scalar_tensor_tensor(
                                            XR[:, 0:512],
                                            PS[:, bk + d2 * 512:bk + d2 * 512 + 512],
                                            bbsb[1][:, cg:cg + 1], KF[:, cg * 512:(cg + 1) * 512],
                                            op0=ALU.add, op1=ALU.add))
                                        sDVE.inc(v.scalar_tensor_tensor(
                                            XR[:, 512:1024],
                                            PS[:, bk + d2 * 512:bk + d2 * 512 + 512],
                                            bdiff[:, cg:cg + 1],
                                            BUF1[:, 20480 + cg * 512:20480 + cg * 512 + 512],
                                            op0=ALU.add, op1=ALU.subtract))
                                        v.tensor_tensor(XR[:, 512:1024], XR[:, 512:1024],
                                                        FSCR[:, 3072:3584], op=ALU.mult)
                                        ins = v.tensor_tensor(VT[:, cg * 512:(cg + 1) * 512],
                                                              XR[:, 0:512], XR[:, 512:1024],
                                                              op=ALU.subtract)
                                        z2_at[cg] = sDVE.inc(ins)
                                    cons_at[(1, gpi)] = (sDVE.h, sDVE.n)

            for cg in range(NCH):
                @blk.scalar
                def _(sc, cg=cg):
                    sc.wait_ge(sDVE.h, z2_at[cg])
                    if cg >= 2:
                        sc.wait_ge(sPE.h, sq2_at[cg - 2])
                    sq2_at[(cg, "a")] = sACT.inc(
                        sc.activation(BUF5[:, (cg % 2) * 512:(cg % 2) * 512 + 512],
                                      VT[:, cg * 512:(cg + 1) * 512], AF.Square))

                @blk.tensor
                def _(te, cg=cg):
                    te.wait_ge(sACT.h, sq2_at[(cg, "a")])
                    te.matmul(PS[0:1, 3072:3584], onesb[:, 0:1], VT[:, cg * 512:(cg + 1) * 512],
                              start=(cg == 0), stop=(cg == NCH - 1))
                    ins = te.matmul(PS[0:1, 3584:4096], onesb[:, 0:1],
                                    BUF5[:, (cg % 2) * 512:(cg % 2) * 512 + 512],
                                    start=(cg == 0), stop=(cg == NCH - 1))
                    sq2_at[cg] = sPE.inc(ins)

            @blk.scalar
            def _(sc):
                sc.wait_ge(sPE.h, sq2_at[NCH - 1])
                sACT.inc(sc.mul(ROWS[0:1, 0:512], PS[0:1, 3072:3584], 1.0 / D))
                sACT.inc(sc.mul(ROWS[0:1, 512:1024], PS[0:1, 3584:4096], 1.0 / D))

            @blk.gpsimd
            def _(gp):
                gp.wait_ge(sACT.h, sACT.n)
                sPOOL.inc(gp.tensor_tensor(MROW[0:1, 0:512], ROWS[0:1, 0:512],
                                           ROWS[0:1, 0:512], op=ALU.mult))

            @blk.vector
            def _(v):
                v.wait_ge(sPOOL.h, sPOOL.n)
                sDVE.inc(v.scalar_tensor_tensor(ROWS[0:1, 512:1024], ROWS[0:1, 512:1024],
                                                EPS, MROW[0:1, 0:512],
                                                op0=ALU.add, op1=ALU.subtract))

            @blk.scalar
            def _(sc):
                sc.wait_ge(sDVE.h, sDVE.n)
                sACT.inc(sc.activation(ROWS[0:1, 512:1024], ROWS[0:1, 512:1024], AF.Sqrt))

            @blk.vector
            def _(v):
                v.wait_ge(sACT.h, sACT.n)
                sDVE.inc(v.reciprocal(ROWS[0:1, 512:1024], ROWS[0:1, 512:1024]))

            @blk.gpsimd
            def _(gp):
                gp.wait_ge(sDVE.h, sDVE.n)
                sPOOL.inc(gp.tensor_tensor(MROW[0:1, 0:512], ROWS[0:1, 0:512],
                                           ROWS[0:1, 512:1024], op=ALU.mult))

            @blk.vector
            def _(v):
                v.wait_ge(sPOOL.h, sPOOL.n)
                sDVE.inc(v.tensor_scalar_mul(MROW[0:1, 0:512], MROW[0:1, 0:512], -1.0))

            bc2 = {}

            @blk.tensor
            def _(te):
                te.wait_ge(sDVE.h, sDVE.n)
                te.matmul(PS[:, 3072:3584], onesr[0:1, :], ROWS[0:1, 512:1024],
                          start=True, stop=True)
                sPE.inc(te.matmul(PS[:, 3584:4096], onesr[0:1, :], MROW[0:1, 0:512],
                                  start=True, stop=True))
                bc2["mm"] = sPE.n

            @blk.scalar
            def _(sc):
                sc.wait_ge(sPE.h, bc2["mm"])
                sc.copy(FSCR[:, 2048:2560], PS[:, 3072:3584])
                bc2[0] = sACT.inc(sc.copy(FSCR[:, 2560:3072], PS[:, 3584:4096]))

            ap2 = {}
            for cg in range(NCH):
                @blk.vector
                def _(v, cg=cg):
                    if cg == 0:
                        v.wait_ge(sACT.h, bc2[0])
                    if cg >= 2:
                        v.wait_ge(sPOOL.h, ap2[(cg - 2, "p")])
                    tpar = cg % 2
                    ap2[cg] = sDVE.inc(
                        v.tensor_tensor(FSCR[:, 1024 + tpar * 512:1024 + tpar * 512 + 512],
                                        VT[:, cg * 512:(cg + 1) * 512], FSCR[:, 2048:2560],
                                        op=ALU.mult))

                @blk.gpsimd
                def _(gp, cg=cg, gi=gi):
                    gp.wait_ge(sDVE.h, ap2[cg])
                    if cg >= 2:
                        gp.wait_ge(sOUT.h, sOUT.n - 16)
                    tpar = cg % 2
                    ap2[(cg, "p")] = sPOOL.inc(
                        gp.tensor_tensor(FSCR[:, tpar * 512:tpar * 512 + 512],
                                         FSCR[:, 1024 + tpar * 512:1024 + tpar * 512 + 512],
                                         FSCR[:, 2560:3072], op=ALU.add))
                    sOUT.dma(gp.dma_start(dap(out_img, cg * 4096 + gi * GB, GB),
                                          FSCR[:, tpar * 512:tpar * 512 + 512]))

    def scoped(name, f, *a):
        with nc.named_scope(name):
            f(*a)

    scoped("a0a1_b0", phase_a0a1, 0)
    scoped("a2_b0", phase_a2, 0)
    if stop_after != "qkv":
        scoped("a4_b0", phase_a4, 0)
        if stop_after != "attn0":
            scoped("a0a1_b1", phase_a0a1, 1)
            scoped("a2_b1", phase_a2, 1)
            scoped("a4_b1", phase_a4, 1)
            scoped("aprime", phase_aprime)
            if stop_after == "full":
                for gi in range(NG):
                    scoped(f"b{gi}", phase_b, gi)

    with nc.Block() as blk:
        @blk.gpsimd
        def _(gp):
            if dbg:
                sOUT.dma(gp.dma_start(dbg_t["dbg_kf"].ap(), KF[:, 0:NKV * 2048]))
                sOUT.dma(gp.dma_start(dbg_t["dbg_vt"].ap(), VT[:, :]))
                sOUT.dma(gp.dma_start(dbg_t["dbg_q"].ap(), q_scr.ap()))
                if stop_after in ("attn0", "x1", "full"):
                    sOUT.dma(gp.dma_start(dbg_t["dbg_y"].ap(), y_scr.ap()))
                if stop_after in ("x1", "full"):
                    sOUT.dma(gp.dma_start(dbg_t["dbg_x1"].ap(), x1_scr.ap()))
                    sOUT.dma(gp.dma_start(dbg_t["dbg_m"].ap(), mrow_scr.ap()))
            gp.wait_ge(sOUT.h, sOUT.n)
            gp.wait_ge(sX1.h, sX1.n)
            gp.wait_ge(sYW.h, sYW.n)
            gp.wait_ge(sQW.h, sQW.n)
            gp.wait_ge(sCVO.h, sCVO.n)
            gp.wait_ge(sMRW.h, sMRW.n)

    return nc


def kernel(L0_attn_x, L0_attn_Wq, L0_attn_Wk, L0_attn_Wv, L0_attn_Wo,
           L0_ffn_Wg, L0_ffn_W0a, L0_ffn_b0a, L0_ffn_W0b, L0_ffn_b0b,
           L0_ffn_W1a, L0_ffn_b1a, L0_ffn_W1b, L0_ffn_b1b):
    x = np.asarray(L0_attn_x, np.float32)
    common = host_inputs(L0_attn_Wq, L0_attn_Wk, L0_attn_Wv, L0_attn_Wo, L0_ffn_Wg,
                         L0_ffn_W0a, L0_ffn_b0a, L0_ffn_W0b, L0_ffn_b0b,
                         L0_ffn_W1a, L0_ffn_b1a, L0_ffn_W1b, L0_ffn_b1b)
    in_maps = []
    for core in range(8):
        imgs = [_sbimg(np.ascontiguousarray(x[core * 2 + j].T)) for j in range(2)]
        m = dict(common)
        m["x_img"] = np.ascontiguousarray(np.concatenate(imgs, axis=1))
        in_maps.append(m)
    nc = build("full", dbg=False)
    res = run_bass_kernel_spmd(nc, in_maps, core_ids=list(range(8)))
    outs = []
    for r in res.results:
        o = np.asarray(r["out_img"]).reshape(128, 16, 4096).transpose(1, 0, 2).reshape(2048, 4096)
        outs.append(np.ascontiguousarray(o[:, 0:2048].T))
        outs.append(np.ascontiguousarray(o[:, 2048:4096].T))
    return np.stack(outs).astype(np.float32)


# version 06
# speedup vs baseline: 1.5120x; 1.0303x over previous
import contextlib
import math
import numpy as np
import ml_dtypes
import concourse.bass as bass
import concourse.mybir as mybir
from concourse.bass_utils import run_bass_kernel_spmd

F32 = mybir.dt.float32
BF16 = mybir.dt.bfloat16
AF = mybir.ActivationFunctionType
ALU = mybir.AluOpType
AX = mybir.AxisListType

B_LOC, T, D, NCH = 2, 2048, 2048, 16
NH, NKV, HS = 16, 4, 128
WINDOW, SINK = 1024, 4
FF, NFT = 4096, 32
GB = 512
NG = (B_LOC * T) // GB
QSCALE = 1.0 / math.sqrt(HS)
EPS = 1e-5
NEGBIG = -1.0e30


def _sbimg(a):
    K, M = a.shape
    return np.ascontiguousarray(a.reshape(K // 128, 128, M).transpose(1, 0, 2).reshape(128, -1))


def host_inputs(Wq, Wk, Wv, Wo, Wg, W0a, b0a, W0b, b0b, W1a, b1a, W1b, b1b):
    f = np.float32
    out = {}
    out["wq_img"] = _sbimg(np.ascontiguousarray(np.asarray(Wq, f).T))
    out["wk_img"] = _sbimg(np.ascontiguousarray(np.asarray(Wk, f).T))
    out["wv_img"] = _sbimg(np.ascontiguousarray(np.asarray(Wv, f).T))
    out["wo_img"] = _sbimg(np.ascontiguousarray(np.asarray(Wo, f).T))
    for e, W in ((0, W0a), (1, W1a)):
        A = np.ascontiguousarray(np.asarray(W, f).T)
        out[f"w{e}a_img"] = np.ascontiguousarray(
            A.reshape(16, 128, 32, 128).transpose(1, 2, 0, 3).reshape(128, 65536))
    for e, W in ((0, W0b), (1, W1b)):
        Bt = np.ascontiguousarray(np.asarray(W, f).T)
        out[f"w{e}b_img"] = np.ascontiguousarray(
            Bt.reshape(32, 128, 4, 4, 128).transpose(1, 2, 0, 3, 4).reshape(128, 65536))
    out["wg_img"] = _sbimg(np.ascontiguousarray(np.asarray(Wg, f).T))
    out["b0a_img"] = np.ascontiguousarray(np.asarray(b0a, f).reshape(32, 128).T)
    out["b1a_img"] = np.ascontiguousarray(np.asarray(b1a, f).reshape(32, 128).T)
    out["b0b_img"] = np.ascontiguousarray(np.asarray(b0b, f).reshape(16, 128).T)
    out["b1b_img"] = np.ascontiguousarray(np.asarray(b1b, f).reshape(16, 128).T)
    ii = np.arange(128)[:, None]
    jj = np.arange(128)[None, :]
    m_lower = np.where(jj <= ii, 0.0, NEGBIG)
    m_far = np.where(jj > ii, 0.0, NEGBIG)
    m_farsink = np.where((jj > ii) | (jj < SINK), 0.0, NEGBIG)
    out["masks_img"] = np.concatenate([m_lower, m_far, m_farsink], 1).astype(ml_dtypes.bfloat16)
    out["ident_img"] = np.eye(128, dtype=ml_dtypes.bfloat16)
    out["onesb_img"] = np.ones((128, 1), dtype=ml_dtypes.bfloat16)
    out["onesr_img"] = np.ones((1, 128), np.float32)
    return out


def build(stop_after="full", dbg=False):
    es = contextlib.ExitStack()
    nc = bass.Bass("TRN2", target_bir_lowering=False, debug=False)

    def din(name, shape, dt):
        return nc.dram_tensor(name, shape, dt, kind="ExternalInput")

    x_img = din("x_img", [128, B_LOC * NCH * T], F32)
    wq_img = din("wq_img", [128, NCH * D], F32)
    wk_img = din("wk_img", [128, NCH * 512], F32)
    wv_img = din("wv_img", [128, NCH * 512], F32)
    wo_img = din("wo_img", [128, NCH * D], F32)
    wa_img = [din("w0a_img", [128, 65536], F32), din("w1a_img", [128, 65536], F32)]
    wb_img = [din("w0b_img", [128, 65536], F32), din("w1b_img", [128, 65536], F32)]
    wg_img = din("wg_img", [128, 32], F32)
    ba_img = [din("b0a_img", [128, 32], F32), din("b1a_img", [128, 32], F32)]
    bb_img = [din("b0b_img", [128, 16], F32), din("b1b_img", [128, 16], F32)]
    masks_img = din("masks_img", [128, 384], BF16)
    ident_img = din("ident_img", [128, 128], BF16)
    onesb_img = din("onesb_img", [128, 1], BF16)
    onesr_img = din("onesr_img", [1, 128], F32)

    q_scr = nc.dram_tensor("q_scr", [128, NH * 4096], BF16, kind="Internal")
    y_scr = nc.dram_tensor("y_scr", [128, NCH * 4096], BF16, kind="Internal")
    x1_scr = nc.dram_tensor("x1_scr", [128, NCH * 4096], BF16, kind="Internal")
    mrow_scr = nc.dram_tensor("mrow_scr", [1, 4096], F32, kind="Internal")
    wa_bf = [nc.dram_tensor(f"w{e}a_bf", [128, 65536], BF16, kind="Internal") for e in range(2)]
    wb_bf = [nc.dram_tensor(f"w{e}b_bf", [128, 65536], BF16, kind="Internal") for e in range(2)]
    out_img = nc.dram_tensor("out_img", [128, NCH * 4096], F32, kind="ExternalOutput")
    dbg_t = {}
    if dbg:
        dbg_t["dbg_kf"] = nc.dram_tensor("dbg_kf", [128, NKV * T], BF16, kind="ExternalOutput")
        dbg_t["dbg_vt"] = nc.dram_tensor("dbg_vt", [128, 16 * 512], BF16, kind="ExternalOutput")
        dbg_t["dbg_q"] = nc.dram_tensor("dbg_q", [128, NH * 4096], BF16, kind="ExternalOutput")
        dbg_t["dbg_y"] = nc.dram_tensor("dbg_y", [128, NCH * 4096], BF16, kind="ExternalOutput")
        dbg_t["dbg_x1"] = nc.dram_tensor("dbg_x1", [128, NCH * 4096], BF16, kind="ExternalOutput")
        dbg_t["dbg_m"] = nc.dram_tensor("dbg_m", [1, 4096], F32, kind="ExternalOutput")

    def dap(t, col0, cols, nseg=1, segstride=0):
        W = t.shape[1]
        if nseg == 1:
            return bass.AP(t, col0, [[W, 128], [1, cols]])
        return bass.AP(t, col0, [[W, 128], [segstride, nseg], [1, cols]])

    BUF1 = nc.alloc_sbuf_tensor("BUF1", [128, 32768], BF16)
    BUF2 = nc.alloc_sbuf_tensor("BUF2", [128, 32768], BF16)
    KF = nc.alloc_sbuf_tensor("KF", [128, 8192], BF16)
    VT = nc.alloc_sbuf_tensor("VT", [128, 8192], BF16)
    BUF5 = nc.alloc_sbuf_tensor("BUF5", [128, 7680], BF16)
    FSCR = nc.alloc_sbuf_tensor("FSCR", [128, 4096], F32)
    XR = nc.alloc_sbuf_tensor("XR", [128, 2048], F32)
    ROWS = nc.alloc_sbuf_tensor("ROWS", [1, 1024], F32)
    MROW = nc.alloc_sbuf_tensor("MROW", [1, 512], F32)
    masks = nc.alloc_sbuf_tensor("masks", [128, 384], BF16)
    ident = nc.alloc_sbuf_tensor("ident", [128, 128], BF16)
    onesb = nc.alloc_sbuf_tensor("onesb", [128, 1], BF16)
    onesr = nc.alloc_sbuf_tensor("onesr", [1, 128], F32)
    wgsb = nc.alloc_sbuf_tensor("wgsb", [128, 32], F32)
    wgd = nc.alloc_sbuf_tensor("wgd", [128, 16], F32)
    swgd = nc.alloc_sbuf_tensor("swgd", [1, 1], F32)
    basb = [nc.alloc_sbuf_tensor(f"b{e}a_sb", [128, 32], F32) for e in range(2)]
    bbsb = [nc.alloc_sbuf_tensor(f"b{e}b_sb", [128, 16], F32) for e in range(2)]
    bdiff = nc.alloc_sbuf_tensor("bdiff", [128, 16], F32)
    nmx = nc.alloc_sbuf_tensor("nmx", [128, 2], F32)
    rsm = nc.alloc_sbuf_tensor("rsm", [128, 2], F32)
    rcp = nc.alloc_sbuf_tensor("rcp", [128, 2], F32)
    diagb = nc.alloc_sbuf_tensor("diagb", [128, 256], BF16)
    fenb = nc.alloc_sbuf_tensor("fenb", [128, 32], BF16)

    PS = nc.alloc_psum_tensor("PS", [128, 4096], F32)

    PP0, PP1, YQ0, YQ1, CBBF_O = 0, 1280, 2560, 4608, 6656

    class Sem:
        def __init__(self, name):
            self.h = es.enter_context(nc.semaphore(name))
            self.n = 0

        def inc(self, instr, k=1):
            instr.then_inc(self.h, k)
            self.n += k
            return self.n

        def dma(self, instr):
            return self.inc(instr, 16)

    sPE, sACT, sDVE, sPOOL = Sem("sPE"), Sem("sACT"), Sem("sDVE"), Sem("sPOOL")

    def pool_fence(gp):
        gp.tensor_copy(fenb[:, 0:16], fenb[:, 16:32])
        gp.tensor_copy(fenb[:, 16:32], fenb[:, 0:16])
        gp.tensor_copy(fenb[:, 0:16], fenb[:, 16:32])
        return sPOOL.inc(gp.tensor_copy(fenb[:, 16:32], fenb[:, 0:16]))
    sLD, sLDQ, sLDX = Sem("sLD"), Sem("sLDQ"), Sem("sLDX")
    sCV, sCVO = Sem("sCV"), Sem("sCVO")
    sQW, sYW, sX1, sOUT, sMRW = Sem("sQW"), Sem("sYW"), Sem("sX1"), Sem("sOUT"), Sem("sMRW")

    CVW = 1024
    conv_jobs = []
    for e in range(2):
        for s in range(65536 // CVW):
            conv_jobs.append((wa_img[e], wa_bf[e], s * CVW))
    for e in range(2):
        for s in range(65536 // CVW):
            conv_jobs.append((wb_img[e], wb_bf[e], s * CVW))
    conv_li = [0]
    conv_si = [0]

    def emit_conversion_load(sy, njobs):
        n = 0
        while n < njobs and conv_li[0] < len(conv_jobs):
            i = conv_li[0]
            conv_li[0] += 1
            n += 1
            par = i % 2
            src, dst, col = conv_jobs[i]
            if i >= 2:
                sy.wait_ge(sCVO.h, 16 * (i - 1))
            sy.dma_start(FSCR[:, par * 1024:(par + 1) * 1024], dap(src, col, CVW)).then_inc(sCV.h, 16)
            sCV.n += 16

    def emit_conversion_store(gp, njobs):
        n = 0
        while n < njobs and conv_si[0] < len(conv_jobs):
            i = conv_si[0]
            conv_si[0] += 1
            n += 1
            par = i % 2
            src, dst, col = conv_jobs[i]
            gp.wait_ge(sCV.h, 16 * (i + 1))
            if i >= 1:
                gp.wait_ge(sCVO.h, 16 * i)
            gp.tensor_copy(BUF5[:, CBBF_O:CBBF_O + CVW], FSCR[:, par * 1024:(par + 1) * 1024])
            gp.dma_start(dap(dst, col, CVW), BUF5[:, CBBF_O:CBBF_O + CVW]).then_inc(sCVO.h, 16)
            sCVO.n += 16

    with nc.Block(no_gpsimd_drain=True) as blk:
        @blk.sync
        def _(sy):
            for dst, src in ((masks, masks_img), (ident, ident_img), (onesb, onesb_img),
                             (onesr, onesr_img),
                             (wgsb, wg_img), (basb[0], ba_img[0]), (basb[1], ba_img[1]),
                             (bbsb[0], bb_img[0]), (bbsb[1], bb_img[1])):
                sLD.dma(sy.dma_start(dst[:, :], src.ap()))
            sy.wait_ge(sLD.h, sLD.n)

        @blk.vector
        def _(v):
            v.wait_ge(sLD.h, sLD.n)
            v.tensor_tensor(bdiff[:, :], bbsb[1][:, :], bbsb[0][:, :], op=ALU.subtract)
            sDVE.inc(v.tensor_tensor(wgd[:, :],
                                     bass.AP(wgsb, 0, [[32, 128], [2, 16]]),
                                     bass.AP(wgsb, 1, [[32, 128], [2, 16]]), op=ALU.subtract))

        @blk.gpsimd
        def _(gp):
            gp.wait_ge(sDVE.h, sDVE.n)
            sPOOL.inc(gp.tensor_reduce(ROWS[0:1, 0:1], wgd[:, :], axis=AX.XYZWC, op=ALU.add))

        @blk.scalar
        def _(sc):
            sc.wait_ge(sPOOL.h, sPOOL.n)
            sACT.inc(sc.copy(swgd[0:1, 0:1], ROWS[0:1, 0:1]))

    def phase_a0a1(b):
        xbase = b * (NCH * T)
        with nc.Block(no_gpsimd_drain=True) as blk:
            strips = ([("wk", s) for s in range(4)] + [("wv", s) for s in range(4)]
                      + [("x", c) for c in range(NCH)])
            cast_at = {}
            pool_base = sPOOL.n

            @blk.sync
            def _(sy):
                for i, (kind, idx) in enumerate(strips):
                    par = i % 2
                    if i >= 2:
                        sy.wait_ge(sPOOL.h, pool_base + i - 1)
                    if kind == "x":
                        src = dap(x_img, xbase + idx * T, T)
                    elif kind == "wk":
                        src = dap(wk_img, idx * 2048, 2048)
                    else:
                        src = dap(wv_img, idx * 2048, 2048)
                    sLD.dma(sy.dma_start(FSCR[:, par * 2048:(par + 1) * 2048], src))

            @blk.gpsimd
            def _(gp):
                n0 = sLD.n - 16 * len(strips)
                for i, (kind, idx) in enumerate(strips):
                    par = i % 2
                    gp.wait_ge(sLD.h, n0 + 16 * (i + 1))
                    if kind == "x":
                        dst = BUF1[:, idx * 2048:(idx + 1) * 2048]
                    elif kind == "wk":
                        dst = BUF2[:, idx * 2048:(idx + 1) * 2048]
                    else:
                        dst = BUF2[:, 8192 + idx * 2048:8192 + (idx + 1) * 2048]
                    cast_at[(kind, idx)] = sPOOL.inc(
                        gp.tensor_copy(dst, FSCR[:, par * 2048:(par + 1) * 2048]))
                cast_at["fence"] = pool_fence(gp)

            kv_jobs = [("k", ko, tb) for ko in range(4) for tb in range(4)] \
                    + [("v", tt, 0) for tt in range(16)]
            mm_at, ev_at = {}, {}
            act_base = sACT.n

            @blk.tensor
            def _(te):
                for j, job in enumerate(kv_jobs):
                    pcol = (j % 2) * 512
                    if j == 1:
                        te.wait_ge(sPOOL.h, cast_at["fence"])
                    if j >= 2:
                        te.wait_ge(sACT.h, act_base + j - 1)
                    ins = None
                    for dc in range(NCH):
                        if j == 0:
                            te.wait_ge(sPOOL.h, min(cast_at[("x", dc)] + 2, cast_at["fence"]))
                        if job[0] == "k":
                            ko, tb = job[1], job[2]
                            lhsT = BUF2[:, dc * 512 + ko * 128:dc * 512 + ko * 128 + 128]
                            rhs = BUF1[:, dc * 2048 + tb * 512:dc * 2048 + tb * 512 + 512]
                        else:
                            tt = job[1]
                            lhsT = BUF1[:, dc * 2048 + tt * 128:dc * 2048 + tt * 128 + 128]
                            rhs = BUF2[:, 8192 + dc * 512:8192 + dc * 512 + 512]
                        ins = te.matmul(PS[:, pcol:pcol + 512], lhsT, rhs,
                                        start=(dc == 0), stop=(dc == NCH - 1))
                    mm_at[j] = sPE.inc(ins)

            @blk.scalar
            def _(sc):
                for j, job in enumerate(kv_jobs):
                    pcol = (j % 2) * 512
                    sc.wait_ge(sPE.h, mm_at[j])
                    if job[0] == "k":
                        ko, tb = job[1], job[2]
                        dst = KF[:, ko * 2048 + tb * 512:ko * 2048 + tb * 512 + 512]
                    else:
                        tt = job[1]
                        dst = VT[:, tt * 512:(tt + 1) * 512]
                    ev_at[j] = sACT.inc(sc.copy(dst, PS[:, pcol:pcol + 512]))

    def phase_a2(b):
        with nc.Block(no_gpsimd_drain=True) as blk:
            cast_at = {}
            q_mm = {}
            act_base = sACT.n
            q_half_done = {}
            blk_pool_base = sPOOL.n
            qw_base = sQW.n

            for half in range(2):
                pool_base0 = blk_pool_base

                @blk.sync
                def _(sy, half=half):
                    def cast_ctr(h, c):
                        return pool_base0 + h * (NCH + 1) + c + 1
                    for c in range(NCH):
                        i = half * NCH + c
                        par = i % 4
                        if i >= 4:
                            ph, pc = divmod(i - 4, NCH)
                            sy.wait_ge(sPOOL.h, cast_ctr(ph, pc))
                        sLD.dma(sy.dma_start(FSCR[:, par * 1024:(par + 1) * 1024],
                                             dap(wq_img, c * 2048 + half * 1024, 1024)))

                @blk.gpsimd
                def _(gp, half=half):
                    n0 = sLD.n - 16 * NCH
                    for c in range(NCH):
                        i = half * NCH + c
                        par = i % 4
                        gp.wait_ge(sLD.h, n0 + 16 * (c + 1))
                        if half == 1 and c == 0:
                            gp.wait_ge(sPE.h, q_half_done[0])
                        cast_at[(half, c)] = sPOOL.inc(
                            gp.tensor_copy(BUF2[:, c * 1024:(c + 1) * 1024],
                                           FSCR[:, par * 1024:(par + 1) * 1024]))
                    cast_at[(half, "fence")] = pool_fence(gp)

                @blk.tensor
                def _(te, half=half):
                    for o8 in range(8):
                        for tb in range(4):
                            j = half * 32 + o8 * 4 + tb
                            jj = o8 * 4 + tb
                            pcol = (j % 2) * 512
                            if jj == 1:
                                te.wait_ge(sPOOL.h, cast_at[(half, "fence")])
                            if j >= 2:
                                te.wait_ge(sACT.h, act_base + j - 1)
                            ins = None
                            for dc in range(NCH):
                                if jj == 0:
                                    te.wait_ge(sPOOL.h,
                                               min(cast_at[(half, dc)] + 2, cast_at[(half, "fence")]))
                                ins = te.matmul(PS[:, pcol:pcol + 512],
                                                BUF2[:, dc * 1024 + o8 * 128:dc * 1024 + o8 * 128 + 128],
                                                BUF1[:, dc * 2048 + tb * 512:dc * 2048 + tb * 512 + 512],
                                                start=(dc == 0), stop=(dc == NCH - 1))
                            q_mm[j] = sPE.inc(ins)
                    if half == 0:
                        q_half_done[0] = sPE.n

                q_ev = {}

                @blk.scalar
                def _(sc, half=half):
                    for o8 in range(8):
                        for tb in range(4):
                            j = half * 32 + o8 * 4 + tb
                            pcol = (j % 2) * 512
                            par4 = j % 4
                            sc.wait_ge(sPE.h, q_mm[j])
                            if j >= 4:
                                sc.wait_ge(sQW.h, qw_base + 16 * (j - 3))
                            dst = BUF2[:, 16384 + par4 * 512:16384 + par4 * 512 + 512]
                            q_ev[j] = sACT.inc(sc.mul(dst, PS[:, pcol:pcol + 512], QSCALE))

                @blk.sync
                def _(sy, half=half, b=b):
                    for o8 in range(8):
                        for tb in range(4):
                            j = half * 32 + o8 * 4 + tb
                            par4 = j % 4
                            sy.wait_ge(sACT.h, q_ev[j])
                            o = half * 8 + o8
                            sQW.dma(sy.dma_start(
                                dap(q_scr, o * 4096 + b * 2048 + tb * 512, 512),
                                BUF2[:, 16384 + par4 * 512:16384 + par4 * 512 + 512]))

    def qt_geom(qt):
        nw = min(qt + 1, 9)
        k0t = qt - nw + 1
        krange = nw * 128
        has_sink = qt >= 9
        kw = krange + (SINK if has_sink else 0)
        return nw, k0t, krange, has_sink, kw

    def phase_a4(b):
        NHEADS = 16 * NH
        with nc.Block(no_gpsimd_drain=True) as blk:
            qld_at, scdone = {}, {}
            sc_at, max_at, exp_at, recip_at, diag_at = {}, {}, {}, {}, {}
            trans_at, ptev_at, av_at, yev_at, ydma_at = {}, {}, {}, {}, {}

            @blk.sync
            def _(sy):
                for qt in (0, 1):
                    sLDX.dma(sy.dma_start(BUF2[:, (qt % 2) * 2048:(qt % 2) * 2048 + 2048],
                                          dap(q_scr, b * 2048 + qt * 128, 128, nseg=16, segstride=4096)))
                    qld_at[qt] = sLDX.n

            def geom(g):
                qt, h = divmod(g, NH)
                nw, k0t, krange, has_sink, kw = qt_geom(qt)
                return dict(qt=qt, h=h, kv=h // 4, par=g % 2, nw=nw, k0t=k0t,
                            krange=krange, has_sink=has_sink, kw=kw,
                            ntile=nw + (1 if has_sink else 0))

            for t in range(NHEADS + 2):
                gs = geom(t) if t < NHEADS else None
                g1 = geom(t - 1) if t >= 1 else None
                g2 = geom(t - 2) if t >= 2 else None

                if 1 <= t <= NHEADS:
                    @blk.vector
                    def _(v, t=t, g1=g1):
                        v.wait_ge(sACT.h, exp_at[t - 1])
                        recip_at[t - 1] = sDVE.inc(
                            v.reciprocal(rcp[:, g1["par"]:g1["par"] + 1],
                                         rsm[:, g1["par"]:g1["par"] + 1]))

                    @blk.scalar
                    def _(sc, t=t, g1=g1):
                        sc.wait_ge(sDVE.h, recip_at[t - 1])
                        if t >= 3:
                            sc.wait_ge(sPE.h, trans_at[t - 3])
                        diag_at[t - 1] = sACT.inc(
                            sc.mul(diagb[:, g1["par"] * 128:g1["par"] * 128 + 128],
                                   ident[:, :], rcp[:, g1["par"]:g1["par"] + 1]))

                @blk.tensor
                def _(te, t=t, gs=gs, g1=g1, g2=g2, b=b):
                    if t >= 2:
                        te.wait_ge(sDVE.h, ptev_at[t - 2])
                        if t >= 4:
                            te.wait_ge(sACT.h, yev_at[t - 4])
                        sb2 = 1536 * g2["par"]
                        ppo = PP0 if g2["par"] == 0 else PP1
                        ycol = 3072 if g2["par"] == 0 else 3584
                        ins = None
                        for i in range(g2["nw"]):
                            kt = g2["k0t"] + i
                            ins = te.matmul(PS[:, ycol:ycol + 128],
                                            VT[:, kt * 512 + g2["kv"] * 128:kt * 512 + g2["kv"] * 128 + 128],
                                            BUF5[:, ppo + i * 128:ppo + i * 128 + 128],
                                            start=(i == 0),
                                            stop=(i == g2["nw"] - 1 and not g2["has_sink"]))
                        if g2["has_sink"]:
                            ins = te.matmul(PS[:, ycol:ycol + 128],
                                            VT[0:4, g2["kv"] * 128:g2["kv"] * 128 + 128],
                                            BUF5[0:4, ppo + g2["nw"] * 128:ppo + g2["nw"] * 128 + 128],
                                            start=False, stop=True)
                        av_at[t - 2] = sPE.inc(ins)
                    if t < NHEADS:
                        te.wait_ge(sLDX.h, qld_at[gs["qt"]])
                        if t >= 2:
                            te.wait_ge(sACT.h, exp_at[t - 2])
                            te.wait_ge(sDVE.h, ptev_at[t - 2])
                        sb = 1536 * gs["par"]
                        lhsq = BUF2[:, (gs["qt"] % 2) * 2048 + gs["h"] * 128:
                                    (gs["qt"] % 2) * 2048 + gs["h"] * 128 + 128]
                        col = 0
                        ins = None
                        while col < gs["krange"]:
                            w = min(512, gs["krange"] - col)
                            rhs = KF[:, gs["kv"] * 2048 + gs["k0t"] * 128 + col:
                                     gs["kv"] * 2048 + gs["k0t"] * 128 + col + w]
                            ins = te.matmul(PS[:, sb + col:sb + col + w], lhsq, rhs,
                                            start=True, stop=True)
                            col += w
                        if gs["has_sink"]:
                            ins = te.matmul(PS[:, sb + gs["krange"]:sb + gs["krange"] + SINK],
                                            lhsq, KF[:, gs["kv"] * 2048:gs["kv"] * 2048 + SINK],
                                            start=True, stop=True)
                        sc_at[t] = sPE.inc(ins)
                        if gs["h"] == NH - 1:
                            scdone[gs["qt"]] = sPE.n
                    if 1 <= t <= NHEADS:
                        te.wait_ge(sACT.h, diag_at[t - 1])
                        sb1 = 1536 * g1["par"]
                        ppo1 = PP0 if g1["par"] == 0 else PP1
                        dg = diagb[:, g1["par"] * 128:g1["par"] * 128 + 128]
                        ins = None
                        for i in range(g1["ntile"]):
                            sink_t = g1["has_sink"] and (i == g1["ntile"] - 1)
                            if sink_t:
                                ins = te.matmul(PS[0:4, sb1 + i * 128:sb1 + (i + 1) * 128],
                                                BUF5[:, ppo1 + g1["krange"]:ppo1 + g1["krange"] + SINK],
                                                dg, start=True, stop=True)
                            else:
                                ins = te.matmul(PS[:, sb1 + i * 128:sb1 + (i + 1) * 128],
                                                BUF5[:, ppo1 + i * 128:ppo1 + i * 128 + 128],
                                                dg, start=True, stop=True)
                        trans_at[t - 1] = sPE.inc(ins)

                @blk.vector
                def _(v, t=t, gs=gs, g1=g1):
                    if t < NHEADS:
                        v.wait_ge(sPE.h, sc_at[t])
                        if t >= 2:
                            v.wait_ge(sACT.h, exp_at[t - 2])
                        sb = 1536 * gs["par"]
                        qt = gs["qt"]
                        krange = gs["krange"]
                        v.tensor_tensor(PS[:, sb + krange - 128:sb + krange],
                                        PS[:, sb + krange - 128:sb + krange],
                                        masks[:, 0:128], op=ALU.add)
                        if qt >= 9:
                            v.tensor_tensor(PS[:, sb:sb + 128], PS[:, sb:sb + 128],
                                            masks[:, 128:256], op=ALU.add)
                        elif qt == 8:
                            v.tensor_tensor(PS[:, sb:sb + 128], PS[:, sb:sb + 128],
                                            masks[:, 256:384], op=ALU.add)
                        max_at[t] = sDVE.inc(
                            v.reduce_max(nmx[:, gs["par"]:gs["par"] + 1],
                                         PS[:, sb + krange - 128:sb + krange],
                                         axis=AX.X, negate=True))
                    if 1 <= t <= NHEADS:
                        v.wait_ge(sPE.h, trans_at[t - 1])
                        sb1 = 1536 * g1["par"]
                        ppo1 = PP0 if g1["par"] == 0 else PP1
                        w = g1["ntile"] * 128
                        ptev_at[t - 1] = sDVE.inc(
                            v.tensor_copy(BUF5[:, ppo1:ppo1 + w], PS[:, sb1:sb1 + w]))

                @blk.scalar
                def _(sc, t=t, gs=gs, g2=g2):
                    if t < NHEADS:
                        sc.wait_ge(sDVE.h, max_at[t])
                        if t >= 2:
                            sc.wait_ge(sPE.h, av_at[t - 2])
                        sb = 1536 * gs["par"]
                        ppo = PP0 if gs["par"] == 0 else PP1
                        exp_at[t] = sACT.inc(
                            sc.activation(BUF5[:, ppo:ppo + gs["kw"]], PS[:, sb:sb + gs["kw"]],
                                          AF.Exp, bias=nmx[:, gs["par"]:gs["par"] + 1],
                                          scale=1.0, accum_out=rsm[:, gs["par"]:gs["par"] + 1]))
                    if t >= 2:
                        sc.wait_ge(sPE.h, av_at[t - 2])
                        qt2, h2 = divmod(t - 2, NH)
                        if h2 == 0 and qt2 >= 2:
                            sc.wait_ge(sYW.h, ydma_at[qt2 - 2])
                        ycol = 3072 if g2["par"] == 0 else 3584
                        yo = (YQ0 if qt2 % 2 == 0 else YQ1) + h2 * 128
                        yev_at[t - 2] = sACT.inc(sc.copy(BUF5[:, yo:yo + 128],
                                                         PS[:, ycol:ycol + 128]))

                if t < NHEADS and t % NH == NH - 1:
                    @blk.sync
                    def _(sy, t=t, b=b):
                        qt = t // NH
                        if qt + 2 < 16:
                            sy.wait_ge(sPE.h, scdone[qt])
                            nqt = qt + 2
                            sLDX.dma(sy.dma_start(
                                BUF2[:, (nqt % 2) * 2048:(nqt % 2) * 2048 + 2048],
                                dap(q_scr, b * 2048 + nqt * 128, 128, nseg=16, segstride=4096)))
                            qld_at[nqt] = sLDX.n
                        emit_conversion_load(sy, 8)

                if t >= 2 and (t - 2) % NH == NH - 1:
                    @blk.gpsimd
                    def _(gp, t=t, b=b):
                        qt = (t - 2) // NH
                        gp.wait_ge(sACT.h, yev_at[t - 2])
                        yo = YQ0 if qt % 2 == 0 else YQ1
                        sYW.dma(gp.dma_start(
                            dap(y_scr, b * 2048 + qt * 128, 128, nseg=16, segstride=4096),
                            BUF5[:, yo:yo + 2048]))
                        ydma_at[qt] = sYW.n
                        emit_conversion_store(gp, 8)

            @blk.sync
            def _(sy):
                emit_conversion_load(sy, conv_si[0] - conv_li[0])

    def phase_aprime():
        ntb = (B_LOC * T) // 256
        with nc.Block(no_gpsimd_drain=True) as blk:
            cast_at = {}
            pool_base = sPOOL.n

            @blk.sync
            def _(sy):
                for c in range(NCH):
                    par = c % 2
                    if c >= 2:
                        sy.wait_ge(sPOOL.h, pool_base + c - 1)
                    sLD.dma(sy.dma_start(FSCR[:, par * 2048:(par + 1) * 2048],
                                         dap(wo_img, c * 2048, 2048)))

            @blk.gpsimd
            def _(gp):
                n0 = sLD.n - 16 * NCH
                for c in range(NCH):
                    par = c % 2
                    gp.wait_ge(sLD.h, n0 + 16 * (c + 1))
                    cast_at[c] = sPOOL.inc(gp.tensor_copy(BUF1[:, c * 2048:(c + 1) * 2048],
                                                          FSCR[:, par * 2048:(par + 1) * 2048]))
                cast_at["fence"] = pool_fence(gp)
                gp.wait_ge(sYW.h, sYW.n)

            ytb_at, xr_at, wo_at, ev_at = {}, {}, {}, {}
            zb_at, sq_at, stats_at, rowmath_done, apm_at = {}, {}, {}, {}, {}
            bc_done = {}

            @blk.sync
            def _(sy):
                for tb in (0, 1):
                    sLDX.dma(sy.dma_start(BUF2[:, (tb % 2) * 4096:(tb % 2) * 4096 + 4096],
                                          dap(y_scr, tb * 256, 256, nseg=16, segstride=4096)))
                    ytb_at[tb] = sLDX.n

            def emit_stats(te, tb, dot):
                te.wait_ge(sACT.h, sq_at[(tb, dot)])
                if dot == 0 and tb >= 1:
                    te.wait_ge(sACT.h, rowmath_done[tb - 1])
                te.matmul(PS[0:1, 2048:2304], onesb[:, 0:1],
                          BUF5[:, (dot % 2) * 256:(dot % 2) * 256 + 256],
                          start=(dot == 0), stop=(dot == NCH - 1))
                te.matmul(PS[0:1, 2560:2816], onesb[:, 0:1],
                          BUF5[:, 512 + (dot % 2) * 256:512 + (dot % 2) * 256 + 256],
                          start=(dot == 0), stop=(dot == NCH - 1))
                ins = te.matmul(PS[0:1, 3072:3328], wgd[:, dot:dot + 1],
                                FSCR[:, dot * 256:(dot + 1) * 256],
                                start=(dot == 0), stop=(dot == NCH - 1))
                stats_at[(tb, dot)] = sPE.inc(ins)

            for tb in range(ntb):
                par = tb % 2
                xoff = (tb // 8) * (NCH * T) + (tb % 8) * 256

                for dot in range(NCH):
                    slot = dot % 4
                    gidx = tb * NCH + dot

                    @blk.sync
                    def _(sy, tb=tb, dot=dot, slot=slot, gidx=gidx, xoff=xoff):
                        if gidx >= 4:
                            pk = (tb, dot - 4) if dot >= 4 else (tb - 1, dot + 12)
                            sy.wait_ge(sDVE.h, ev_at[pk])
                        sLD.dma(sy.dma_start(XR[:, slot * 256:(slot + 1) * 256],
                                             dap(x_img, xoff + dot * T, 256)))
                        xr_at[(tb, dot)] = sLD.n

                    @blk.tensor
                    def _(te, tb=tb, dot=dot, par=par):
                        if dot == 0:
                            te.wait_ge(sPOOL.h, cast_at["fence"])
                            te.wait_ge(sLDX.h, ytb_at[tb])
                        pcol = (dot % 4) * 512
                        pk = (tb, dot - 4) if dot >= 4 else ((tb - 1, dot + 12) if tb >= 1 else None)
                        if pk is not None:
                            te.wait_ge(sDVE.h, ev_at[pk])
                        ins = None
                        for dc in range(NCH):
                            ins = te.matmul(PS[:, pcol:pcol + 256],
                                            BUF1[:, dc * 2048 + dot * 128:dc * 2048 + dot * 128 + 128],
                                            BUF2[:, par * 4096 + dc * 256:par * 4096 + dc * 256 + 256],
                                            start=(dc == 0), stop=(dc == NCH - 1))
                        wo_at[(tb, dot)] = sPE.inc(ins)
                        if dot >= 1:
                            emit_stats(te, tb, dot - 1)

                    @blk.vector
                    def _(v, tb=tb, dot=dot, slot=slot):
                        v.wait_ge(sPE.h, wo_at[(tb, dot)])
                        v.wait_ge(sLD.h, xr_at[(tb, dot)])
                        if tb >= 1 and dot == 0:
                            v.wait_ge(sDVE.h, apm_at[(tb - 1, NCH - 1)])
                            v.wait_ge(sACT.h, sq_at[(tb - 1, NCH - 1)])
                            v.wait_ge(sPE.h, stats_at[(tb - 1, NCH - 1)])
                        pcol = (dot % 4) * 512
                        ev_at[(tb, dot)] = sDVE.inc(
                            v.tensor_add(FSCR[:, dot * 256:(dot + 1) * 256],
                                         PS[:, pcol:pcol + 256], XR[:, slot * 256:(slot + 1) * 256]))

                    @blk.scalar
                    def _(sc, tb=tb, dot=dot):
                        sc.wait_ge(sDVE.h, ev_at[(tb, dot)])
                        if (tb, dot - 2) in stats_at:
                            sc.wait_ge(sPE.h, stats_at[(tb, dot - 2)])
                        elif tb >= 1 and dot < 2:
                            sc.wait_ge(sPE.h, stats_at[(tb - 1, dot + 14)])
                        sc.copy(BUF5[:, (dot % 2) * 256:(dot % 2) * 256 + 256],
                                FSCR[:, dot * 256:(dot + 1) * 256])
                        sq_at[(tb, dot)] = sACT.inc(
                            sc.activation(BUF5[:, 512 + (dot % 2) * 256:512 + (dot % 2) * 256 + 256],
                                          BUF5[:, (dot % 2) * 256:(dot % 2) * 256 + 256], AF.Square))

                @blk.tensor
                def _(te, tb=tb):
                    emit_stats(te, tb, NCH - 1)

                @blk.scalar
                def _(sc, tb=tb):
                    sc.wait_ge(sPE.h, stats_at[(tb, NCH - 1)])
                    if tb >= 1:
                        sc.wait_ge(sACT.h, bc_done[tb - 1])
                    sACT.inc(sc.mul(ROWS[0:1, 0:256], PS[0:1, 2048:2304], 1.0 / D))
                    sACT.inc(sc.mul(ROWS[0:1, 256:512], PS[0:1, 2560:2816], 1.0 / D))
                    sACT.inc(sc.copy(MROW[0:1, 0:256], PS[0:1, 3072:3328]))
                    rowmath_done[tb] = sACT.n

                @blk.gpsimd
                def _(gp, tb=tb):
                    gp.wait_ge(sACT.h, sACT.n)
                    sPOOL.inc(gp.tensor_tensor(ROWS[0:1, 512:768], ROWS[0:1, 0:256],
                                               ROWS[0:1, 0:256], op=ALU.mult))

                @blk.vector
                def _(v, tb=tb):
                    v.wait_ge(sPOOL.h, sPOOL.n)
                    sDVE.inc(v.scalar_tensor_tensor(ROWS[0:1, 768:1024], ROWS[0:1, 256:512],
                                                    EPS, ROWS[0:1, 512:768],
                                                    op0=ALU.add, op1=ALU.subtract))

                @blk.scalar
                def _(sc, tb=tb):
                    sc.wait_ge(sDVE.h, sDVE.n)
                    sACT.inc(sc.activation(ROWS[0:1, 512:768], ROWS[0:1, 768:1024], AF.Sqrt))
                    sACT.inc(sc.mul(ROWS[0:1, 256:512], ROWS[0:1, 0:256], swgd[0:1, 0:1]))

                @blk.vector
                def _(v, tb=tb):
                    v.wait_ge(sACT.h, sACT.n)
                    v.wait_ge(sMRW.h, sMRW.n)
                    sDVE.inc(v.reciprocal(ROWS[0:1, 768:1024], ROWS[0:1, 512:768]))
                    sDVE.inc(v.tensor_tensor(MROW[0:1, 256:512], MROW[0:1, 0:256],
                                             ROWS[0:1, 256:512], op=ALU.is_ge))

                @blk.gpsimd
                def _(gp, tb=tb):
                    gp.wait_ge(sDVE.h, sDVE.n)
                    sPOOL.inc(gp.tensor_tensor(ROWS[0:1, 512:768], ROWS[0:1, 0:256],
                                               ROWS[0:1, 768:1024], op=ALU.mult))

                @blk.vector
                def _(v, tb=tb):
                    v.wait_ge(sPOOL.h, sPOOL.n)
                    sDVE.inc(v.tensor_scalar_mul(ROWS[0:1, 512:768], ROWS[0:1, 512:768], -1.0))

                @blk.sync
                def _(sy, tb=tb):
                    sy.wait_ge(sDVE.h, sDVE.n)
                    sMRW.dma(sy.dma_start(bass.AP(mrow_scr, tb * 256, [[4096, 1], [1, 256]]),
                                          MROW[0:1, 256:512]))

                @blk.tensor
                def _(te, tb=tb):
                    te.wait_ge(sDVE.h, sDVE.n)
                    if tb >= 1:
                        te.wait_ge(sACT.h, bc_done[tb - 1])
                    te.matmul(PS[:, 3584:3840], onesr[0:1, :], ROWS[0:1, 768:1024],
                              start=True, stop=True)
                    sPE.inc(te.matmul(PS[:, 3840:4096], onesr[0:1, :], ROWS[0:1, 512:768],
                                      start=True, stop=True))
                    bc_done[(tb, "mm")] = sPE.n

                @blk.scalar
                def _(sc, tb=tb):
                    sc.wait_ge(sPE.h, bc_done[(tb, "mm")])
                    sc.copy(XR[:, 1024:1280], PS[:, 3584:3840])
                    bc_done[tb] = sACT.inc(sc.copy(XR[:, 1280:1536], PS[:, 3840:4096]))

                x1o = 16384 + par * 4096
                for dot in range(NCH):
                    @blk.vector
                    def _(v, tb=tb, dot=dot):
                        v.wait_ge(sACT.h, bc_done[tb])
                        if dot >= 2:
                            v.wait_ge(sPOOL.h, apm_at[(tb, dot - 2, "p")])
                        tpar = dot % 2
                        apm_at[(tb, dot)] = sDVE.inc(
                            v.tensor_tensor(XR[:, 1536 + tpar * 256:1536 + tpar * 256 + 256],
                                            FSCR[:, dot * 256:(dot + 1) * 256],
                                            XR[:, 1024:1280], op=ALU.mult))

                    @blk.gpsimd
                    def _(gp, tb=tb, dot=dot, x1o=x1o):
                        gp.wait_ge(sDVE.h, apm_at[(tb, dot)])
                        if dot == 0 and tb >= 2:
                            gp.wait_ge(sX1.h, sX1.n - 16)
                        tpar = dot % 2
                        apm_at[(tb, dot, "p")] = sPOOL.inc(
                            gp.tensor_tensor(BUF2[:, x1o + dot * 256:x1o + dot * 256 + 256],
                                             XR[:, 1536 + tpar * 256:1536 + tpar * 256 + 256],
                                             XR[:, 1280:1536], op=ALU.add))

                @blk.gpsimd
                def _(gp, tb=tb, x1o=x1o):
                    sX1.dma(gp.dma_start(dap(x1_scr, tb * 256, 256, nseg=16, segstride=4096),
                                         BUF2[:, x1o:x1o + 4096]))

                if tb + 2 < ntb:
                    @blk.sync
                    def _(sy, tb=tb):
                        sy.wait_ge(sPE.h, wo_at[(tb, NCH - 1)])
                        ntb2 = tb + 2
                        sLDX.dma(sy.dma_start(
                            BUF2[:, (ntb2 % 2) * 4096:(ntb2 % 2) * 4096 + 4096],
                            dap(y_scr, ntb2 * 256, 256, nseg=16, segstride=4096)))
                        ytb_at[ntb2] = sLDX.n

            @blk.gpsimd
            def _(gp):
                emit_conversion_store(gp, len(conv_jobs))
                gp.wait_ge(sX1.h, sX1.n)
                gp.wait_ge(sMRW.h, sMRW.n)
                gp.wait_ge(sCVO.h, sCVO.n)

            @blk.sync
            def _(sy):
                emit_conversion_load(sy, len(conv_jobs))

    def phase_b(gi, blk, prev):
        x1_at, mb_at, wea_at, web_at = {}, {}, {}, {}

        def wea_slot(i):
            s = i % 4
            return 16384 + s * 2048 if s < 2 else 28672 + (s - 2) * 2048
        hmm_at, hev_at, omm_at, cons_at = {}, {}, {}, {}
        sel_at, z2_at, sq2_at = {}, {}, {}

        @blk.sync
        def _(sy):
            if prev is not None:
                sy.wait_ge(sDVE.h, prev["z2"][15])
                sy.wait_ge(sPE.h, prev["hmm"][63])
                sy.wait_ge(sACT.h, prev["bc2ev"])
            sLDX.dma(sy.dma_start(KF[:, 0:8192],
                                  dap(x1_scr, gi * GB, GB, nseg=16, segstride=4096)))
            x1_at["x1"] = sLDX.n
            sLDX.dma(sy.dma_start(MROW[0:1, 0:512],
                                  bass.AP(mrow_scr, gi * GB, [[4096, 1], [1, 512]])))
            x1_at["mr"] = sLDX.n
            for i in (0, 1, 2, 3):
                e, s = divmod(i, 32)
                wo_ = wea_slot(i)
                sLD.dma(sy.dma_start(BUF1[:, wo_:wo_ + 2048], dap(wa_bf[e], s * 2048, 2048)))
                wea_at[i] = sLD.n
            for q in (0, 1):
                if prev is not None:
                    sy.wait_ge(sPE.h, prev["omm"][(1, 2 + q)])
                sLDQ.dma(sy.dma_start(BUF2[:, (q % 2) * 16384:(q % 2) * 16384 + 16384],
                                      dap(wb_bf[0], q * 16384, 16384)))
                web_at[(0, q)] = sLDQ.n

        @blk.sync
        def _(sy, gi=gi):
            if prev is not None:
                sy.wait_ge(sDVE.h, prev["z2"][15])
            sLDX.dma(sy.dma_start(FSCR[:, 3072:3584],
                                  bass.AP(mrow_scr, gi * GB, [[0, 128], [1, 512]])))
            mb_at[0] = sLDX.n

        for i in range(64):
            e, ft = divmod(i, 32)
            par = i % 2
            pcol = (i % 2) * 512

            @blk.tensor
            def _(te, i=i, e=e, ft=ft, pcol=pcol):
                te.wait_ge(sLD.h, wea_at[i])
                if i == 0:
                    te.wait_ge(sLDX.h, x1_at["x1"])
                if i >= 2:
                    te.wait_ge(sACT.h, hev_at[i - 2])
                elif prev is not None:
                    te.wait_ge(sACT.h, prev["hev"][62 + i])
                wo_ = wea_slot(i)
                ins = None
                for dc in range(NCH):
                    ins = te.matmul(PS[:, pcol:pcol + 512],
                                    BUF1[:, wo_ + dc * 128:wo_ + dc * 128 + 128],
                                    KF[:, dc * 512:(dc + 1) * 512],
                                    start=(dc == 0), stop=(dc == NCH - 1))
                hmm_at[i] = sPE.inc(ins)

            if i + 4 < 64:
                @blk.sync
                def _(sy, i=i):
                    sy.wait_ge(sPE.h, hmm_at[i])
                    j = i + 4
                    e2, s2 = divmod(j, 32)
                    wo_ = wea_slot(j)
                    sLD.dma(sy.dma_start(BUF1[:, wo_:wo_ + 2048],
                                         dap(wa_bf[e2], s2 * 2048, 2048)))
                    wea_at[j] = sLD.n

            @blk.scalar
            def _(sc, i=i, e=e, ft=ft, pcol=pcol):
                sc.wait_ge(sPE.h, hmm_at[i])
                if e == 1 and ft == 0:
                    sc.wait_ge(sPE.h, omm_at[(0, 3)])
                elif e == 0 and ft == 0 and prev is not None:
                    sc.wait_ge(sPE.h, prev["omm"][(1, 3)])
                fn = AF.Gelu if e == 0 else AF.Relu
                hev_at[i] = sACT.inc(
                    sc.activation(BUF1[:, ft * 512:(ft + 1) * 512], PS[:, pcol:pcol + 512],
                                  fn, bias=basb[e][:, ft:ft + 1], scale=1.0))

            if ft == 31:
                for q in range(4):
                    if not (e == 0 and q < 2):
                        @blk.sync
                        def _(sy, e=e, q=q):
                            j = e * 4 + q
                            pe2, pq2 = divmod(j - 2, 4)
                            sy.wait_ge(sPE.h, omm_at[(pe2, pq2)])
                            sLDQ.dma(sy.dma_start(
                                BUF2[:, (q % 2) * 16384:(q % 2) * 16384 + 16384],
                                dap(wb_bf[e], q * 16384, 16384)))
                            web_at[(e, q)] = sLDQ.n

                    for p in range(2):
                        gpi = q * 2 + p
                        bk = 1024 + (gpi % 2) * 1024

                        @blk.tensor
                        def _(te, e=e, q=q, p=p, gpi=gpi, bk=bk):
                            if p == 0:
                                te.wait_ge(sLDQ.h, web_at[(e, q)])
                                te.wait_ge(sACT.h, hev_at[e * 32 + 31])
                            pk = (e, gpi - 2) if gpi >= 2 else ((e - 1, gpi + 6) if e == 1 else None)
                            if pk is not None:
                                eng, val = cons_at[pk]
                                te.wait_ge(eng, val)
                            elif prev is not None:
                                eng, val = prev["cons"][(1, gpi + 6)]
                                te.wait_ge(eng, val)
                            ins = None
                            for Fc in range(NFT):
                                for d2 in range(2):
                                    dot = p * 2 + d2
                                    ins = te.matmul(
                                        PS[:, bk + d2 * 512:bk + d2 * 512 + 512],
                                        BUF2[:, (q % 2) * 16384 + Fc * 512 + dot * 128:
                                             (q % 2) * 16384 + Fc * 512 + dot * 128 + 128],
                                        BUF1[:, Fc * 512:(Fc + 1) * 512],
                                        start=(Fc == 0), stop=(Fc == NFT - 1))
                            omm_at[(e, gpi, "p")] = sPE.inc(ins)
                            if p == 1:
                                omm_at[(e, q)] = sPE.n

                        if e == 0:
                            @blk.scalar
                            def _(sc, q=q, p=p, gpi=gpi, bk=bk):
                                sc.wait_ge(sPE.h, omm_at[(0, gpi, "p")])
                                if gpi == 0 and prev is not None:
                                    sc.wait_ge(sDVE.h, prev["z2"][15])
                                ins = None
                                for d2 in range(2):
                                    cg = q * 4 + p * 2 + d2
                                    ins = sc.copy(
                                        BUF1[:, 20480 + cg * 512:20480 + cg * 512 + 512],
                                        PS[:, bk + d2 * 512:bk + d2 * 512 + 512])
                                cons_at[(0, gpi)] = (sACT.h, sACT.inc(ins))
                        else:
                            @blk.vector
                            def _(v, q=q, p=p, gpi=gpi, bk=bk):
                                v.wait_ge(sPE.h, omm_at[(1, gpi, "p")])
                                if gpi == 0:
                                    v.wait_ge(sLDX.h, mb_at[0])
                                ins = None
                                for d2 in range(2):
                                    cg = q * 4 + p * 2 + d2
                                    sDVE.inc(v.scalar_tensor_tensor(
                                        XR[:, 0:512],
                                        PS[:, bk + d2 * 512:bk + d2 * 512 + 512],
                                        bbsb[1][:, cg:cg + 1], KF[:, cg * 512:(cg + 1) * 512],
                                        op0=ALU.add, op1=ALU.add))
                                    sDVE.inc(v.scalar_tensor_tensor(
                                        XR[:, 512:1024],
                                        PS[:, bk + d2 * 512:bk + d2 * 512 + 512],
                                        bdiff[:, cg:cg + 1],
                                        BUF1[:, 20480 + cg * 512:20480 + cg * 512 + 512],
                                        op0=ALU.add, op1=ALU.subtract))
                                    v.tensor_tensor(XR[:, 512:1024], XR[:, 512:1024],
                                                    FSCR[:, 3072:3584], op=ALU.mult)
                                    ins = v.tensor_tensor(VT[:, cg * 512:(cg + 1) * 512],
                                                          XR[:, 0:512], XR[:, 512:1024],
                                                          op=ALU.subtract)
                                    z2_at[cg] = sDVE.inc(ins)
                                cons_at[(1, gpi)] = (sDVE.h, sDVE.n)

        for cg in range(NCH):
            @blk.scalar
            def _(sc, cg=cg):
                sc.wait_ge(sDVE.h, z2_at[cg])
                if cg >= 2:
                    sc.wait_ge(sPE.h, sq2_at[cg - 2])
                elif prev is not None:
                    sc.wait_ge(sPE.h, prev["sq2"][14 + cg])
                sq2_at[(cg, "a")] = sACT.inc(
                    sc.activation(BUF5[:, (cg % 2) * 512:(cg % 2) * 512 + 512],
                                  VT[:, cg * 512:(cg + 1) * 512], AF.Square))

            @blk.tensor
            def _(te, cg=cg):
                te.wait_ge(sACT.h, sq2_at[(cg, "a")])
                if cg == 0 and prev is not None:
                    te.wait_ge(sACT.h, prev["bc2ev"])
                te.matmul(PS[0:1, 3072:3584], onesb[:, 0:1], VT[:, cg * 512:(cg + 1) * 512],
                          start=(cg == 0), stop=(cg == NCH - 1))
                ins = te.matmul(PS[0:1, 3584:4096], onesb[:, 0:1],
                                BUF5[:, (cg % 2) * 512:(cg % 2) * 512 + 512],
                                start=(cg == 0), stop=(cg == NCH - 1))
                sq2_at[cg] = sPE.inc(ins)

        @blk.scalar
        def _(sc):
            sc.wait_ge(sPE.h, sq2_at[NCH - 1])
            sACT.inc(sc.mul(ROWS[0:1, 0:512], PS[0:1, 3072:3584], 1.0 / D))
            sACT.inc(sc.mul(ROWS[0:1, 512:1024], PS[0:1, 3584:4096], 1.0 / D))

        @blk.gpsimd
        def _(gp):
            gp.wait_ge(sACT.h, sACT.n)
            sPOOL.inc(gp.tensor_tensor(MROW[0:1, 0:512], ROWS[0:1, 0:512],
                                       ROWS[0:1, 0:512], op=ALU.mult))

        @blk.vector
        def _(v):
            v.wait_ge(sPOOL.h, sPOOL.n)
            sDVE.inc(v.scalar_tensor_tensor(ROWS[0:1, 512:1024], ROWS[0:1, 512:1024],
                                            EPS, MROW[0:1, 0:512],
                                            op0=ALU.add, op1=ALU.subtract))

        @blk.scalar
        def _(sc):
            sc.wait_ge(sDVE.h, sDVE.n)
            sACT.inc(sc.activation(ROWS[0:1, 512:1024], ROWS[0:1, 512:1024], AF.Sqrt))

        @blk.vector
        def _(v):
            v.wait_ge(sACT.h, sACT.n)
            sDVE.inc(v.reciprocal(ROWS[0:1, 512:1024], ROWS[0:1, 512:1024]))

        @blk.gpsimd
        def _(gp):
            gp.wait_ge(sDVE.h, sDVE.n)
            sPOOL.inc(gp.tensor_tensor(MROW[0:1, 0:512], ROWS[0:1, 0:512],
                                       ROWS[0:1, 512:1024], op=ALU.mult))

        @blk.vector
        def _(v):
            v.wait_ge(sPOOL.h, sPOOL.n)
            sDVE.inc(v.tensor_scalar_mul(MROW[0:1, 0:512], MROW[0:1, 0:512], -1.0))

        bc2 = {}

        @blk.tensor
        def _(te):
            te.wait_ge(sDVE.h, sDVE.n)
            te.matmul(PS[:, 3072:3584], onesr[0:1, :], ROWS[0:1, 512:1024],
                      start=True, stop=True)
            sPE.inc(te.matmul(PS[:, 3584:4096], onesr[0:1, :], MROW[0:1, 0:512],
                              start=True, stop=True))
            bc2["mm"] = sPE.n

        @blk.scalar
        def _(sc):
            sc.wait_ge(sPE.h, bc2["mm"])
            if prev is not None:
                sc.wait_ge(sPOOL.h, prev["ap2p15"])
            sc.copy(FSCR[:, 2048:2560], PS[:, 3072:3584])
            bc2[0] = sACT.inc(sc.copy(FSCR[:, 2560:3072], PS[:, 3584:4096]))

        ap2 = {}
        for cg in range(NCH):
            @blk.vector
            def _(v, cg=cg):
                if cg == 0:
                    v.wait_ge(sACT.h, bc2[0])
                if cg >= 2:
                    v.wait_ge(sPOOL.h, ap2[(cg - 2, "p")])
                tpar = cg % 2
                ap2[cg] = sDVE.inc(
                    v.tensor_tensor(FSCR[:, 1024 + tpar * 512:1024 + tpar * 512 + 512],
                                    VT[:, cg * 512:(cg + 1) * 512], FSCR[:, 2048:2560],
                                    op=ALU.mult))

            @blk.gpsimd
            def _(gp, cg=cg, gi=gi):
                gp.wait_ge(sDVE.h, ap2[cg])
                if cg >= 2 or prev is not None:
                    gp.wait_ge(sOUT.h, sOUT.n - 16)
                tpar = cg % 2
                ap2[(cg, "p")] = sPOOL.inc(
                    gp.tensor_tensor(FSCR[:, tpar * 512:tpar * 512 + 512],
                                     FSCR[:, 1024 + tpar * 512:1024 + tpar * 512 + 512],
                                     FSCR[:, 2560:3072], op=ALU.add))
                sOUT.dma(gp.dma_start(dap(out_img, cg * 4096 + gi * GB, GB),
                                      FSCR[:, tpar * 512:tpar * 512 + 512]))

        return {"z2": z2_at, "hmm": hmm_at, "hev": hev_at, "omm": omm_at,
                "cons": cons_at, "sq2": sq2_at, "bc2ev": bc2[0],
                "ap2p15": ap2[(NCH - 1, "p")]}

    def scoped(name, f, *a):
        with nc.named_scope(name):
            f(*a)

    scoped("a0a1_b0", phase_a0a1, 0)
    scoped("a2_b0", phase_a2, 0)
    if stop_after != "qkv":
        scoped("a4_b0", phase_a4, 0)
        if stop_after != "attn0":
            scoped("a0a1_b1", phase_a0a1, 1)
            scoped("a2_b1", phase_a2, 1)
            scoped("a4_b1", phase_a4, 1)
            scoped("aprime", phase_aprime)
            if stop_after == "full":
                with nc.Block(no_gpsimd_drain=True) as bblk:
                    prevg = None
                    for gi in range(NG):
                        with nc.named_scope(f"b{gi}"):
                            prevg = phase_b(gi, bblk, prevg)

    with nc.Block(no_gpsimd_drain=True) as blk:
        @blk.gpsimd
        def _(gp):
            if dbg:
                sOUT.dma(gp.dma_start(dbg_t["dbg_kf"].ap(), KF[:, 0:NKV * 2048]))
                sOUT.dma(gp.dma_start(dbg_t["dbg_vt"].ap(), VT[:, :]))
                sOUT.dma(gp.dma_start(dbg_t["dbg_q"].ap(), q_scr.ap()))
                if stop_after in ("attn0", "x1", "full"):
                    sOUT.dma(gp.dma_start(dbg_t["dbg_y"].ap(), y_scr.ap()))
                if stop_after in ("x1", "full"):
                    sOUT.dma(gp.dma_start(dbg_t["dbg_x1"].ap(), x1_scr.ap()))
                    sOUT.dma(gp.dma_start(dbg_t["dbg_m"].ap(), mrow_scr.ap()))
            gp.wait_ge(sOUT.h, sOUT.n)
            gp.wait_ge(sX1.h, sX1.n)
            gp.wait_ge(sYW.h, sYW.n)
            gp.wait_ge(sQW.h, sQW.n)
            gp.wait_ge(sCVO.h, sCVO.n)
            gp.wait_ge(sMRW.h, sMRW.n)

    return nc


def kernel(L0_attn_x, L0_attn_Wq, L0_attn_Wk, L0_attn_Wv, L0_attn_Wo,
           L0_ffn_Wg, L0_ffn_W0a, L0_ffn_b0a, L0_ffn_W0b, L0_ffn_b0b,
           L0_ffn_W1a, L0_ffn_b1a, L0_ffn_W1b, L0_ffn_b1b):
    x = np.asarray(L0_attn_x, np.float32)
    common = host_inputs(L0_attn_Wq, L0_attn_Wk, L0_attn_Wv, L0_attn_Wo, L0_ffn_Wg,
                         L0_ffn_W0a, L0_ffn_b0a, L0_ffn_W0b, L0_ffn_b0b,
                         L0_ffn_W1a, L0_ffn_b1a, L0_ffn_W1b, L0_ffn_b1b)
    in_maps = []
    for core in range(8):
        imgs = [_sbimg(np.ascontiguousarray(x[core * 2 + j].T)) for j in range(2)]
        m = dict(common)
        m["x_img"] = np.ascontiguousarray(np.concatenate(imgs, axis=1))
        in_maps.append(m)
    nc = build("full", dbg=False)
    res = run_bass_kernel_spmd(nc, in_maps, core_ids=list(range(8)))
    outs = []
    for r in res.results:
        o = np.asarray(r["out_img"]).reshape(128, 16, 4096).transpose(1, 0, 2).reshape(2048, 4096)
        outs.append(np.ascontiguousarray(o[:, 0:2048].T))
        outs.append(np.ascontiguousarray(o[:, 2048:4096].T))
    return np.stack(outs).astype(np.float32)
